# version 01
# baseline (speedup 1.0000x reference)
import math
from contextlib import ExitStack

import numpy as np
import ml_dtypes

import concourse.bass as bass
import concourse.mybir as mybir
from concourse.alu_op_type import AluOpType as OP
from concourse.bass_utils import run_bass_kernel_spmd

F32 = mybir.dt.float32
BF16 = mybir.dt.bfloat16
AF = mybir.ActivationFunctionType
AX = mybir.AxisListType

SEQ = 2048
DM = 4096
H = 32
HKV = 8
NT = 16
NIC = 32
NEG = -1.0e30
LN_EPS = 1.0e-5
RMS_EPS = 1.0e-6
INV_SQRT_HD = 1.0 / math.sqrt(128.0)


class Sem:
    def __init__(self, ctx, nc, name):
        self.h = ctx.enter_context(nc.semaphore(name))
        self.v = 0


def _alibi_slopes(n_heads):
    start = 2 ** (-(2 ** -(math.log2(n_heads) - 3)))
    return np.array([start * start ** i for i in range(n_heads)], dtype=np.float32)


def host_constants():
    bf = ml_dtypes.bfloat16
    ident = np.eye(128, dtype=np.float32).astype(bf)
    p = np.arange(128)
    dmask = np.where(p[None, :] <= p[:, None], 0.0, NEG).astype(np.float32)
    slopes = _alibi_slopes(H)
    alw = np.zeros((4, H * 128), dtype=np.float32)
    for h in range(H):
        s_hi = np.float32(bf(slopes[h]))
        s_lo = np.float32(bf(np.float32(slopes[h]) - s_hi))
        alw[0, h * 128:(h + 1) * 128] = 256.0 * s_hi
        alw[1, h * 128:(h + 1) * 128] = s_hi
        alw[2, h * 128:(h + 1) * 128] = 256.0 * s_lo
        alw[3, h * 128:(h + 1) * 128] = s_lo
    alw = alw.astype(bf)
    k = np.arange(SEQ)
    alk = np.zeros((4, SEQ), dtype=np.float32)
    alk[0] = (k // 256).astype(np.float32)
    alk[1] = (k % 256).astype(np.float32)
    alk[2] = alk[0]
    alk[3] = alk[1]
    alk = alk.astype(bf)
    negu = np.where(p[None, :] > p[:, None], NEG, 0.0).astype(np.float32).astype(bf)
    return ident, dmask, alw, alk, negu


def build(taps=(), phases=8):
    nc = bass.Bass("TRN2", target_bir_lowering=False)

    din = {}
    for name, shape, dt in [
        ("x", [SEQ, DM], F32),
        ("wqt", [DM, DM], F32),
        ("wkt", [DM, HKV * 128], F32),
        ("wvt", [DM, HKV * 128], F32),
        ("wot", [DM, DM], F32),
        ("wt", [DM, DM], F32),
        ("wrtb", [128, 128], F32),
        ("atb", [128, 1024], F32),
        ("bmt", [32, DM], F32),
        ("bb", [128, DM], F32),
        ("qnw", [128, 1], F32),
        ("knw", [128, 1], F32),
        ("ident", [128, 128], BF16),
        ("dmask", [128, 128], F32),
        ("alw", [4, H * 128], BF16),
        ("alk", [4, SEQ], BF16),
        ("negu", [128, 128], BF16),
    ]:
        din[name] = nc.dram_tensor(name, shape, dt, kind="ExternalInput").ap()

    def scratch(name, shape):
        kind = "ExternalOutput" if name in taps else "Internal"
        return nc.dram_tensor(name, shape, BF16, kind=kind).ap()
    qhat = scratch("qhat", [H * 128, SEQ])
    kscr = scratch("kscr", [HKV * 128, SEQ])
    othbm = scratch("ot", [NT, 128, H, 128])
    gsc = scratch("gsc", [SEQ, DM])
    vsc = scratch("vsc", [HKV, 128, SEQ])
    out = nc.dram_tensor("out", [SEQ, DM], F32, kind="ExternalOutput").ap()

    tap_out = {}
    for t in taps:
        if t in ("qhat", "kscr", "ot", "gsc", "vsc"):
            continue
        if t == "dbg":
            tap_out[t] = nc.dram_tensor("tap_dbg", [128, 16], F32,
                                        kind="ExternalOutput").ap()
            tap_out["dbg2"] = nc.dram_tensor("tap_dbg2", [128, 2], F32,
                                             kind="ExternalOutput").ap()
            continue
        shp = {"xnt": [128, NIC * SEQ]}[t]
        tap_out[t] = nc.dram_tensor("tap_" + t, shp, BF16, kind="ExternalOutput").ap()

    g = ExitStack()
    sb = lambda n, s, d, **kw: g.enter_context(nc.sbuf_tensor(n, s, d, **kw))
    s_ident = sb("s_ident", [128, 128], BF16)
    s_qnw = sb("s_qnw", [128, 1], F32)
    s_knw = sb("s_knw", [128, 1], F32)
    s_wqe = sb("s_wqe", [128, 1], F32)
    s_ones128 = sb("s_ones128", [128, 1], BF16)
    s_ones1f = sb("s_ones1f", [1, 128], F32)
    s_cols = sb("s_cols", [128, 16], F32)
    s_eps = sb("s_eps", [128, 2], F32)
    s_kols = sb("s_kols", [128, 4], F32)
    s_pgT = sb("s_pgT", [32, SEQ], BF16)

    sctx = ExitStack()
    dL = Sem(sctx, nc, "dL")
    dLp = [Sem(sctx, nc, f"dL{i}") for i in range(2)]
    dS = Sem(sctx, nc, "dS")
    dSp = [Sem(sctx, nc, f"dS{i}") for i in range(2)]
    dW = Sem(sctx, nc, "dW")
    dWp = [Sem(sctx, nc, f"dW{i}") for i in range(2)]
    dX = Sem(sctx, nc, "dX")
    sPE = Sem(sctx, nc, "sPE")
    sDV = Sem(sctx, nc, "sDV")
    sAC = Sem(sctx, nc, "sAC")
    sS5 = Sem(sctx, nc, "sS5")
    sM5 = Sem(sctx, nc, "sM5")
    sE5 = Sem(sctx, nc, "sE5")
    sN5 = Sem(sctx, nc, "sN5")
    sP5 = Sem(sctx, nc, "sP5")
    sV5 = Sem(sctx, nc, "sV5")
    sF5 = Sem(sctx, nc, "sF5")

    def wge(e, sem, val):
        if val > 0:
            e.wait_ge(sem.h, val)

    def esync(e, kind):
        for s in (sPE, sDV, sAC, sS5, sP5, sV5, sM5, sN5, sE5, sF5,
                  dL, dS, dW, dX, dLp[0], dLp[1], dSp[0], dSp[1],
                  dWp[0], dWp[1]):
            if s.v > 0:
                e.wait_ge(s.h, s.v)

    xnt_ctx = ExitStack()
    xnt = xnt_ctx.enter_context(nc.sbuf_tensor("xnt", [128, NIC * SEQ], BF16))

    def xn_stat(ic, tt):
        return xnt[:, tt * 4096 + ic * 128: tt * 4096 + (ic + 1) * 128]

    def xn_mov(ic, tc):
        v3 = xnt[:, :].rearrange("p (a b) -> p a b", b=4096)
        return v3[:, tc * 4:(tc + 1) * 4, ic * 128:(ic + 1) * 128]

    with nc.Block() as blk:
        @blk.scalar
        def _(e):
            esync(e, "a")
            for s, d in [(s_ident, din["ident"]), (s_qnw, din["qnw"]),
                         (s_knw, din["knw"])]:
                e.dma_start(out=s[:], in_=d[:, :]).then_inc(dL.h, 16)

        @blk.vector
        def _(e):
            esync(e, "v")
            e.wait_ge(dL.h, dL.v + 48)
            e.memset(s_ones128[:], 1.0)
            e.memset(s_ones1f[:], 1.0)
            e.memset(s_eps[:, 0:1], LN_EPS)
            e.memset(s_eps[:, 1:2], RMS_EPS)
            e.tensor_scalar_mul(s_wqe[:], s_qnw[:], INV_SQRT_HD).then_inc(sDV.h, 1)
    dL.v += 48
    sDV.v += 1

    if phases >= 1:
      with ExitStack() as pctx, nc.Block() as blk:
        psb = lambda n, s, d, **kw: pctx.enter_context(nc.sbuf_tensor(n, s, d, **kw))
        xt = [psb(f"xt{i}", [128, DM], F32) for i in range(2)]
        ast = [psb(f"ast{i}", [128, DM], BF16) for i in range(2)]
        b_dX, b_AC, b_DV = dX.v, sAC.v, sDV.v
        b_dLp = [dLp[0].v, dLp[1].v]

        @blk.scalar
        def _(e):
            esync(e, "a")
            for tt in range(NT):
                p = tt % 2
                if tt >= 2:
                    wge(e, sDV, b_DV + 6 * (tt - 2) + 6)
                e.dma_start(out=xt[p][:],
                            in_=din["x"][tt * 128:(tt + 1) * 128, :]) \
                    .then_inc(dLp[p].h, 16)
                wge(e, sDV, b_DV + 6 * tt + 4)
                e.activation(s_kols[:, 0:1], s_cols[:, 4:5], AF.Sqrt,
                             scale=1.0 / DM, bias=s_eps[:, 0:1]).then_inc(sAC.h, 1)

        @blk.vector
        def _(e):
            esync(e, "v")
            for tt in range(NT):
                p = tt % 2
                wge(e, dLp[p], b_dLp[p] + 16 * (tt // 2 + 1))
                e.reduce_sum(s_cols[:, 0:1], xt[p][:], axis=AX.X)
                if tt >= 2:
                    wge(e, dX, b_dX + 16 * (tt - 1))
                e.tensor_tensor(out=ast[p][:], in0=xt[p][:], in1=xt[p][:],
                                op=OP.mult).then_inc(sDV.h, 1)
                wge(e, sDV, b_DV + 6 * tt + 1)
                e.tensor_scalar(out=ast[p][:], in0=ast[p][:], scalar1=1.0,
                                scalar2=0.0, op0=OP.mult, op1=OP.add,
                                accum_out=s_cols[:, 1:2]).then_inc(sDV.h, 1)
                wge(e, sDV, b_DV + 6 * tt + 2)
                e.tensor_scalar(out=s_cols[:, 5:6], in0=s_cols[:, 0:1],
                                scalar1=1.0 / DM, scalar2=s_cols[:, 0:1],
                                op0=OP.mult, op1=OP.mult)
                e.tensor_scalar(out=s_cols[:, 2:3], in0=s_cols[:, 0:1],
                                scalar1=1.0 / DM, scalar2=None,
                                op0=OP.mult).then_inc(sDV.h, 1)
                wge(e, sDV, b_DV + 6 * tt + 3)
                if tt >= 1:
                    wge(e, sAC, b_AC + tt)
                e.tensor_tensor(out=s_cols[:, 4:5], in0=s_cols[:, 1:2],
                                in1=s_cols[:, 5:6],
                                op=OP.subtract).then_inc(sDV.h, 1)
                wge(e, sAC, b_AC + tt + 1)
                e.reciprocal(s_cols[:, 7:8], s_kols[:, 0:1]).then_inc(sDV.h, 1)
                wge(e, sDV, b_DV + 6 * tt + 5)
                e.tensor_scalar(out=ast[p][:], in0=xt[p][:],
                                scalar1=s_cols[:, 2:3], scalar2=s_cols[:, 7:8],
                                op0=OP.subtract, op1=OP.mult).then_inc(sDV.h, 1)

        @blk.sync
        def _(e):
            esync(e, "g")
            for tt in range(NT):
                p = tt % 2
                wge(e, sDV, b_DV + 6 * tt + 6)
                dst = xnt[:, tt * 4096:(tt + 1) * 4096].rearrange(
                    "p (c r) -> p c r", r=128)
                e.dma_start_transpose(dst, ast[p][:]).then_inc(dX.h, 16)
                wge(e, dX, b_dX + 16 * (tt + 1))
      dLp[0].v += 16 * (NT // 2)
      dLp[1].v += 16 * (NT // 2)
      sAC.v += NT
      sDV.v += 6 * NT
      dX.v += 16 * NT

    if phases >= 2:
      with ExitStack() as pctx, nc.Block() as blk:
        psb = lambda n, s, d, **kw: pctx.enter_context(nc.sbuf_tensor(n, s, d, **kw))
        pps = lambda n, s, d: pctx.enter_context(nc.psum_tensor(n, s, d))
        s_wrt = psb("s_wrt", [128, 128], BF16)
        s_at = psb("s_at", [128, 1024], BF16)
        s_eg = psb("s_eg", [128, 4], F32)
        s_g4 = psb("s_g4", [128, 4], F32)
        s_pg = [psb(f"s_pg{i}", [128, 32], BF16) for i in range(2)]
        pgp = [pps(f"pgp{i}", [128, 512], F32) for i in range(2)]
        ppgt = [pps(f"ppgt{i}", [128, 1024], BF16) for i in range(2)]
        b_dW = dW.v
        b_PE, b_DV, b_AC = sPE.v, sDV.v, sAC.v
        pe_mm, pe_tr = [0] * NT, [0] * NT
        c = 0
        for tt in range(NT):
            c += 1
            pe_mm[tt] = c
            if tt >= 1:
                c += 1
                pe_tr[tt - 1] = c
        c += 1
        pe_tr[NT - 1] = c
        n_pe_pro = c
        dv_negmax, dv_projg, dv_evac = [0] * NT, [0] * NT, [0] * NT
        c = 0
        for tt in range(NT):
            c += 1; dv_negmax[tt] = c
            c += 1
            c += 1
            c += 1; dv_projg[tt] = c
            if tt >= 1:
                c += 1; dv_evac[tt - 1] = c
        c += 1; dv_evac[NT - 1] = c
        n_dv_pro = c

        @blk.gpsimd
        def _(e):
            esync(e, "g")
            b = dW.v
            for k, (s, d) in enumerate([(s_wrt, din["wrtb"]), (s_at, din["atb"])]):
                e.dma_start(out=s[:], in_=d[:, :]).then_inc(dW.h, 16)
                e.wait_ge(dW.h, b + 16 * (k + 1))

        @blk.tensor
        def _(e):
            esync(e, "t")
            wge(e, dW, b_dW + 16 * 2)
            wge(e, dL, 48)
            wge(e, dX, dX.v)
            for tt in range(NT):
                p = tt % 2
                if tt >= 2:
                    wge(e, sDV, b_DV + dv_projg[tt - 2])
                for ic in range(NIC):
                    e.matmul(pgp[p][:, 0:4], xn_stat(ic, tt),
                             s_wrt[:, ic * 4:(ic + 1) * 4],
                             start=(ic == 0), stop=(ic == NIC - 1))
                for ic in range(NIC):
                    ins = e.matmul(pgp[p][:, 4:36], xn_stat(ic, tt),
                                   s_at[:, ic * 32:(ic + 1) * 32],
                                   start=(ic == 0), stop=(ic == NIC - 1))
                ins.then_inc(sPE.h, 1)
                if tt >= 1:
                    wge(e, sDV, b_DV + dv_projg[tt - 1])
                    e.transpose(ppgt[(tt - 1) % 2][0:32, 0:128],
                                s_pg[(tt - 1) % 2][:], s_ident[:]).then_inc(sPE.h, 1)
            wge(e, sDV, b_DV + dv_projg[NT - 1])
            e.transpose(ppgt[(NT - 1) % 2][0:32, 0:128],
                        s_pg[(NT - 1) % 2][:], s_ident[:]).then_inc(sPE.h, 1)

        @blk.vector
        def _(e):
            esync(e, "v")
            for tt in range(NT):
                p = tt % 2
                wge(e, sPE, b_PE + pe_mm[tt])
                if tt >= 1:
                    wge(e, sAC, b_AC + tt)
                e.reduce_max(s_cols[:, 8:9], pgp[p][:, 0:4], axis=AX.X,
                             negate=True).then_inc(sDV.h, 1)
                wge(e, sAC, b_AC + tt + 1)
                e.reciprocal(s_cols[:, 10:11], s_kols[:, 2:3]).then_inc(sDV.h, 1)
                wge(e, sDV, b_DV + dv_negmax[tt] + 1)
                e.tensor_scalar(out=s_g4[:], in0=s_eg[:],
                                scalar1=s_cols[:, 10:11], scalar2=2.0,
                                op0=OP.mult, op1=OP.mult).then_inc(sDV.h, 1)
                wge(e, sDV, b_DV + dv_negmax[tt] + 2)
                for ee in range(4):
                    ins = e.tensor_scalar(out=s_pg[p][:, ee * 8:(ee + 1) * 8],
                                          in0=pgp[p][:, 4 + ee * 8: 4 + (ee + 1) * 8],
                                          scalar1=s_g4[:, ee:ee + 1], scalar2=None,
                                          op0=OP.mult)
                ins.then_inc(sDV.h, 1)
                if tt >= 1:
                    wge(e, sPE, b_PE + pe_tr[tt - 1])
                    e.tensor_copy(s_pgT[0:32, (tt - 1) * 128: tt * 128],
                                  ppgt[(tt - 1) % 2][0:32, 0:128]).then_inc(sDV.h, 1)
            wge(e, sPE, b_PE + pe_tr[NT - 1])
            e.tensor_copy(s_pgT[0:32, (NT - 1) * 128: NT * 128],
                          ppgt[(NT - 1) % 2][0:32, 0:128]).then_inc(sDV.h, 1)

        @blk.scalar
        def _(e):
            esync(e, "a")
            for tt in range(NT):
                wge(e, sDV, b_DV + dv_negmax[tt])
                e.activation(s_eg[:], pgp[tt % 2][:, 0:4], AF.Exp,
                             bias=s_cols[:, 8:9],
                             accum_out=s_kols[:, 2:3])
                e.copy(s_kols[:, 1:2], s_eps[:, 0:1]).then_inc(sAC.h, 1)
      dW.v += 16 * 2
      sPE.v += n_pe_pro
      sAC.v += NT
      sDV.v += n_dv_pro

    NOC = 16
    if phases >= 3:
      with ExitStack() as pctx, nc.Block() as blk:
        psb = lambda n, s, d, **kw: pctx.enter_context(nc.sbuf_tensor(n, s, d, **kw))
        pps = lambda n, s, d: pctx.enter_context(nc.psum_tensor(n, s, d))
        wbuf2 = psb("wbuf2", [128, 16384], BF16)
        s_bmt = psb("s_bmt", [32, DM], BF16)
        s_bb = psb("s_bb", [128, DM], F32)
        s_f1 = [psb(f"s_f1{i}", [128, 256], F32) for i in range(2)]
        s_gt = [psb(f"s_gt{i}", [128, 256], BF16) for i in range(2)]
        pg = [pps(f"pg{i}", [128, 512], F32) for i in range(2)]
        b_dW, b_dS, b_dL = dW.v, dS.v, dL.v
        b_PE, b_DV, b_AC = sPE.v, sDV.v, sAC.v
        b_dWp = [dWp[0].v, dWp[1].v]
        b_dSp = [dSp[0].v, dSp[1].v]

        @blk.gpsimd
        def _(e):
            esync(e, "g")
            e.dma_start(out=s_bmt[:], in_=din["bmt"][:, :]).then_inc(dW.h, 16)
            e.wait_ge(dW.h, b_dW + 16)
            for oc in range(NOC):
                p = oc % 2
                if oc >= 2:
                    wge(e, sPE, b_PE + NT * (oc - 1))
                src = din["wt"][:, oc * 256:(oc + 1) * 256].rearrange(
                    "(c p) o -> p c o", p=128)
                dst = wbuf2[:, p * 8192:(p + 1) * 8192].rearrange(
                    "p (c o) -> p c o", o=256)
                e.dma_start(out=dst, in_=src).then_inc(dWp[p].h, 16)

        @blk.tensor
        def _(e):
            esync(e, "t")
            wge(e, dW, b_dW + 16)
            for oc in range(NOC):
                p = oc % 2
                wge(e, dWp[p], b_dWp[p] + 16 * (oc // 2 + 1))
                for tt in range(NT):
                    i = oc * NT + tt
                    pp = i % 2
                    if i >= 2:
                        wge(e, sDV, b_DV + (i - 2) + 1)
                    for ic in range(NIC):
                        e.matmul(pg[pp][:, 0:256],
                                 xn_stat(ic, tt),
                                 wbuf2[:, p * 8192 + ic * 256: p * 8192 + (ic + 1) * 256],
                                 start=(ic == 0), stop=False)
                    e.matmul(pg[pp][:, 0:256],
                             s_pgT[0:32, tt * 128:(tt + 1) * 128],
                             s_bmt[0:32, oc * 256:(oc + 1) * 256],
                             start=False, stop=True).then_inc(sPE.h, 1)

        @blk.scalar
        def _(e):
            esync(e, "a")
            e.dma_start(out=s_bb[:], in_=din["bb"][:, :]).then_inc(dL.h, 16)
            for oc in range(NOC):
                for tt in range(NT):
                    i = oc * NT + tt
                    pp = i % 2
                    wge(e, sDV, b_DV + i + 1)
                    if i >= 2:
                        wge(e, dSp[pp], b_dSp[pp] + 16 * ((i - 2) // 2 + 1))
                    e.activation(s_gt[pp][:], s_f1[pp][:], AF.Silu).then_inc(sAC.h, 1)
                    wge(e, sAC, b_AC + i + 1)
                    e.dma_start(out=gsc[tt * 128:(tt + 1) * 128,
                                        oc * 256:(oc + 1) * 256],
                                in_=s_gt[pp][:]).then_inc(dSp[pp].h, 16)

        @blk.vector
        def _(e):
            esync(e, "v")
            wge(e, dL, b_dL + 16)
            for oc in range(NOC):
                for tt in range(NT):
                    i = oc * NT + tt
                    pp = i % 2
                    wge(e, sPE, b_PE + i + 1)
                    if i >= 2:
                        wge(e, sAC, b_AC + (i - 2) + 1)
                    e.scalar_tensor_tensor(
                        out=s_f1[pp][:], in0=pg[pp][:, 0:256], scalar=1.0,
                        in1=s_bb[:, oc * 256:(oc + 1) * 256],
                        op0=OP.mult, op1=OP.add).then_inc(sDV.h, 1)
      dW.v += 16
      dWp[0].v += 16 * (NOC // 2)
      dWp[1].v += 16 * (NOC // 2)
      dL.v += 16
      dSp[0].v += 16 * (NOC * NT // 2)
      dSp[1].v += 16 * (NOC * NT // 2)
      sPE.v += NOC * NT
      sAC.v += NOC * NT
      sDV.v += NOC * NT

    def proj_phase(pctx, blk, heads, wsrc, dest_kind):
        psb = lambda n, s, d, **kw: pctx.enter_context(nc.sbuf_tensor(n, s, d, **kw))
        pps = lambda n, s, d: pctx.enter_context(nc.psum_tensor(n, s, d))
        wb = psb("wb_" + dest_kind, [128, 8192], BF16)
        psQ = pps("psQ_" + dest_kind, [128, 2048], F32)
        pbc = pps("pbc_" + dest_kind, [128, 2048], F32)
        norm = dest_kind in ("q", "k")
        if norm:
            s_sq = psb("sq_" + dest_kind, [128, 2048], BF16)
            s_w = psb("sw_" + dest_kind, [128, 2048], F32)
            s_rr = psb("srr_" + dest_kind, [1, 2048], F32)
        s_st = [psb(f"st_{dest_kind}{i}", [128, 2048], BF16) for i in range(2)]
        ndst = qhat if dest_kind == "q" else kscr
        wcol = s_wqe if dest_kind == "q" else s_knw
        b_dW, b_dS = dW.v, dS.v
        b_PE, b_DV, b_AC = sPE.v, sDV.v, sAC.v
        b_dWp = [dWp[0].v, dWp[1].v]
        b_dSp = [dSp[0].v, dSp[1].v]

        if norm:
            pe_mm, pe_msq, pe_bc = [0] * heads, [0] * heads, [0] * heads
            c = 0
            for h in range(heads + 1):
                if h < heads:
                    c += 1; pe_mm[h] = c
                if h >= 1:
                    c += 1; pe_msq[h - 1] = c
                    c += 1; pe_bc[h - 1] = c
            n_pe = c
            ac_sq, ac_ln, ac_rs = [0] * heads, [0] * heads, [0] * heads
            c = 0
            for h in range(heads + 1):
                if h < heads:
                    c += 1; ac_sq[h] = c
                if h >= 1:
                    c += 1; ac_ln[h - 1] = c
                    c += 1; ac_rs[h - 1] = c
            n_ac = c
            dv_w, dv_stt = [0] * heads, [0] * heads
            c = 0
            for h in range(heads + 1):
                if h >= 1:
                    c += 1; dv_stt[h - 1] = c
                if h < heads:
                    c += 1; dv_w[h] = c
            n_dv = c
        else:
            pe_mm = [h + 1 for h in range(heads)]
            n_pe = heads
            dv_cp = [h + 1 for h in range(heads)]
            n_dv = heads
            n_ac = 0

        @blk.gpsimd
        def _(e):
            esync(e, "g")
            for h in range(heads):
                p = h % 2
                for half in range(2):
                    ch = 2 * h + half
                    if h >= 1:
                        wge(e, sPE, b_PE + pe_mm[h - 1])
                    src = wsrc[half * 2048:(half + 1) * 2048,
                               h * 128:(h + 1) * 128].rearrange(
                        "(c p) o -> p c o", p=128)
                    dst = wb[:, p * 4096 + half * 2048:
                             p * 4096 + (half + 1) * 2048].rearrange(
                        "p (c o) -> p c o", o=128)
                    e.dma_start(out=dst, in_=src).then_inc(dWp[ch % 2].h, 16)

        @blk.tensor
        def _(e):
            esync(e, "t")
            for h in range(heads + 1):
                if h < heads:
                    p = h % 2
                    if h >= 1:
                        if norm:
                            wge(e, sDV, b_DV + dv_w[h - 1])
                            wge(e, sAC, b_AC + ac_sq[h - 1])
                        else:
                            wge(e, sDV, b_DV + dv_cp[h - 1])
                    for half in range(2):
                        ch = 2 * h + half
                        wge(e, dWp[ch % 2], b_dWp[ch % 2] + 16 * (ch // 2 + 1))
                        for tc in range(4):
                            for ici in range(16):
                                ic = half * 16 + ici
                                ins = e.matmul(
                                    psQ[:, tc * 512:(tc + 1) * 512],
                                    wb[:, p * 4096 + half * 2048 + ici * 128:
                                       p * 4096 + half * 2048 + (ici + 1) * 128],
                                    xn_mov(ic, tc),
                                    start=(ic == 0), stop=(ic == NIC - 1))
                    ins.then_inc(sPE.h, 1)
                if norm and h >= 1:
                    hh = h - 1
                    wge(e, sAC, b_AC + ac_sq[hh])
                    if hh >= 1:
                        wge(e, sDV, b_DV + dv_stt[hh - 1])
                    for tc in range(4):
                        ins = e.matmul(pbc[0:1, tc * 512:(tc + 1) * 512],
                                       s_ones128[:],
                                       s_sq[:, tc * 512:(tc + 1) * 512],
                                       start=True, stop=True)
                    ins.then_inc(sPE.h, 1)
                    wge(e, sAC, b_AC + ac_rs[hh])
                    for tc in range(4):
                        ins = e.matmul(pbc[:, tc * 512:(tc + 1) * 512], s_ones1f[:],
                                       s_rr[0:1, tc * 512:(tc + 1) * 512],
                                       start=True, stop=True)
                    ins.then_inc(sPE.h, 1)

        if norm:
            @blk.scalar
            def _(e):
                esync(e, "a")
                for h in range(heads + 2):
                    if h < heads:
                        wge(e, sPE, b_PE + pe_mm[h])
                        if h >= 1:
                            wge(e, sPE, b_PE + pe_msq[h - 1])
                        e.activation(s_sq[:], psQ[:],
                                     AF.Square).then_inc(sAC.h, 1)
                    if 1 <= h <= heads:
                        hh = h - 1
                        wge(e, sPE, b_PE + pe_msq[hh])
                        e.activation(pbc[32:33, 0:2048], pbc[0:1, 0:2048], AF.Ln,
                                     scale=1.0 / 128.0, bias=s_eps[0:1, 1:2]).then_inc(sAC.h, 1)
                        if hh >= 1:
                            wge(e, sPE, b_PE + pe_bc[hh - 1])
                        wge(e, sAC, b_AC + ac_ln[hh])
                        e.activation(s_rr[:], pbc[32:33, 0:2048], AF.Exp,
                                     scale=-0.5).then_inc(sAC.h, 1)
                    if h >= 2:
                        hs = h - 2
                        wge(e, sDV, b_DV + dv_stt[hs])
                        e.dma_start(out=ndst[hs * 128:(hs + 1) * 128, :],
                                    in_=s_st[hs % 2][:]).then_inc(dSp[hs % 2].h, 16)
        else:
            @blk.scalar
            def _(e):
                esync(e, "a")
                for h in range(heads):
                    wge(e, sDV, b_DV + dv_cp[h])
                    e.dma_start(out=vsc[h, :, :],
                                in_=s_st[h % 2][:]).then_inc(dSp[h % 2].h, 16)

        @blk.vector
        def _(e):
            esync(e, "v")
            for h in range(heads + 1):
                if norm and h >= 1:
                    hh = h - 1
                    wge(e, sPE, b_PE + pe_bc[hh])
                    if hh >= 2:
                        wge(e, dSp[hh % 2], b_dSp[hh % 2] + 16 * ((hh - 2) // 2 + 1))
                    wge(e, sDV, b_DV + dv_w[hh])
                    dst = s_st[hh % 2][:]
                    e.scalar_tensor_tensor(out=dst, in0=pbc[:], scalar=1.0,
                                           in1=s_w[:], op0=OP.mult,
                                           op1=OP.mult).then_inc(sDV.h, 1)
                if h < heads and norm:
                    wge(e, sPE, b_PE + pe_mm[h])
                    if h >= 1:
                        wge(e, sDV, b_DV + dv_stt[h - 1])
                    e.tensor_scalar(out=s_w[:], in0=psQ[:], scalar1=wcol[:, 0:1],
                                    scalar2=None, op0=OP.mult).then_inc(sDV.h, 1)
                if h < heads and not norm:
                    wge(e, sPE, b_PE + pe_mm[h])
                    if h >= 2:
                        wge(e, dSp[h % 2], b_dSp[h % 2] + 16 * ((h - 2) // 2 + 1))
                    e.tensor_copy(s_st[h % 2][:], psQ[:]).then_inc(sDV.h, 1)

        return dict(dWp0=16 * heads, dWp1=16 * heads,
                    dSp0=16 * ((heads + 1) // 2), dSp1=16 * (heads // 2),
                    sPE=n_pe, sDV=n_dv, sAC=n_ac)

    def apply_inc(inc):
        m = {"dWp0": dWp[0], "dWp1": dWp[1], "dSp0": dSp[0], "dSp1": dSp[1],
             "sPE": sPE, "sDV": sDV, "sAC": sAC}
        for k_, v_ in inc.items():
            m[k_].v += v_

    if phases >= 4:
        with ExitStack() as pctx, nc.Block() as blk:
            apply_inc(proj_phase(pctx, blk, H, din["wqt"], "q"))
    if phases >= 5:
        with ExitStack() as pctx, nc.Block() as blk:
            apply_inc(proj_phase(pctx, blk, HKV, din["wkt"], "k"))
    if phases >= 6:
        with ExitStack() as pctx, nc.Block() as blk:
            apply_inc(proj_phase(pctx, blk, HKV, din["wvt"], "v"))

    if tap_out:
        with nc.Block() as blk:
            b_dS = dS.v
            ntap = len(tap_out)

            @blk.scalar
            def _(e):
                esync(e, "a")
                e.wait_ge(dX.h, dX.v)
                n = 0
                if "xnt" in tap_out:
                    n += 1
                    e.dma_start(out=tap_out["xnt"][:, :], in_=xnt[:]).then_inc(dS.h, 16)
                if "dbg" in tap_out:
                    n += 2
                    e.dma_start(out=tap_out["dbg"][:, :], in_=s_cols[:]).then_inc(dS.h, 16)
                    e.dma_start(out=tap_out["dbg2"][:, :],
                                in_=s_eps[:]).then_inc(dS.h, 16)
                e.wait_ge(dS.h, b_dS + 16 * n)
        dS.v += 16 * ntap

    xnt_ctx.close()

    NQ = H * NT
    if phases >= 7:
      with ExitStack() as pctx, nc.Block() as blk:
        psb = lambda n, s, d, **kw: pctx.enter_context(nc.sbuf_tensor(n, s, d, **kw))
        pps = lambda n, s, d: pctx.enter_context(nc.psum_tensor(n, s, d))
        vv = psb("vv", [128, HKV * SEQ], BF16)
        kt = psb("kt", [128, HKV * SEQ], BF16)
        s_alw = psb("s_alw", [4, H * 128], BF16)
        s_alk = psb("s_alk", [4, SEQ], BF16)
        s_dm = psb("s_dm", [128, 128], F32)
        s_nu = psb("s_nu", [128, 128], BF16)
        qb = [psb(f"qb{i}", [128, SEQ], BF16) for i in range(2)]
        eb = [psb(f"eb{i}", [128, SEQ], BF16) for i in range(2)]
        en = [psb(f"en{i}", [128, SEQ], BF16) for i in range(2)]
        et = [psb(f"et{i}", [128, SEQ], BF16) for i in range(2)]
        ost = [psb(f"ost{i}", [128, SEQ], BF16) for i in range(2)]
        s_np = [psb(f"s_np{i}", [128, 16], F32) for i in range(2)]
        s_al = [psb(f"s_al{i}", [128, 8], F32) for i in range(2)]
        psS = pps("psS", [128, 3072], F32)
        psO = [pps(f"psO{i}", [128, 512], F32) for i in range(2)]
        b_dL, b_dX = dL.v, dX.v
        b_dLp = [dLp[0].v, dLp[1].v]
        b_dSp = [dSp[0].v, dSp[1].v]
        b_S, b_M, b_E, b_N, b_P, b_V = sS5.v, sM5.v, sE5.v, sN5.v, sP5.v, sV5.v
        b_F = sF5.v
        dS_tot = (dS.v, dSp[0].v, dSp[1].v)

        def ncnk(qt):
            kw = (qt + 1) * 128
            return (kw + 511) // 512, qt + 1, kw

        slot_of, sfree_wait = {}, {}
        slot_owner = [None] * 6
        gslot = 0
        for h in range(H):
            for qt in range(NT):
                Q = h * NT + qt
                nc512, _, _ = ncnk(qt)
                for cch in range(nc512):
                    sl = gslot % 6
                    gslot += 1
                    own = slot_owner[sl]
                    sfree_wait[(Q, cch)] = (own + 1) if own is not None else 0
                    slot_of[(Q, cch)] = sl
                    slot_owner[sl] = Q

        @blk.scalar
        def _(e):
            esync(e, "a")
            for s, v in zip((dS, dSp[0], dSp[1]), dS_tot):
                if v > 0:
                    e.wait_ge(s.h, v)
            for s, d in [(s_alw, din["alw"]), (s_alk, din["alk"]),
                         (s_dm, din["dmask"]), (s_nu, din["negu"])]:
                e.dma_start(out=s[:], in_=d[:, :]).then_inc(dL.h, 16)
            for kv in range(HKV):
                e.dma_start(out=kt[:, kv * SEQ:(kv + 1) * SEQ],
                            in_=kscr[kv * 128:(kv + 1) * 128, :]).then_inc(dL.h, 16)
            for h in range(2):
                e.dma_start(out=qb[h][:], in_=qhat[h * 128:(h + 1) * 128, :]) \
                    .then_inc(dLp[h].h, 16)
            def ostore(e, hs):
                wge(e, sV5, b_V + NT * (hs + 1))
                dsto = othbm[:, :, hs, :].rearrange("q p r -> p q r")
                srco = ost[hs % 2][:].rearrange("p (q r) -> p q r", r=128)
                e.dma_start(out=dsto, in_=srco).then_inc(dSp[hs % 2].h, 16)
            for h in range(H):
                for qt in range(NT):
                    Q = h * NT + qt
                    qp = Q % 2
                    nc512, nk, kw = ncnk(qt)
                    wge(e, sM5, b_M + Q + 1)
                    if Q >= 2:
                        wge(e, sN5, b_N + Q - 1)
                    for cch in range(nc512):
                        wc = min(512, kw - cch * 512)
                        sl = slot_of[(Q, cch)]
                        e.activation(eb[qp][:, cch * 512: cch * 512 + wc],
                                     psS[:, sl * 512: sl * 512 + wc], AF.Exp,
                                     bias=s_np[qp][:, 4:5],
                                     accum_out=s_al[qp][:, cch:cch + 1])
                    e.copy(s_al[qp][:, 4:5],
                           s_eps[:, 0:1]).then_inc(sE5.h, 1)
                    if qt == 0 and h >= 1:
                        ostore(e, h - 1)
                if h + 2 < H:
                    wge(e, sS5, b_S + NT * (h + 1))
                    e.dma_start(out=qb[h % 2][:],
                                in_=qhat[(h + 2) * 128:(h + 3) * 128, :]) \
                        .then_inc(dLp[h % 2].h, 16)
            ostore(e, H - 1)

        def _pv(e, Qp):
            hp, qtp = Qp // NT, Qp % NT
            kvp = hp // 4
            qpp = Qp % 2
            nkp = qtp + 1
            wge(e, dX, b_dX + 16 * (HKV + Qp + 1))
            if Qp >= 2:
                wge(e, sV5, b_V + Qp - 1)
            ins = None
            for kc in range(nkp):
                ins = e.matmul(psO[qpp][:, 0:128],
                               vv[:, kvp * SEQ + kc * 128: kvp * SEQ + (kc + 1) * 128],
                               et[qpp][:, kc * 128:(kc + 1) * 128],
                               start=(kc == 0), stop=(kc == nkp - 1))
            ins.then_inc(sP5.h, 1)

        def _evac(e, Qp):
            hp, qtp = Qp // NT, Qp % NT
            qpp = Qp % 2
            wge(e, sP5, b_P + Qp + 1)
            if hp >= 2 and qtp == 0:
                wge(e, dSp[hp % 2], b_dSp[hp % 2] + 16 * ((hp - 2) // 2 + 1))
            e.tensor_copy(ost[hp % 2][:, qtp * 128:(qtp + 1) * 128],
                          psO[qpp][:, 0:128]).then_inc(sV5.h, 1)

        @blk.tensor
        def _(e):
            esync(e, "t")
            wge(e, dL, b_dL + 64 + 16 * HKV)
            for h in range(H):
                kv = h // 4
                for qt in range(NT):
                    Q = h * NT + qt
                    qp = Q % 2
                    nc512, nk, kw = ncnk(qt)
                    wge(e, dLp[h % 2], b_dLp[h % 2] + 16 * (h // 2 + 1))
                    cl = nc512 - 1
                    off = qt * 128 - cl * 512
                    for cch in range(nc512):
                        wc = min(512, kw - cch * 512)
                        sl = slot_of[(Q, cch)]
                        diag = (cch == cl)
                        wge(e, sE5, b_E + sfree_wait[(Q, cch)])
                        e.matmul(psS[:, sl * 512: sl * 512 + wc],
                                 qb[h % 2][:, qt * 128:(qt + 1) * 128],
                                 kt[:, kv * SEQ + cch * 512:
                                    kv * SEQ + cch * 512 + wc],
                                 start=True, stop=False)
                        ins = e.matmul(psS[:, sl * 512: sl * 512 + wc],
                                       s_alw[0:4, h * 128:(h + 1) * 128],
                                       s_alk[0:4, cch * 512: cch * 512 + wc],
                                       start=False, stop=not diag)
                        if diag:
                            ins = e.matmul(psS[:, sl * 512 + off: sl * 512 + off + 128],
                                           s_ident[:], s_nu[:],
                                           start=False, stop=True)
                            ins.then_inc(sS5.h, 1)
                    if Q >= 1:
                        _pv(e, Q - 1)
            _pv(e, NQ - 1)

        @blk.vector
        def _(e):
            esync(e, "v")

            for Q in range(NQ + 1):
                qp = Q % 2
                qp1 = (Q - 1) % 2
                if Q >= 1:
                    nc1 = ncnk((Q - 1) % NT)[0]
                    wge(e, sE5, b_E + Q)
                    e.reduce_sum(s_np[qp1][:, 12:13], s_al[qp1][:, 0:nc1],
                                 axis=AX.X)
                if Q < NQ:
                    nc512, nk, kw = ncnk(Q % NT)
                    wge(e, sS5, b_S + Q + 1)
                    if Q >= 2:
                        wge(e, sE5, b_E + Q - 1)
                    for cch in range(nc512):
                        wc = min(512, kw - cch * 512)
                        sl = slot_of[(Q, cch)]
                        ins = e.reduce_max(s_np[qp][:, cch:cch + 1],
                                           psS[:, sl * 512: sl * 512 + wc],
                                           axis=AX.X, negate=True)
                    ins.then_inc(sF5.h, 1)
                    wge(e, sF5, b_F + Q + 1)
                else:
                    e.tensor_copy(s_np[0][:, 15:16], s_eps[:, 0:1])                         .then_inc(sF5.h, 1)
                    wge(e, sF5, b_F + Q + 1)
                if Q >= 1:
                    e.reciprocal(s_np[qp1][:, 13:14], s_np[qp1][:, 12:13])
                if Q < NQ:
                    nc512 = ncnk(Q % NT)[0]
                    e.tensor_reduce(s_np[qp][:, 4:5], s_np[qp][:, 0:nc512],
                                    axis=AX.X, op=OP.min).then_inc(sM5.h, 1)
                    wge(e, sM5, b_M + Q + 1)
                else:
                    e.tensor_copy(s_np[0][:, 14:15], s_eps[:, 0:1])                         .then_inc(sM5.h, 1)
                    wge(e, sM5, b_M + Q + 1)
                if Q >= 1:
                    kw1 = ncnk((Q - 1) % NT)[2]
                    if Q >= 3:
                        wge(e, dX, b_dX + 16 * (HKV + Q - 2))
                    e.tensor_scalar(out=en[qp1][:, 0:kw1], in0=eb[qp1][:, 0:kw1],
                                    scalar1=s_np[qp1][:, 13:14], scalar2=None,
                                    op0=OP.mult).then_inc(sN5.h, 1)
                if Q >= 2:
                    _evac(e, Q - 2)
            _evac(e, NQ - 1)

        @blk.sync
        def _(e):
            esync(e, "g")
            for s, v in zip((dS, dSp[0], dSp[1]), dS_tot):
                if v > 0:
                    e.wait_ge(s.h, v)
            for dc in range(HKV):
                dstv = vv[:, dc * SEQ:(dc + 1) * SEQ].rearrange(
                    "p (c r) -> p c r", r=128)
                e.dma_start_transpose(dstv, vsc[dc, :, :]).then_inc(dX.h, 16)
                wge(e, dX, b_dX + 16 * (dc + 1))
            for h in range(H):
                for qt in range(NT):
                    Q = h * NT + qt
                    qp = Q % 2
                    nc512, nk, kw = ncnk(qt)
                    wge(e, sN5, b_N + Q + 1)
                    if Q >= 2:
                        wge(e, sP5, b_P + Q - 1)
                    dstt = et[qp][:, 0:kw].rearrange("p (c r) -> p c r", r=128)
                    e.dma_start_transpose(dstt, en[qp][:, 0:kw]).then_inc(dX.h, 16)
                    wge(e, dX, b_dX + 16 * (HKV + Q + 1))
      dL.v += 64 + 16 * HKV
      dLp[0].v += 16 * (H // 2)
      dLp[1].v += 16 * (H // 2)
      dSp[0].v += 16 * (H // 2)
      dSp[1].v += 16 * (H // 2)
      sF5.v += NQ + 1
      dX.v += 16 * (HKV + NQ)
      sS5.v += NQ
      sM5.v += NQ + 1
      sE5.v += NQ
      sN5.v += NQ
      sP5.v += NQ
      sV5.v += NQ

    NJC = 4
    if phases >= 8:
      with ExitStack() as pctx, nc.Block() as blk:
        psb = lambda n, s, d, **kw: pctx.enter_context(nc.sbuf_tensor(n, s, d, **kw))
        pps = lambda n, s, d: pctx.enter_context(nc.psum_tensor(n, s, d))
        wo6 = [psb(f"wo6_{i}", [128, 32768], BF16) for i in range(2)]
        s_ot = [psb(f"s_ot{i}", [128, H * 128], BF16) for i in range(2)]
        s_g6 = [psb(f"s_g6{i}", [128, 1024], BF16) for i in range(2)]
        s_x6 = [psb(f"s_x6{i}", [128, 1024], F32) for i in range(2)]
        s_t6 = [psb(f"s_t6{i}", [128, 512], F32) for i in range(2)]
        s_o6 = [psb(f"s_o6{i}", [128, 1024], F32) for i in range(2)]
        py = [pps(f"py{i}", [128, 512], F32) for i in range(2)]
        b_dW, b_dL, b_dS = dW.v, dL.v, dS.v
        b_PE, b_DV = sPE.v, sDV.v
        b_dWp = [dWp[0].v, dWp[1].v]
        b_dLp = [dLp[0].v, dLp[1].v]
        b_dSp = [dSp[0].v, dSp[1].v]
        dS_prev6 = (dS.v, dSp[0].v, dSp[1].v)

        @blk.gpsimd
        def _(e):
            esync(e, "g")
            for jc in range(NJC):
                p = jc % 2
                if jc >= 2:
                    wge(e, sPE, b_PE + 2 * NT * (jc - 1))
                for cc in range(4):
                    src = din["wot"][cc * 1024:(cc + 1) * 1024,
                                     jc * 1024:(jc + 1) * 1024].rearrange(
                        "(c p) o -> p c o", p=128)
                    dst = wo6[p][:, cc * 8192:(cc + 1) * 8192].rearrange(
                        "p (c o) -> p c o", o=1024)
                    e.dma_start(out=dst, in_=src).then_inc(dWp[jc % 2].h, 16)

        @blk.scalar
        def _(e):
            esync(e, "a")
            wge(e, dS, dS_prev6[0])
            wge(e, dSp[0], dS_prev6[1])
            wge(e, dSp[1], dS_prev6[2])
            for jc in range(NJC):
                for tt in range(NT):
                    i = jc * NT + tt
                    p = i % 2
                    if i >= 2:
                        wge(e, sPE, b_PE + 2 * (i - 2) + 2)
                        wge(e, sDV, b_DV + 4 * (i - 2) + 4)
                    e.dma_start(out=s_ot[p][:],
                                in_=othbm[tt, :, :, :].rearrange(
                                    "p h r -> p (h r)")).then_inc(dLp[p].h, 16)
                    e.dma_start(out=s_g6[p][:],
                                in_=gsc[tt * 128:(tt + 1) * 128,
                                        jc * 1024:(jc + 1) * 1024]).then_inc(dLp[p].h, 16)
                    e.dma_start(out=s_x6[p][:],
                                in_=din["x"][tt * 128:(tt + 1) * 128,
                                             jc * 1024:(jc + 1) * 1024]).then_inc(dLp[p].h, 16)
                    wge(e, sDV, b_DV + 4 * i + 4)
                    e.dma_start(out=out[tt * 128:(tt + 1) * 128,
                                        jc * 1024:(jc + 1) * 1024],
                                in_=s_o6[p][:]).then_inc(dSp[p].h, 16)
            e.wait_ge(dSp[0].h, b_dSp[0] + 16 * (NJC * NT // 2))
            e.wait_ge(dSp[1].h, b_dSp[1] + 16 * (NJC * NT // 2))

        @blk.tensor
        def _(e):
            esync(e, "t")
            for jc in range(NJC):
                p = jc % 2
                wge(e, dWp[jc % 2], b_dWp[jc % 2] + 64 * (jc // 2 + 1))
                for tt in range(NT):
                    i = jc * NT + tt
                    pp = i % 2
                    wge(e, dLp[pp], b_dLp[pp] + 48 * (i // 2 + 1))
                    for j2 in range(2):
                        if i >= 1:
                            wge(e, sDV, b_DV + 4 * (i - 1) + 2 * j2 + 2)
                        for hc in range(H):
                            ins = e.matmul(py[j2][:, 0:512],
                                           s_ot[pp][:, hc * 128:(hc + 1) * 128],
                                           wo6[p][:, hc * 1024 + j2 * 512:
                                                  hc * 1024 + (j2 + 1) * 512],
                                           start=(hc == 0), stop=(hc == H - 1))
                        ins.then_inc(sPE.h, 1)

        @blk.vector
        def _(e):
            esync(e, "v")
            for jc in range(NJC):
                for tt in range(NT):
                    i = jc * NT + tt
                    pp = i % 2
                    wge(e, dLp[pp], b_dLp[pp] + 48 * (i // 2 + 1))
                    if i >= 2:
                        wge(e, dSp[pp], b_dSp[pp] + 16 * (i // 2))
                    for j2 in range(2):
                        wge(e, sPE, b_PE + 2 * i + j2 + 1)
                        e.scalar_tensor_tensor(
                            out=s_t6[j2][:], in0=py[j2][:, 0:512], scalar=1.0,
                            in1=s_g6[pp][:, j2 * 512:(j2 + 1) * 512],
                            op0=OP.mult, op1=OP.mult).then_inc(sDV.h, 1)
                    wge(e, sDV, b_DV + 4 * i + 2)
                    for j2 in range(2):
                        e.tensor_tensor(
                            out=s_o6[pp][:, j2 * 512:(j2 + 1) * 512],
                            in0=s_t6[j2][:],
                            in1=s_x6[pp][:, j2 * 512:(j2 + 1) * 512],
                            op=OP.add).then_inc(sDV.h, 1)
      dWp[0].v += 64 * (NJC // 2)
      dWp[1].v += 64 * (NJC // 2)
      dLp[0].v += 48 * (NJC * NT // 2)
      dLp[1].v += 48 * (NJC * NT // 2)
      dSp[0].v += 16 * (NJC * NT // 2)
      dSp[1].v += 16 * (NJC * NT // 2)
      sPE.v += 2 * NJC * NT
      sDV.v += 4 * NJC * NT

    with nc.Block() as blk:
        @blk.scalar
        def _(e):
            for s in (dL, dLp[0], dLp[1], dS, dSp[0], dSp[1], dW,
                      dWp[0], dWp[1], dX):
                if s.v > 0:
                    e.wait_ge(s.h, s.v)

    g.close()
    sctx.close()
    return nc, din, out, tap_out


def _host_layout(inputs):
    x = np.ascontiguousarray(np.asarray(inputs["L0_attn_x"], dtype=np.float32))
    wq = np.asarray(inputs["L0_attn_Wq"], np.float32)
    wk = np.asarray(inputs["L0_attn_Wk"], np.float32)
    wv = np.asarray(inputs["L0_attn_Wv"], np.float32)
    wo = np.asarray(inputs["L0_attn_Wo"], np.float32)
    w = np.asarray(inputs["L0_ffn_W"], np.float32)
    b = np.asarray(inputs["L0_ffn_b"], np.float32)
    A = np.asarray(inputs["L0_ffn_A"], np.float32)
    Bm = np.asarray(inputs["L0_ffn_Bm"], np.float32)
    Wr = np.asarray(inputs["L0_ffn_Wr"], np.float32)
    qnw = np.asarray(inputs["L0_attn_q_norm_w"], np.float32).reshape(128, 1)
    knw = np.asarray(inputs["L0_attn_k_norm_w"], np.float32).reshape(128, 1)

    ident, dmask, alw, alk, negu = host_constants()
    com = {
        "wqt": np.ascontiguousarray(wq.T),
        "wkt": np.ascontiguousarray(wk.T),
        "wvt": np.ascontiguousarray(wv.T),
        "wot": np.ascontiguousarray(wo.T),
        "wt": np.ascontiguousarray(w.T),
        "wrtb": np.ascontiguousarray(
            Wr.T.reshape(32, 128, 4).transpose(1, 0, 2).reshape(128, 128)),
        "atb": np.ascontiguousarray(
            A.reshape(32, DM).T.reshape(32, 128, 32).transpose(1, 0, 2)
            .reshape(128, 1024)),
        "bmt": np.ascontiguousarray(Bm.transpose(0, 2, 1).reshape(32, DM)),
        "bb": np.ascontiguousarray(np.broadcast_to(b, (128, DM))),
        "qnw": np.ascontiguousarray(qnw),
        "knw": np.ascontiguousarray(knw),
        "ident": ident, "dmask": dmask, "alw": alw, "alk": alk, "negu": negu,
    }
    return x, com


def kernel(**inputs):
    x, com = _host_layout(inputs)
    nc, din, out_ap, taps = build()
    in_maps = [dict(com, x=np.ascontiguousarray(x[c])) for c in range(8)]
    res = run_bass_kernel_spmd(nc, in_maps, core_ids=list(range(8)))
    return np.stack([np.asarray(r["out"]) for r in res.results], axis=0)


# version 02
# speedup vs baseline: 1.1196x; 1.1196x over previous
import math
from contextlib import ExitStack

import numpy as np
import ml_dtypes

import concourse.bass as bass
import concourse.mybir as mybir
from concourse.alu_op_type import AluOpType as OP
from concourse.bass_utils import run_bass_kernel_spmd

F32 = mybir.dt.float32
BF16 = mybir.dt.bfloat16
AF = mybir.ActivationFunctionType
AX = mybir.AxisListType

SEQ = 2048
DM = 4096
H = 32
HKV = 8
NT = 16
NIC = 32
NEG = -1.0e30
LN_EPS = 1.0e-5
RMS_EPS = 1.0e-6
INV_SQRT_HD = 1.0 / math.sqrt(128.0)


class Sem:
    def __init__(self, ctx, nc, name):
        self.h = ctx.enter_context(nc.semaphore(name))
        self.v = 0


def _alibi_slopes(n_heads):
    start = 2 ** (-(2 ** -(math.log2(n_heads) - 3)))
    return np.array([start * start ** i for i in range(n_heads)], dtype=np.float32)


def host_constants():
    bf = ml_dtypes.bfloat16
    ident = np.eye(128, dtype=np.float32).astype(bf)
    p = np.arange(128)
    dmask = np.where(p[None, :] <= p[:, None], 0.0, NEG).astype(np.float32)
    slopes = _alibi_slopes(H)
    alw = np.zeros((4, H * 128), dtype=np.float32)
    for h in range(H):
        s_hi = np.float32(bf(slopes[h]))
        s_lo = np.float32(bf(np.float32(slopes[h]) - s_hi))
        alw[0, h * 128:(h + 1) * 128] = 256.0 * s_hi
        alw[1, h * 128:(h + 1) * 128] = s_hi
        alw[2, h * 128:(h + 1) * 128] = 256.0 * s_lo
        alw[3, h * 128:(h + 1) * 128] = s_lo
    alw = alw.astype(bf)
    k = np.arange(SEQ)
    alk = np.zeros((4, SEQ), dtype=np.float32)
    alk[0] = (k // 256).astype(np.float32)
    alk[1] = (k % 256).astype(np.float32)
    alk[2] = alk[0]
    alk[3] = alk[1]
    alk = alk.astype(bf)
    negu = np.where(p[None, :] > p[:, None], NEG, 0.0).astype(np.float32).astype(bf)
    return ident, dmask, alw, alk, negu


def build(taps=(), phases=8):
    nc = bass.Bass("TRN2", target_bir_lowering=False)

    din = {}
    for name, shape, dt in [
        ("x", [SEQ, DM], F32),
        ("wqt", [DM, DM], F32),
        ("wkt", [DM, HKV * 128], F32),
        ("wvt", [DM, HKV * 128], F32),
        ("wot", [DM, DM], F32),
        ("wt", [DM, DM], F32),
        ("wrtb", [128, 128], F32),
        ("atb", [128, 1024], F32),
        ("bmt", [32, DM], F32),
        ("bb", [128, DM], F32),
        ("qnw", [128, 1], F32),
        ("knw", [128, 1], F32),
        ("ident", [128, 128], BF16),
        ("dmask", [128, 128], F32),
        ("alw", [4, H * 128], BF16),
        ("alk", [4, SEQ], BF16),
        ("negu", [128, 128], BF16),
    ]:
        din[name] = nc.dram_tensor(name, shape, dt, kind="ExternalInput").ap()

    def scratch(name, shape):
        kind = "ExternalOutput" if name in taps else "Internal"
        return nc.dram_tensor(name, shape, BF16, kind=kind).ap()
    qhat = scratch("qhat", [H * 128, SEQ])
    kscr = scratch("kscr", [HKV * 128, SEQ])
    othbm = scratch("ot", [NT, 128, H, 128])
    gsc = scratch("gsc", [SEQ, DM])
    vsc = scratch("vsc", [HKV, 128, SEQ])
    out = nc.dram_tensor("out", [SEQ, DM], F32, kind="ExternalOutput").ap()

    tap_out = {}
    for t in taps:
        if t in ("qhat", "kscr", "ot", "gsc", "vsc"):
            continue
        if t == "dbg":
            tap_out[t] = nc.dram_tensor("tap_dbg", [128, 16], F32,
                                        kind="ExternalOutput").ap()
            tap_out["dbg2"] = nc.dram_tensor("tap_dbg2", [128, 2], F32,
                                             kind="ExternalOutput").ap()
            continue
        shp = {"xnt": [128, NIC * SEQ]}[t]
        tap_out[t] = nc.dram_tensor("tap_" + t, shp, BF16, kind="ExternalOutput").ap()

    g = ExitStack()
    sb = lambda n, s, d, **kw: g.enter_context(nc.sbuf_tensor(n, s, d, **kw))
    s_ident = sb("s_ident", [128, 128], BF16)
    s_qnw = sb("s_qnw", [128, 1], F32)
    s_knw = sb("s_knw", [128, 1], F32)
    s_wqe = sb("s_wqe", [128, 1], F32)
    s_ones128 = sb("s_ones128", [128, 1], BF16)
    s_ones1f = sb("s_ones1f", [1, 128], F32)
    s_cols = sb("s_cols", [128, 16], F32)
    s_eps = sb("s_eps", [128, 2], F32)
    s_kols = sb("s_kols", [128, 4], F32)
    s_pgT = sb("s_pgT", [32, SEQ], BF16)

    sctx = ExitStack()
    dL = Sem(sctx, nc, "dL")
    dLp = [Sem(sctx, nc, f"dL{i}") for i in range(2)]
    dS = Sem(sctx, nc, "dS")
    dSp = [Sem(sctx, nc, f"dS{i}") for i in range(2)]
    dW = Sem(sctx, nc, "dW")
    dWp = [Sem(sctx, nc, f"dW{i}") for i in range(2)]
    dX = Sem(sctx, nc, "dX")
    sPE = Sem(sctx, nc, "sPE")
    sDV = Sem(sctx, nc, "sDV")
    sAC = Sem(sctx, nc, "sAC")
    sS5 = Sem(sctx, nc, "sS5")
    sM5 = Sem(sctx, nc, "sM5")
    sE5 = Sem(sctx, nc, "sE5")
    sN5 = Sem(sctx, nc, "sN5")
    sP5 = Sem(sctx, nc, "sP5")
    sV5 = Sem(sctx, nc, "sV5")
    sF5 = Sem(sctx, nc, "sF5")

    def wge(e, sem, val):
        if val > 0:
            e.wait_ge(sem.h, val)

    def esync(e, kind):
        for s in (sPE, sDV, sAC, sS5, sP5, sV5, sM5, sN5, sE5, sF5,
                  dL, dS, dW, dX, dLp[0], dLp[1], dSp[0], dSp[1],
                  dWp[0], dWp[1]):
            if s.v > 0:
                e.wait_ge(s.h, s.v)

    xnt_ctx = ExitStack()
    xnt = xnt_ctx.enter_context(nc.sbuf_tensor("xnt", [128, NIC * SEQ], BF16))

    def xn_stat(ic, tt):
        return xnt[:, tt * 4096 + ic * 128: tt * 4096 + (ic + 1) * 128]

    def xn_mov(ic, tc):
        v3 = xnt[:, :].rearrange("p (a b) -> p a b", b=4096)
        return v3[:, tc * 4:(tc + 1) * 4, ic * 128:(ic + 1) * 128]

    with nc.Block() as blk:
        @blk.scalar
        def _(e):
            esync(e, "a")
            for s, d in [(s_ident, din["ident"]), (s_qnw, din["qnw"]),
                         (s_knw, din["knw"])]:
                e.dma_start(out=s[:], in_=d[:, :]).then_inc(dL.h, 16)

        @blk.vector
        def _(e):
            esync(e, "v")
            e.wait_ge(dL.h, dL.v + 48)
            e.memset(s_ones128[:], 1.0)
            e.memset(s_ones1f[:], 1.0)
            e.memset(s_eps[:, 0:1], LN_EPS)
            e.memset(s_eps[:, 1:2], RMS_EPS)
            e.tensor_scalar_mul(s_wqe[:], s_qnw[:], INV_SQRT_HD).then_inc(sDV.h, 1)
    dL.v += 48
    sDV.v += 1

    if phases >= 1:
      with ExitStack() as pctx, nc.Block() as blk:
        psb = lambda n, s, d, **kw: pctx.enter_context(nc.sbuf_tensor(n, s, d, **kw))
        xt = [psb(f"xt{i}", [128, DM], F32) for i in range(2)]
        ast = [psb(f"ast{i}", [128, DM], BF16) for i in range(2)]
        b_dX, b_AC, b_DV = dX.v, sAC.v, sDV.v
        b_dLp = [dLp[0].v, dLp[1].v]

        @blk.scalar
        def _(e):
            esync(e, "a")
            for tt in range(NT):
                p = tt % 2
                if tt >= 2:
                    wge(e, sDV, b_DV + 6 * (tt - 2) + 6)
                e.dma_start(out=xt[p][:],
                            in_=din["x"][tt * 128:(tt + 1) * 128, :]) \
                    .then_inc(dLp[p].h, 16)
                wge(e, sDV, b_DV + 6 * tt + 4)
                e.activation(s_kols[:, 0:1], s_cols[:, 4:5], AF.Sqrt,
                             scale=1.0 / DM, bias=s_eps[:, 0:1]).then_inc(sAC.h, 1)

        @blk.vector
        def _(e):
            esync(e, "v")
            for tt in range(NT):
                p = tt % 2
                wge(e, dLp[p], b_dLp[p] + 16 * (tt // 2 + 1))
                e.reduce_sum(s_cols[:, 0:1], xt[p][:], axis=AX.X)
                if tt >= 2:
                    wge(e, dX, b_dX + 16 * (tt - 1))
                e.tensor_tensor(out=ast[p][:], in0=xt[p][:], in1=xt[p][:],
                                op=OP.mult).then_inc(sDV.h, 1)
                wge(e, sDV, b_DV + 6 * tt + 1)
                e.tensor_scalar(out=ast[p][:], in0=ast[p][:], scalar1=1.0,
                                scalar2=0.0, op0=OP.mult, op1=OP.add,
                                accum_out=s_cols[:, 1:2]).then_inc(sDV.h, 1)
                wge(e, sDV, b_DV + 6 * tt + 2)
                e.tensor_scalar(out=s_cols[:, 5:6], in0=s_cols[:, 0:1],
                                scalar1=1.0 / DM, scalar2=s_cols[:, 0:1],
                                op0=OP.mult, op1=OP.mult)
                e.tensor_scalar(out=s_cols[:, 2:3], in0=s_cols[:, 0:1],
                                scalar1=1.0 / DM, scalar2=None,
                                op0=OP.mult).then_inc(sDV.h, 1)
                wge(e, sDV, b_DV + 6 * tt + 3)
                if tt >= 1:
                    wge(e, sAC, b_AC + tt)
                e.tensor_tensor(out=s_cols[:, 4:5], in0=s_cols[:, 1:2],
                                in1=s_cols[:, 5:6],
                                op=OP.subtract).then_inc(sDV.h, 1)
                wge(e, sAC, b_AC + tt + 1)
                e.reciprocal(s_cols[:, 7:8], s_kols[:, 0:1]).then_inc(sDV.h, 1)
                wge(e, sDV, b_DV + 6 * tt + 5)
                e.tensor_scalar(out=ast[p][:], in0=xt[p][:],
                                scalar1=s_cols[:, 2:3], scalar2=s_cols[:, 7:8],
                                op0=OP.subtract, op1=OP.mult).then_inc(sDV.h, 1)

        @blk.sync
        def _(e):
            esync(e, "g")
            for tt in range(NT):
                p = tt % 2
                wge(e, sDV, b_DV + 6 * tt + 6)
                dst = xnt[:, tt * 4096:(tt + 1) * 4096].rearrange(
                    "p (c r) -> p c r", r=128)
                e.dma_start_transpose(dst, ast[p][:]).then_inc(dX.h, 16)
                wge(e, dX, b_dX + 16 * (tt + 1))
      dLp[0].v += 16 * (NT // 2)
      dLp[1].v += 16 * (NT // 2)
      sAC.v += NT
      sDV.v += 6 * NT
      dX.v += 16 * NT

    if phases >= 2:
      with ExitStack() as pctx, nc.Block() as blk:
        psb = lambda n, s, d, **kw: pctx.enter_context(nc.sbuf_tensor(n, s, d, **kw))
        pps = lambda n, s, d: pctx.enter_context(nc.psum_tensor(n, s, d))
        s_wrt = psb("s_wrt", [128, 128], BF16)
        s_at = psb("s_at", [128, 1024], BF16)
        s_eg = psb("s_eg", [128, 4], F32)
        s_g4 = psb("s_g4", [128, 4], F32)
        s_pg = [psb(f"s_pg{i}", [128, 32], BF16) for i in range(2)]
        pgp = [pps(f"pgp{i}", [128, 512], F32) for i in range(2)]
        ppgt = [pps(f"ppgt{i}", [128, 1024], BF16) for i in range(2)]
        b_dW = dW.v
        b_PE, b_DV, b_AC = sPE.v, sDV.v, sAC.v
        pe_mm, pe_tr = [0] * NT, [0] * NT
        c = 0
        for tt in range(NT):
            c += 1
            pe_mm[tt] = c
            if tt >= 1:
                c += 1
                pe_tr[tt - 1] = c
        c += 1
        pe_tr[NT - 1] = c
        n_pe_pro = c
        dv_negmax, dv_projg, dv_evac = [0] * NT, [0] * NT, [0] * NT
        c = 0
        for tt in range(NT):
            c += 1; dv_negmax[tt] = c
            c += 1
            c += 1
            c += 1; dv_projg[tt] = c
            if tt >= 1:
                c += 1; dv_evac[tt - 1] = c
        c += 1; dv_evac[NT - 1] = c
        n_dv_pro = c

        @blk.gpsimd
        def _(e):
            esync(e, "g")
            b = dW.v
            for k, (s, d) in enumerate([(s_wrt, din["wrtb"]), (s_at, din["atb"])]):
                e.dma_start(out=s[:], in_=d[:, :]).then_inc(dW.h, 16)
                e.wait_ge(dW.h, b + 16 * (k + 1))

        @blk.tensor
        def _(e):
            esync(e, "t")
            wge(e, dW, b_dW + 16 * 2)
            wge(e, dL, 48)
            wge(e, dX, dX.v)
            for tt in range(NT):
                p = tt % 2
                if tt >= 2:
                    wge(e, sDV, b_DV + dv_projg[tt - 2])
                for ic in range(NIC):
                    e.matmul(pgp[p][:, 0:4], xn_stat(ic, tt),
                             s_wrt[:, ic * 4:(ic + 1) * 4],
                             start=(ic == 0), stop=(ic == NIC - 1))
                for ic in range(NIC):
                    ins = e.matmul(pgp[p][:, 4:36], xn_stat(ic, tt),
                                   s_at[:, ic * 32:(ic + 1) * 32],
                                   start=(ic == 0), stop=(ic == NIC - 1))
                ins.then_inc(sPE.h, 1)
                if tt >= 1:
                    wge(e, sDV, b_DV + dv_projg[tt - 1])
                    e.transpose(ppgt[(tt - 1) % 2][0:32, 0:128],
                                s_pg[(tt - 1) % 2][:], s_ident[:]).then_inc(sPE.h, 1)
            wge(e, sDV, b_DV + dv_projg[NT - 1])
            e.transpose(ppgt[(NT - 1) % 2][0:32, 0:128],
                        s_pg[(NT - 1) % 2][:], s_ident[:]).then_inc(sPE.h, 1)

        @blk.vector
        def _(e):
            esync(e, "v")
            for tt in range(NT):
                p = tt % 2
                wge(e, sPE, b_PE + pe_mm[tt])
                if tt >= 1:
                    wge(e, sAC, b_AC + tt)
                e.reduce_max(s_cols[:, 8:9], pgp[p][:, 0:4], axis=AX.X,
                             negate=True).then_inc(sDV.h, 1)
                wge(e, sAC, b_AC + tt + 1)
                e.reciprocal(s_cols[:, 10:11], s_kols[:, 2:3]).then_inc(sDV.h, 1)
                wge(e, sDV, b_DV + dv_negmax[tt] + 1)
                e.tensor_scalar(out=s_g4[:], in0=s_eg[:],
                                scalar1=s_cols[:, 10:11], scalar2=2.0,
                                op0=OP.mult, op1=OP.mult).then_inc(sDV.h, 1)
                wge(e, sDV, b_DV + dv_negmax[tt] + 2)
                for ee in range(4):
                    ins = e.tensor_scalar(out=s_pg[p][:, ee * 8:(ee + 1) * 8],
                                          in0=pgp[p][:, 4 + ee * 8: 4 + (ee + 1) * 8],
                                          scalar1=s_g4[:, ee:ee + 1], scalar2=None,
                                          op0=OP.mult)
                ins.then_inc(sDV.h, 1)
                if tt >= 1:
                    wge(e, sPE, b_PE + pe_tr[tt - 1])
                    e.tensor_copy(s_pgT[0:32, (tt - 1) * 128: tt * 128],
                                  ppgt[(tt - 1) % 2][0:32, 0:128]).then_inc(sDV.h, 1)
            wge(e, sPE, b_PE + pe_tr[NT - 1])
            e.tensor_copy(s_pgT[0:32, (NT - 1) * 128: NT * 128],
                          ppgt[(NT - 1) % 2][0:32, 0:128]).then_inc(sDV.h, 1)

        @blk.scalar
        def _(e):
            esync(e, "a")
            for tt in range(NT):
                wge(e, sDV, b_DV + dv_negmax[tt])
                e.activation(s_eg[:], pgp[tt % 2][:, 0:4], AF.Exp,
                             bias=s_cols[:, 8:9],
                             accum_out=s_kols[:, 2:3])
                e.copy(s_kols[:, 1:2], s_eps[:, 0:1]).then_inc(sAC.h, 1)
      dW.v += 16 * 2
      sPE.v += n_pe_pro
      sAC.v += NT
      sDV.v += n_dv_pro

    NOC = 16
    if phases >= 3:
      with ExitStack() as pctx, nc.Block() as blk:
        psb = lambda n, s, d, **kw: pctx.enter_context(nc.sbuf_tensor(n, s, d, **kw))
        pps = lambda n, s, d: pctx.enter_context(nc.psum_tensor(n, s, d))
        wbuf2 = psb("wbuf2", [128, 16384], BF16)
        s_bmt = psb("s_bmt", [32, DM], BF16)
        s_bb = psb("s_bb", [128, DM], F32)
        s_f1 = [psb(f"s_f1{i}", [128, 256], F32) for i in range(2)]
        s_gt = [psb(f"s_gt{i}", [128, 256], BF16) for i in range(2)]
        pg = [pps(f"pg{i}", [128, 512], F32) for i in range(2)]
        b_dW, b_dS, b_dL = dW.v, dS.v, dL.v
        b_PE, b_DV, b_AC = sPE.v, sDV.v, sAC.v
        b_dWp = [dWp[0].v, dWp[1].v]
        b_dSp = [dSp[0].v, dSp[1].v]

        @blk.gpsimd
        def _(e):
            esync(e, "g")
            e.dma_start(out=s_bmt[:], in_=din["bmt"][:, :]).then_inc(dW.h, 16)
            e.wait_ge(dW.h, b_dW + 16)
            for oc in range(NOC):
                p = oc % 2
                if oc >= 2:
                    wge(e, sPE, b_PE + NT * (oc - 1))
                src = din["wt"][:, oc * 256:(oc + 1) * 256].rearrange(
                    "(c p) o -> p c o", p=128)
                dst = wbuf2[:, p * 8192:(p + 1) * 8192].rearrange(
                    "p (c o) -> p c o", o=256)
                e.dma_start(out=dst, in_=src).then_inc(dWp[p].h, 16)

        @blk.tensor
        def _(e):
            esync(e, "t")
            wge(e, dW, b_dW + 16)
            for oc in range(NOC):
                p = oc % 2
                wge(e, dWp[p], b_dWp[p] + 16 * (oc // 2 + 1))
                for tt in range(NT):
                    i = oc * NT + tt
                    pp = i % 2
                    if i >= 2:
                        wge(e, sDV, b_DV + (i - 2) + 1)
                    for ic in range(NIC):
                        e.matmul(pg[pp][:, 0:256],
                                 xn_stat(ic, tt),
                                 wbuf2[:, p * 8192 + ic * 256: p * 8192 + (ic + 1) * 256],
                                 start=(ic == 0), stop=False)
                    e.matmul(pg[pp][:, 0:256],
                             s_pgT[0:32, tt * 128:(tt + 1) * 128],
                             s_bmt[0:32, oc * 256:(oc + 1) * 256],
                             start=False, stop=True).then_inc(sPE.h, 1)

        @blk.scalar
        def _(e):
            esync(e, "a")
            e.dma_start(out=s_bb[:], in_=din["bb"][:, :]).then_inc(dL.h, 16)
            for oc in range(NOC):
                for tt in range(NT):
                    i = oc * NT + tt
                    pp = i % 2
                    wge(e, sDV, b_DV + i + 1)
                    if i >= 2:
                        wge(e, dSp[pp], b_dSp[pp] + 16 * ((i - 2) // 2 + 1))
                    e.activation(s_gt[pp][:], s_f1[pp][:], AF.Silu).then_inc(sAC.h, 1)
                    wge(e, sAC, b_AC + i + 1)
                    e.dma_start(out=gsc[tt * 128:(tt + 1) * 128,
                                        oc * 256:(oc + 1) * 256],
                                in_=s_gt[pp][:]).then_inc(dSp[pp].h, 16)

        @blk.vector
        def _(e):
            esync(e, "v")
            wge(e, dL, b_dL + 16)
            for oc in range(NOC):
                for tt in range(NT):
                    i = oc * NT + tt
                    pp = i % 2
                    wge(e, sPE, b_PE + i + 1)
                    if i >= 2:
                        wge(e, sAC, b_AC + (i - 2) + 1)
                    e.scalar_tensor_tensor(
                        out=s_f1[pp][:], in0=pg[pp][:, 0:256], scalar=1.0,
                        in1=s_bb[:, oc * 256:(oc + 1) * 256],
                        op0=OP.mult, op1=OP.add).then_inc(sDV.h, 1)
      dW.v += 16
      dWp[0].v += 16 * (NOC // 2)
      dWp[1].v += 16 * (NOC // 2)
      dL.v += 16
      dSp[0].v += 16 * (NOC * NT // 2)
      dSp[1].v += 16 * (NOC * NT // 2)
      sPE.v += NOC * NT
      sAC.v += NOC * NT
      sDV.v += NOC * NT

    def proj_phase(pctx, blk, heads, wsrc, dest_kind):
        psb = lambda n, s, d, **kw: pctx.enter_context(nc.sbuf_tensor(n, s, d, **kw))
        pps = lambda n, s, d: pctx.enter_context(nc.psum_tensor(n, s, d))
        wb = psb("wb_" + dest_kind, [128, 8192], BF16)
        psQ = pps("psQ_" + dest_kind, [128, 2048], F32)
        pbc = pps("pbc_" + dest_kind, [128, 2048], F32)
        norm = dest_kind in ("q", "k")
        if norm:
            s_sq = psb("sq_" + dest_kind, [128, 2048], BF16)
            s_w = psb("sw_" + dest_kind, [128, 2048], F32)
            s_rr = psb("srr_" + dest_kind, [1, 2048], F32)
        s_st = [psb(f"st_{dest_kind}{i}", [128, 2048], BF16) for i in range(2)]
        ndst = qhat if dest_kind == "q" else kscr
        wcol = s_wqe if dest_kind == "q" else s_knw
        b_dW, b_dS = dW.v, dS.v
        b_PE, b_DV, b_AC = sPE.v, sDV.v, sAC.v
        b_dWp = [dWp[0].v, dWp[1].v]
        b_dSp = [dSp[0].v, dSp[1].v]

        if norm:
            pe_mm, pe_msq, pe_bc = [0] * heads, [0] * heads, [0] * heads
            c = 0
            for h in range(heads + 1):
                if h < heads:
                    c += 1; pe_mm[h] = c
                if h >= 1:
                    c += 1; pe_msq[h - 1] = c
                    c += 1; pe_bc[h - 1] = c
            n_pe = c
            ac_sq, ac_ln, ac_rs = [0] * heads, [0] * heads, [0] * heads
            c = 0
            for h in range(heads + 1):
                if h < heads:
                    c += 1; ac_sq[h] = c
                if h >= 1:
                    c += 1; ac_ln[h - 1] = c
                    c += 1; ac_rs[h - 1] = c
            n_ac = c
            dv_w, dv_stt = [0] * heads, [0] * heads
            c = 0
            for h in range(heads + 1):
                if h >= 1:
                    c += 1; dv_stt[h - 1] = c
                if h < heads:
                    c += 1; dv_w[h] = c
            n_dv = c
        else:
            pe_mm = [h + 1 for h in range(heads)]
            n_pe = heads
            dv_cp = [h + 1 for h in range(heads)]
            n_dv = heads
            n_ac = 0

        @blk.gpsimd
        def _(e):
            esync(e, "g")
            for h in range(heads):
                p = h % 2
                for half in range(2):
                    ch = 2 * h + half
                    if h >= 1:
                        wge(e, sPE, b_PE + pe_mm[h - 1])
                    src = wsrc[half * 2048:(half + 1) * 2048,
                               h * 128:(h + 1) * 128].rearrange(
                        "(c p) o -> p c o", p=128)
                    dst = wb[:, p * 4096 + half * 2048:
                             p * 4096 + (half + 1) * 2048].rearrange(
                        "p (c o) -> p c o", o=128)
                    e.dma_start(out=dst, in_=src).then_inc(dWp[ch % 2].h, 16)

        @blk.tensor
        def _(e):
            esync(e, "t")
            for h in range(heads + 1):
                if h < heads:
                    p = h % 2
                    if h >= 1:
                        if norm:
                            wge(e, sDV, b_DV + dv_w[h - 1])
                            wge(e, sAC, b_AC + ac_sq[h - 1])
                        else:
                            wge(e, sDV, b_DV + dv_cp[h - 1])
                    for half in range(2):
                        ch = 2 * h + half
                        wge(e, dWp[ch % 2], b_dWp[ch % 2] + 16 * (ch // 2 + 1))
                        for tc in range(4):
                            for ici in range(16):
                                ic = half * 16 + ici
                                ins = e.matmul(
                                    psQ[:, tc * 512:(tc + 1) * 512],
                                    wb[:, p * 4096 + half * 2048 + ici * 128:
                                       p * 4096 + half * 2048 + (ici + 1) * 128],
                                    xn_mov(ic, tc),
                                    start=(ic == 0), stop=(ic == NIC - 1))
                    ins.then_inc(sPE.h, 1)
                if norm and h >= 1:
                    hh = h - 1
                    wge(e, sAC, b_AC + ac_sq[hh])
                    if hh >= 1:
                        wge(e, sDV, b_DV + dv_stt[hh - 1])
                    for tc in range(4):
                        ins = e.matmul(pbc[0:1, tc * 512:(tc + 1) * 512],
                                       s_ones128[:],
                                       s_sq[:, tc * 512:(tc + 1) * 512],
                                       start=True, stop=True)
                    ins.then_inc(sPE.h, 1)
                    wge(e, sAC, b_AC + ac_rs[hh])
                    for tc in range(4):
                        ins = e.matmul(pbc[:, tc * 512:(tc + 1) * 512], s_ones1f[:],
                                       s_rr[0:1, tc * 512:(tc + 1) * 512],
                                       start=True, stop=True)
                    ins.then_inc(sPE.h, 1)

        if norm:
            @blk.scalar
            def _(e):
                esync(e, "a")
                for h in range(heads + 2):
                    if h < heads:
                        wge(e, sPE, b_PE + pe_mm[h])
                        if h >= 1:
                            wge(e, sPE, b_PE + pe_msq[h - 1])
                        e.activation(s_sq[:], psQ[:],
                                     AF.Square).then_inc(sAC.h, 1)
                    if 1 <= h <= heads:
                        hh = h - 1
                        wge(e, sPE, b_PE + pe_msq[hh])
                        e.activation(pbc[32:33, 0:2048], pbc[0:1, 0:2048], AF.Ln,
                                     scale=1.0 / 128.0, bias=s_eps[0:1, 1:2]).then_inc(sAC.h, 1)
                        if hh >= 1:
                            wge(e, sPE, b_PE + pe_bc[hh - 1])
                        wge(e, sAC, b_AC + ac_ln[hh])
                        e.activation(s_rr[:], pbc[32:33, 0:2048], AF.Exp,
                                     scale=-0.5).then_inc(sAC.h, 1)
                    if h >= 2:
                        hs = h - 2
                        wge(e, sDV, b_DV + dv_stt[hs])
                        e.dma_start(out=ndst[hs * 128:(hs + 1) * 128, :],
                                    in_=s_st[hs % 2][:]).then_inc(dSp[hs % 2].h, 16)
        else:
            @blk.scalar
            def _(e):
                esync(e, "a")
                for h in range(heads):
                    wge(e, sDV, b_DV + dv_cp[h])
                    e.dma_start(out=vsc[h, :, :],
                                in_=s_st[h % 2][:]).then_inc(dSp[h % 2].h, 16)

        @blk.vector
        def _(e):
            esync(e, "v")
            for h in range(heads + 1):
                if norm and h >= 1:
                    hh = h - 1
                    wge(e, sPE, b_PE + pe_bc[hh])
                    if hh >= 2:
                        wge(e, dSp[hh % 2], b_dSp[hh % 2] + 16 * ((hh - 2) // 2 + 1))
                    wge(e, sDV, b_DV + dv_w[hh])
                    dst = s_st[hh % 2][:]
                    e.scalar_tensor_tensor(out=dst, in0=pbc[:], scalar=1.0,
                                           in1=s_w[:], op0=OP.mult,
                                           op1=OP.mult).then_inc(sDV.h, 1)
                if h < heads and norm:
                    wge(e, sPE, b_PE + pe_mm[h])
                    if h >= 1:
                        wge(e, sDV, b_DV + dv_stt[h - 1])
                    e.tensor_scalar(out=s_w[:], in0=psQ[:], scalar1=wcol[:, 0:1],
                                    scalar2=None, op0=OP.mult).then_inc(sDV.h, 1)
                if h < heads and not norm:
                    wge(e, sPE, b_PE + pe_mm[h])
                    if h >= 2:
                        wge(e, dSp[h % 2], b_dSp[h % 2] + 16 * ((h - 2) // 2 + 1))
                    e.tensor_copy(s_st[h % 2][:], psQ[:]).then_inc(sDV.h, 1)

        return dict(dWp0=16 * heads, dWp1=16 * heads,
                    dSp0=16 * ((heads + 1) // 2), dSp1=16 * (heads // 2),
                    sPE=n_pe, sDV=n_dv, sAC=n_ac)

    def apply_inc(inc):
        m = {"dWp0": dWp[0], "dWp1": dWp[1], "dSp0": dSp[0], "dSp1": dSp[1],
             "sPE": sPE, "sDV": sDV, "sAC": sAC}
        for k_, v_ in inc.items():
            m[k_].v += v_

    if phases >= 4:
        with ExitStack() as pctx, nc.Block() as blk:
            apply_inc(proj_phase(pctx, blk, H, din["wqt"], "q"))
    if phases >= 5:
        with ExitStack() as pctx, nc.Block() as blk:
            apply_inc(proj_phase(pctx, blk, HKV, din["wkt"], "k"))
    if phases >= 6:
        with ExitStack() as pctx, nc.Block() as blk:
            apply_inc(proj_phase(pctx, blk, HKV, din["wvt"], "v"))

    if tap_out:
        with nc.Block() as blk:
            b_dS = dS.v
            ntap = len(tap_out)

            @blk.scalar
            def _(e):
                esync(e, "a")
                e.wait_ge(dX.h, dX.v)
                n = 0
                if "xnt" in tap_out:
                    n += 1
                    e.dma_start(out=tap_out["xnt"][:, :], in_=xnt[:]).then_inc(dS.h, 16)
                if "dbg" in tap_out:
                    n += 2
                    e.dma_start(out=tap_out["dbg"][:, :], in_=s_cols[:]).then_inc(dS.h, 16)
                    e.dma_start(out=tap_out["dbg2"][:, :],
                                in_=s_eps[:]).then_inc(dS.h, 16)
                e.wait_ge(dS.h, b_dS + 16 * n)
        dS.v += 16 * ntap

    xnt_ctx.close()

    NQ = H * NT
    if phases >= 7:
      with ExitStack() as pctx, nc.Block() as blk:
        psb = lambda n, s, d, **kw: pctx.enter_context(nc.sbuf_tensor(n, s, d, **kw))
        pps = lambda n, s, d: pctx.enter_context(nc.psum_tensor(n, s, d))
        vv = psb("vv", [128, HKV * SEQ], BF16)
        kt = psb("kt", [128, HKV * SEQ], BF16)
        s_alw = psb("s_alw", [4, H * 128], BF16)
        s_alk = psb("s_alk", [4, SEQ], BF16)
        s_nu = psb("s_nu", [128, 128], BF16)
        qb = [psb(f"qb{i}", [128, SEQ], BF16) for i in range(2)]
        eb = [psb(f"eb{i}", [128, SEQ], BF16) for i in range(4)]
        en = [psb(f"en{i}", [128, SEQ], BF16) for i in range(4)]
        et = [psb(f"et{i}", [128, SEQ], BF16) for i in range(6)]
        ost = [psb(f"ost{i}", [128, SEQ], BF16) for i in range(2)]
        s_np = [psb(f"s_np{i}", [128, 16], F32) for i in range(4)]
        s_al = [psb(f"s_al{i}", [128, 8], F32) for i in range(4)]
        psS = pps("psS", [128, 3072], F32)
        psO = [pps(f"psO{i}", [128, 512], F32) for i in range(2)]
        b_dL, b_dX = dL.v, dX.v
        b_dLp = [dLp[0].v, dLp[1].v]
        b_dSp = [dSp[0].v, dSp[1].v]
        b_S, b_M, b_E, b_N, b_P, b_V = sS5.v, sM5.v, sE5.v, sN5.v, sP5.v, sV5.v
        b_F = sF5.v
        dS_tot = (dS.v, dSp[0].v, dSp[1].v)
        NCONST = 3

        def ncnk(qt):
            kw = (qt + 1) * 128
            return (kw + 511) // 512, qt + 1, kw

        slot_of, sfree_wait = {}, {}
        slot_owner = [None] * 6
        gslot = 0
        for idx in range(NQ):
            for cch in range(ncnk(idx % NT)[0]):
                sl = gslot % 6
                gslot += 1
                own = slot_owner[sl]
                sfree_wait[(idx, cch)] = (own + 1) if own is not None else 0
                slot_of[(idx, cch)] = sl
                slot_owner[sl] = idx

        @blk.scalar
        def _(e):
            esync(e, "a")
            for s, v in zip((dS, dSp[0], dSp[1]), dS_tot):
                if v > 0:
                    e.wait_ge(s.h, v)
            for s, d in [(s_alw, din["alw"]), (s_alk, din["alk"]),
                         (s_nu, din["negu"])]:
                e.dma_start(out=s[:], in_=d[:, :]).then_inc(dL.h, 16)
            for kv in range(HKV):
                e.dma_start(out=kt[:, kv * SEQ:(kv + 1) * SEQ],
                            in_=kscr[kv * 128:(kv + 1) * 128, :]).then_inc(dL.h, 16)
            for h in range(2):
                e.dma_start(out=qb[h][:],
                            in_=qhat[h * 128:(h + 1) * 128, :]).then_inc(dLp[h].h, 16)

            def ostore(e, hs):
                wge(e, sV5, b_V + NT * (hs + 1))
                dsto = othbm[:, :, hs, :].rearrange("q p r -> p q r")
                srco = ost[hs % 2][:].rearrange("p (q r) -> p q r", r=128)
                e.dma_start(out=dsto, in_=srco).then_inc(dSp[hs % 2].h, 16)

            for idx in range(NQ):
                h, qt = idx // NT, idx % NT
                qp = idx % 4
                nc512, nk, kw = ncnk(qt)
                wge(e, sM5, b_M + idx + 1)
                if idx >= 4:
                    wge(e, sN5, b_N + idx - 3)
                for cch in range(nc512):
                    wc = min(512, kw - cch * 512)
                    sl = slot_of[(idx, cch)]
                    e.activation(eb[qp][:, cch * 512: cch * 512 + wc],
                                 psS[:, sl * 512: sl * 512 + wc], AF.Exp,
                                 bias=s_np[qp][:, 4:5],
                                 accum_out=s_al[qp][:, cch:cch + 1])
                e.copy(s_al[qp][:, 4:5],
                       s_eps[:, 0:1]).then_inc(sE5.h, 1)
                if idx >= 17 and (idx - 17) % NT == 0:
                    ostore(e, (idx - 17) // NT)
                if qt == NT - 1 and h + 2 < H:
                    wge(e, sS5, b_S + NT * (h + 1))
                    e.dma_start(out=qb[h % 2][:],
                                in_=qhat[(h + 2) * 128:(h + 3) * 128, :]
                                ).then_inc(dLp[h % 2].h, 16)
            ostore(e, H - 1)

        def _pv(e, I):
            hp, qtp = I // NT, I % NT
            kvp = hp // 4
            nkp = qtp + 1
            wge(e, dX, b_dX + 16 * (HKV + I + 1))
            if I >= 2:
                wge(e, sV5, b_V + I - 1)
            ins = None
            for kc in range(nkp):
                ins = e.matmul(psO[I % 2][:, 0:128],
                               vv[:, kvp * SEQ + kc * 128: kvp * SEQ + (kc + 1) * 128],
                               et[I % 6][:, kc * 128:(kc + 1) * 128],
                               start=(kc == 0), stop=(kc == nkp - 1))
            ins.then_inc(sP5.h, 1)

        @blk.tensor
        def _(e):
            esync(e, "t")
            wge(e, dL, b_dL + 16 * (NCONST + HKV))
            for idx in range(NQ):
                h, qt = idx // NT, idx % NT
                kv = h // 4
                nc512, nk, kw = ncnk(qt)
                wge(e, dLp[h % 2], b_dLp[h % 2] + 16 * (h // 2 + 1))
                cl = nc512 - 1
                off = qt * 128 - cl * 512
                for cch in range(nc512):
                    wc = min(512, kw - cch * 512)
                    sl = slot_of[(idx, cch)]
                    diag = (cch == cl)
                    wge(e, sE5, b_E + sfree_wait[(idx, cch)])
                    e.matmul(psS[:, sl * 512: sl * 512 + wc],
                             qb[h % 2][:, qt * 128:(qt + 1) * 128],
                             kt[:, kv * SEQ + cch * 512:
                                kv * SEQ + cch * 512 + wc],
                             start=True, stop=False)
                    ins = e.matmul(psS[:, sl * 512: sl * 512 + wc],
                                   s_alw[0:4, h * 128:(h + 1) * 128],
                                   s_alk[0:4, cch * 512: cch * 512 + wc],
                                   start=False, stop=not diag)
                    if diag:
                        ins = e.matmul(psS[:, sl * 512 + off: sl * 512 + off + 128],
                                       s_ident[:], s_nu[:],
                                       start=False, stop=True)
                        ins.then_inc(sS5.h, 1)
                if idx >= 4:
                    _pv(e, idx - 4)
            for I in range(NQ - 4, NQ):
                _pv(e, I)

        def _evac(e, I):
            hp, qtp = I // NT, I % NT
            wge(e, sP5, b_P + I + 1)
            if hp >= 2 and qtp == 0:
                wge(e, dSp[hp % 2], b_dSp[hp % 2] + 16 * ((hp - 2) // 2 + 1))
            e.tensor_copy(ost[hp % 2][:, qtp * 128:(qtp + 1) * 128],
                          psO[I % 2][:, 0:128]).then_inc(sV5.h, 1)

        @blk.vector
        def _(e):
            esync(e, "v")
            for idx in range(NQ + 4):
                qp = idx % 4
                qp2 = (idx - 2) % 4
                if 2 <= idx < NQ + 2:
                    nc2 = ncnk((idx - 2) % NT)[0]
                    wge(e, sE5, b_E + idx - 1)
                    e.reduce_sum(s_np[qp2][:, 12:13], s_al[qp2][:, 0:nc2],
                                 axis=AX.X)
                if idx < NQ:
                    nc512, nk, kw = ncnk(idx % NT)
                    wge(e, sS5, b_S + idx + 1)
                    for cch in range(nc512):
                        wc = min(512, kw - cch * 512)
                        sl = slot_of[(idx, cch)]
                        ins = e.reduce_max(s_np[qp][:, cch:cch + 1],
                                           psS[:, sl * 512: sl * 512 + wc],
                                           axis=AX.X, negate=True)
                    ins.then_inc(sF5.h, 1)
                else:
                    e.tensor_copy(s_np[0][:, 15:16],
                                  s_eps[:, 0:1]).then_inc(sF5.h, 1)
                wge(e, sF5, b_F + idx + 1)
                if 2 <= idx < NQ + 2:
                    e.reciprocal(s_np[qp2][:, 13:14], s_np[qp2][:, 12:13])
                if idx < NQ:
                    nc512 = ncnk(idx % NT)[0]
                    e.tensor_reduce(s_np[qp][:, 4:5], s_np[qp][:, 0:nc512],
                                    axis=AX.X, op=OP.min).then_inc(sM5.h, 1)
                else:
                    e.tensor_copy(s_np[0][:, 14:15],
                                  s_eps[:, 0:1]).then_inc(sM5.h, 1)
                wge(e, sM5, b_M + idx + 1)
                if 2 <= idx < NQ + 2:
                    kw2 = ncnk((idx - 2) % NT)[2]
                    if idx >= 6:
                        wge(e, dX, b_dX + 16 * (HKV + idx - 5))
                    e.tensor_scalar(out=en[qp2][:, 0:kw2], in0=eb[qp2][:, 0:kw2],
                                    scalar1=s_np[qp2][:, 13:14], scalar2=None,
                                    op0=OP.mult).then_inc(sN5.h, 1)
                if idx >= 4:
                    _evac(e, idx - 4)

        @blk.sync
        def _(e):
            esync(e, "g")
            for s, v in zip((dS, dSp[0], dSp[1]), dS_tot):
                if v > 0:
                    e.wait_ge(s.h, v)
            for dc in range(HKV):
                dstv = vv[:, dc * SEQ:(dc + 1) * SEQ].rearrange(
                    "p (c r) -> p c r", r=128)
                e.dma_start_transpose(dstv, vsc[dc, :, :]).then_inc(dX.h, 16)
                wge(e, dX, b_dX + 16 * (dc + 1))
            for idx in range(NQ):
                qp = idx % 4
                kw = ncnk(idx % NT)[2]
                wge(e, sN5, b_N + idx + 1)
                if idx >= 6:
                    wge(e, sP5, b_P + idx - 5)
                dstt = et[idx % 6][:, 0:kw].rearrange("p (c r) -> p c r", r=128)
                e.dma_start_transpose(dstt, en[qp][:, 0:kw]).then_inc(dX.h, 16)
                wge(e, dX, b_dX + 16 * (HKV + idx + 1))
      dL.v += 16 * (3 + HKV)
      dLp[0].v += 16 * (H // 2)
      dLp[1].v += 16 * (H // 2)
      dSp[0].v += 16 * (H // 2)
      dSp[1].v += 16 * (H // 2)
      sF5.v += NQ + 4
      dX.v += 16 * (HKV + NQ)
      sS5.v += NQ
      sM5.v += NQ + 4
      sE5.v += NQ
      sN5.v += NQ
      sP5.v += NQ
      sV5.v += NQ

    NJC = 4
    if phases >= 8:
      with ExitStack() as pctx, nc.Block() as blk:
        psb = lambda n, s, d, **kw: pctx.enter_context(nc.sbuf_tensor(n, s, d, **kw))
        pps = lambda n, s, d: pctx.enter_context(nc.psum_tensor(n, s, d))
        wo6 = [psb(f"wo6_{i}", [128, 32768], BF16) for i in range(2)]
        s_ot = [psb(f"s_ot{i}", [128, H * 128], BF16) for i in range(2)]
        s_g6 = [psb(f"s_g6{i}", [128, 1024], BF16) for i in range(2)]
        s_x6 = [psb(f"s_x6{i}", [128, 1024], F32) for i in range(2)]
        s_t6 = [psb(f"s_t6{i}", [128, 512], F32) for i in range(2)]
        s_o6 = [psb(f"s_o6{i}", [128, 1024], F32) for i in range(2)]
        py = [pps(f"py{i}", [128, 512], F32) for i in range(2)]
        b_dW, b_dL, b_dS = dW.v, dL.v, dS.v
        b_PE, b_DV = sPE.v, sDV.v
        b_dWp = [dWp[0].v, dWp[1].v]
        b_dLp = [dLp[0].v, dLp[1].v]
        b_dSp = [dSp[0].v, dSp[1].v]
        dS_prev6 = (dS.v, dSp[0].v, dSp[1].v)

        @blk.gpsimd
        def _(e):
            esync(e, "g")
            for jc in range(NJC):
                p = jc % 2
                if jc >= 2:
                    wge(e, sPE, b_PE + 2 * NT * (jc - 1))
                for cc in range(4):
                    src = din["wot"][cc * 1024:(cc + 1) * 1024,
                                     jc * 1024:(jc + 1) * 1024].rearrange(
                        "(c p) o -> p c o", p=128)
                    dst = wo6[p][:, cc * 8192:(cc + 1) * 8192].rearrange(
                        "p (c o) -> p c o", o=1024)
                    e.dma_start(out=dst, in_=src).then_inc(dWp[jc % 2].h, 16)

        @blk.scalar
        def _(e):
            esync(e, "a")
            wge(e, dS, dS_prev6[0])
            wge(e, dSp[0], dS_prev6[1])
            wge(e, dSp[1], dS_prev6[2])
            for jc in range(NJC):
                for tt in range(NT):
                    i = jc * NT + tt
                    p = i % 2
                    if i >= 2:
                        wge(e, sPE, b_PE + 2 * (i - 2) + 2)
                        wge(e, sDV, b_DV + 4 * (i - 2) + 4)
                    e.dma_start(out=s_ot[p][:],
                                in_=othbm[tt, :, :, :].rearrange(
                                    "p h r -> p (h r)")).then_inc(dLp[p].h, 16)
                    e.dma_start(out=s_g6[p][:],
                                in_=gsc[tt * 128:(tt + 1) * 128,
                                        jc * 1024:(jc + 1) * 1024]).then_inc(dLp[p].h, 16)
                    e.dma_start(out=s_x6[p][:],
                                in_=din["x"][tt * 128:(tt + 1) * 128,
                                             jc * 1024:(jc + 1) * 1024]).then_inc(dLp[p].h, 16)
                    wge(e, sDV, b_DV + 4 * i + 4)
                    e.dma_start(out=out[tt * 128:(tt + 1) * 128,
                                        jc * 1024:(jc + 1) * 1024],
                                in_=s_o6[p][:]).then_inc(dSp[p].h, 16)
            e.wait_ge(dSp[0].h, b_dSp[0] + 16 * (NJC * NT // 2))
            e.wait_ge(dSp[1].h, b_dSp[1] + 16 * (NJC * NT // 2))

        @blk.tensor
        def _(e):
            esync(e, "t")
            for jc in range(NJC):
                p = jc % 2
                wge(e, dWp[jc % 2], b_dWp[jc % 2] + 64 * (jc // 2 + 1))
                for tt in range(NT):
                    i = jc * NT + tt
                    pp = i % 2
                    wge(e, dLp[pp], b_dLp[pp] + 48 * (i // 2 + 1))
                    for j2 in range(2):
                        if i >= 1:
                            wge(e, sDV, b_DV + 4 * (i - 1) + 2 * j2 + 2)
                        for hc in range(H):
                            ins = e.matmul(py[j2][:, 0:512],
                                           s_ot[pp][:, hc * 128:(hc + 1) * 128],
                                           wo6[p][:, hc * 1024 + j2 * 512:
                                                  hc * 1024 + (j2 + 1) * 512],
                                           start=(hc == 0), stop=(hc == H - 1))
                        ins.then_inc(sPE.h, 1)

        @blk.vector
        def _(e):
            esync(e, "v")
            for jc in range(NJC):
                for tt in range(NT):
                    i = jc * NT + tt
                    pp = i % 2
                    wge(e, dLp[pp], b_dLp[pp] + 48 * (i // 2 + 1))
                    if i >= 2:
                        wge(e, dSp[pp], b_dSp[pp] + 16 * (i // 2))
                    for j2 in range(2):
                        wge(e, sPE, b_PE + 2 * i + j2 + 1)
                        e.scalar_tensor_tensor(
                            out=s_t6[j2][:], in0=py[j2][:, 0:512], scalar=1.0,
                            in1=s_g6[pp][:, j2 * 512:(j2 + 1) * 512],
                            op0=OP.mult, op1=OP.mult).then_inc(sDV.h, 1)
                    wge(e, sDV, b_DV + 4 * i + 2)
                    for j2 in range(2):
                        e.tensor_tensor(
                            out=s_o6[pp][:, j2 * 512:(j2 + 1) * 512],
                            in0=s_t6[j2][:],
                            in1=s_x6[pp][:, j2 * 512:(j2 + 1) * 512],
                            op=OP.add).then_inc(sDV.h, 1)
      dWp[0].v += 64 * (NJC // 2)
      dWp[1].v += 64 * (NJC // 2)
      dLp[0].v += 48 * (NJC * NT // 2)
      dLp[1].v += 48 * (NJC * NT // 2)
      dSp[0].v += 16 * (NJC * NT // 2)
      dSp[1].v += 16 * (NJC * NT // 2)
      sPE.v += 2 * NJC * NT
      sDV.v += 4 * NJC * NT

    with nc.Block() as blk:
        @blk.scalar
        def _(e):
            for s in (dL, dLp[0], dLp[1], dS, dSp[0], dSp[1], dW,
                      dWp[0], dWp[1], dX):
                if s.v > 0:
                    e.wait_ge(s.h, s.v)

    g.close()
    sctx.close()
    return nc, din, out, tap_out


def _host_layout(inputs):
    x = np.ascontiguousarray(np.asarray(inputs["L0_attn_x"], dtype=np.float32))
    wq = np.asarray(inputs["L0_attn_Wq"], np.float32)
    wk = np.asarray(inputs["L0_attn_Wk"], np.float32)
    wv = np.asarray(inputs["L0_attn_Wv"], np.float32)
    wo = np.asarray(inputs["L0_attn_Wo"], np.float32)
    w = np.asarray(inputs["L0_ffn_W"], np.float32)
    b = np.asarray(inputs["L0_ffn_b"], np.float32)
    A = np.asarray(inputs["L0_ffn_A"], np.float32)
    Bm = np.asarray(inputs["L0_ffn_Bm"], np.float32)
    Wr = np.asarray(inputs["L0_ffn_Wr"], np.float32)
    qnw = np.asarray(inputs["L0_attn_q_norm_w"], np.float32).reshape(128, 1)
    knw = np.asarray(inputs["L0_attn_k_norm_w"], np.float32).reshape(128, 1)

    ident, dmask, alw, alk, negu = host_constants()
    com = {
        "wqt": np.ascontiguousarray(wq.T),
        "wkt": np.ascontiguousarray(wk.T),
        "wvt": np.ascontiguousarray(wv.T),
        "wot": np.ascontiguousarray(wo.T),
        "wt": np.ascontiguousarray(w.T),
        "wrtb": np.ascontiguousarray(
            Wr.T.reshape(32, 128, 4).transpose(1, 0, 2).reshape(128, 128)),
        "atb": np.ascontiguousarray(
            A.reshape(32, DM).T.reshape(32, 128, 32).transpose(1, 0, 2)
            .reshape(128, 1024)),
        "bmt": np.ascontiguousarray(Bm.transpose(0, 2, 1).reshape(32, DM)),
        "bb": np.ascontiguousarray(np.broadcast_to(b, (128, DM))),
        "qnw": np.ascontiguousarray(qnw),
        "knw": np.ascontiguousarray(knw),
        "ident": ident, "dmask": dmask, "alw": alw, "alk": alk, "negu": negu,
    }
    return x, com


def kernel(**inputs):
    x, com = _host_layout(inputs)
    nc, din, out_ap, taps = build()
    in_maps = [dict(com, x=np.ascontiguousarray(x[c])) for c in range(8)]
    res = run_bass_kernel_spmd(nc, in_maps, core_ids=list(range(8)))
    return np.stack([np.asarray(r["out"]) for r in res.results], axis=0)


# version 03
# speedup vs baseline: 1.3176x; 1.1768x over previous
import math
from contextlib import ExitStack

import numpy as np
import ml_dtypes

import concourse.bass as bass
import concourse.mybir as mybir
from concourse.alu_op_type import AluOpType as OP
from concourse.bass_utils import run_bass_kernel_spmd

F32 = mybir.dt.float32
BF16 = mybir.dt.bfloat16
AF = mybir.ActivationFunctionType
AX = mybir.AxisListType

SEQ = 2048
DM = 4096
H = 32
HKV = 8
NT = 16
NIC = 32
NEG = -1.0e30
LN_EPS = 1.0e-5
RMS_EPS = 1.0e-6
INV_SQRT_HD = 1.0 / math.sqrt(128.0)


class Sem:
    def __init__(self, ctx, nc, name):
        self.h = ctx.enter_context(nc.semaphore(name))
        self.v = 0


def _alibi_slopes(n_heads):
    start = 2 ** (-(2 ** -(math.log2(n_heads) - 3)))
    return np.array([start * start ** i for i in range(n_heads)], dtype=np.float32)


def host_constants():
    bf = ml_dtypes.bfloat16
    ident = np.eye(128, dtype=np.float32).astype(bf)
    p = np.arange(128)
    dmask = np.where(p[None, :] <= p[:, None], 0.0, NEG).astype(np.float32)
    slopes = _alibi_slopes(H)
    alw = np.zeros((4, H * 128), dtype=np.float32)
    for h in range(H):
        s_hi = np.float32(bf(slopes[h]))
        s_lo = np.float32(bf(np.float32(slopes[h]) - s_hi))
        alw[0, h * 128:(h + 1) * 128] = 256.0 * s_hi
        alw[1, h * 128:(h + 1) * 128] = s_hi
        alw[2, h * 128:(h + 1) * 128] = 256.0 * s_lo
        alw[3, h * 128:(h + 1) * 128] = s_lo
    alw = alw.astype(bf)
    k = np.arange(SEQ)
    alk = np.zeros((4, SEQ), dtype=np.float32)
    alk[0] = (k // 256).astype(np.float32)
    alk[1] = (k % 256).astype(np.float32)
    alk[2] = alk[0]
    alk[3] = alk[1]
    alk = alk.astype(bf)
    negu = np.where(p[None, :] > p[:, None], NEG, 0.0).astype(np.float32).astype(bf)
    return ident, dmask, alw, alk, negu


def build(taps=(), phases=8):
    nc = bass.Bass("TRN2", target_bir_lowering=False)

    din = {}
    for name, shape, dt in [
        ("x", [SEQ, DM], F32),
        ("wqt", [DM, DM], F32),
        ("wkt", [DM, HKV * 128], F32),
        ("wvt", [DM, HKV * 128], F32),
        ("wot", [DM, DM], F32),
        ("wt", [DM, DM], F32),
        ("wrtb", [128, 128], F32),
        ("atb", [128, 1024], F32),
        ("bmt", [32, DM], F32),
        ("bb", [128, DM], F32),
        ("qnw", [128, 1], F32),
        ("knw", [128, 1], F32),
        ("ident", [128, 128], BF16),
        ("dmask", [128, 128], F32),
        ("alw", [4, H * 128], BF16),
        ("alk", [4, SEQ], BF16),
        ("negu", [128, 128], BF16),
    ]:
        din[name] = nc.dram_tensor(name, shape, dt, kind="ExternalInput").ap()

    def scratch(name, shape):
        kind = "ExternalOutput" if name in taps else "Internal"
        return nc.dram_tensor(name, shape, BF16, kind=kind).ap()
    qhat = scratch("qhat", [H * 128, SEQ])
    kscr = scratch("kscr", [HKV * 128, SEQ])
    othbm = scratch("ot", [NT, 128, H, 128])
    gsc = scratch("gsc", [SEQ, DM])
    vsc = scratch("vsc", [HKV, 128, SEQ])
    out = nc.dram_tensor("out", [SEQ, DM], F32, kind="ExternalOutput").ap()

    tap_out = {}
    for t in taps:
        if t in ("qhat", "kscr", "ot", "gsc", "vsc"):
            continue
        if t == "dbg":
            tap_out[t] = nc.dram_tensor("tap_dbg", [128, 16], F32,
                                        kind="ExternalOutput").ap()
            tap_out["dbg2"] = nc.dram_tensor("tap_dbg2", [128, 2], F32,
                                             kind="ExternalOutput").ap()
            continue
        shp = {"xnt": [128, NIC * SEQ]}[t]
        tap_out[t] = nc.dram_tensor("tap_" + t, shp, BF16, kind="ExternalOutput").ap()

    g = ExitStack()
    sb = lambda n, s, d, **kw: g.enter_context(nc.sbuf_tensor(n, s, d, **kw))
    s_ident = sb("s_ident", [128, 128], BF16)
    s_qnw = sb("s_qnw", [128, 1], F32)
    s_knw = sb("s_knw", [128, 1], F32)
    s_wqe = sb("s_wqe", [128, 1], F32)
    s_ones128 = sb("s_ones128", [128, 1], BF16)
    s_ones1f = sb("s_ones1f", [1, 128], F32)
    s_cols = sb("s_cols", [128, 16], F32)
    s_eps = sb("s_eps", [128, 2], F32)
    s_kols = sb("s_kols", [128, 4], F32)
    s_pgT = sb("s_pgT", [32, SEQ], BF16)

    sctx = ExitStack()
    dL = Sem(sctx, nc, "dL")
    dLp = [Sem(sctx, nc, f"dL{i}") for i in range(2)]
    dS = Sem(sctx, nc, "dS")
    dSp = [Sem(sctx, nc, f"dS{i}") for i in range(2)]
    dW = Sem(sctx, nc, "dW")
    dWp = [Sem(sctx, nc, f"dW{i}") for i in range(2)]
    dX = Sem(sctx, nc, "dX")
    sPE = Sem(sctx, nc, "sPE")
    sDV = Sem(sctx, nc, "sDV")
    sAC = Sem(sctx, nc, "sAC")
    sS5 = Sem(sctx, nc, "sS5")
    sM5 = Sem(sctx, nc, "sM5")
    sE5 = Sem(sctx, nc, "sE5")
    sN5 = Sem(sctx, nc, "sN5")
    sP5 = Sem(sctx, nc, "sP5")
    sV5 = Sem(sctx, nc, "sV5")
    sF5 = Sem(sctx, nc, "sF5")

    def wge(e, sem, val):
        if val > 0:
            e.wait_ge(sem.h, val)

    def esync(e, kind):
        for s in (sPE, sDV, sAC, sS5, sP5, sV5, sM5, sN5, sE5, sF5,
                  dL, dS, dW, dX, dLp[0], dLp[1], dSp[0], dSp[1],
                  dWp[0], dWp[1]):
            if s.v > 0:
                e.wait_ge(s.h, s.v)

    xnt_ctx = ExitStack()
    xnt = xnt_ctx.enter_context(nc.sbuf_tensor("xnt", [128, NIC * SEQ], BF16))

    def xn_stat(ic, tt):
        return xnt[:, tt * 4096 + ic * 128: tt * 4096 + (ic + 1) * 128]

    def xn_mov(ic, tc):
        v3 = xnt[:, :].rearrange("p (a b) -> p a b", b=4096)
        return v3[:, tc * 4:(tc + 1) * 4, ic * 128:(ic + 1) * 128]

    with nc.Block() as blk:
        @blk.scalar
        def _(e):
            esync(e, "a")
            for s, d in [(s_ident, din["ident"]), (s_qnw, din["qnw"]),
                         (s_knw, din["knw"])]:
                e.dma_start(out=s[:], in_=d[:, :]).then_inc(dL.h, 16)

        @blk.vector
        def _(e):
            esync(e, "v")
            e.wait_ge(dL.h, dL.v + 48)
            e.memset(s_ones128[:], 1.0)
            e.memset(s_ones1f[:], 1.0)
            e.memset(s_eps[:, 0:1], LN_EPS)
            e.memset(s_eps[:, 1:2], RMS_EPS)
            e.tensor_scalar_mul(s_wqe[:], s_qnw[:], INV_SQRT_HD).then_inc(sDV.h, 1)
    dL.v += 48
    sDV.v += 1

    if phases >= 1:
      with ExitStack() as pctx, nc.Block() as blk:
        psb = lambda n, s, d, **kw: pctx.enter_context(nc.sbuf_tensor(n, s, d, **kw))
        xt = [psb(f"xt{i}", [128, DM], F32) for i in range(2)]
        ast = [psb(f"ast{i}", [128, DM], BF16) for i in range(2)]
        b_dX, b_AC, b_DV = dX.v, sAC.v, sDV.v
        b_dLp = [dLp[0].v, dLp[1].v]

        @blk.scalar
        def _(e):
            esync(e, "a")
            for tt in range(NT):
                p = tt % 2
                if tt >= 2:
                    wge(e, sDV, b_DV + 6 * (tt - 2) + 6)
                e.dma_start(out=xt[p][:],
                            in_=din["x"][tt * 128:(tt + 1) * 128, :]) \
                    .then_inc(dLp[p].h, 16)
                wge(e, sDV, b_DV + 6 * tt + 4)
                e.activation(s_kols[:, 0:1], s_cols[:, 4:5], AF.Sqrt,
                             scale=1.0 / DM, bias=s_eps[:, 0:1]).then_inc(sAC.h, 1)

        @blk.vector
        def _(e):
            esync(e, "v")
            for tt in range(NT):
                p = tt % 2
                wge(e, dLp[p], b_dLp[p] + 16 * (tt // 2 + 1))
                e.reduce_sum(s_cols[:, 0:1], xt[p][:], axis=AX.X)
                if tt >= 2:
                    wge(e, dX, b_dX + 16 * (tt - 1))
                e.tensor_tensor(out=ast[p][:], in0=xt[p][:], in1=xt[p][:],
                                op=OP.mult).then_inc(sDV.h, 1)
                wge(e, sDV, b_DV + 6 * tt + 1)
                e.tensor_scalar(out=ast[p][:], in0=ast[p][:], scalar1=1.0,
                                scalar2=0.0, op0=OP.mult, op1=OP.add,
                                accum_out=s_cols[:, 1:2]).then_inc(sDV.h, 1)
                wge(e, sDV, b_DV + 6 * tt + 2)
                e.tensor_scalar(out=s_cols[:, 5:6], in0=s_cols[:, 0:1],
                                scalar1=1.0 / DM, scalar2=s_cols[:, 0:1],
                                op0=OP.mult, op1=OP.mult)
                e.tensor_scalar(out=s_cols[:, 2:3], in0=s_cols[:, 0:1],
                                scalar1=1.0 / DM, scalar2=None,
                                op0=OP.mult).then_inc(sDV.h, 1)
                wge(e, sDV, b_DV + 6 * tt + 3)
                if tt >= 1:
                    wge(e, sAC, b_AC + tt)
                e.tensor_tensor(out=s_cols[:, 4:5], in0=s_cols[:, 1:2],
                                in1=s_cols[:, 5:6],
                                op=OP.subtract).then_inc(sDV.h, 1)
                wge(e, sAC, b_AC + tt + 1)
                e.reciprocal(s_cols[:, 7:8], s_kols[:, 0:1]).then_inc(sDV.h, 1)
                wge(e, sDV, b_DV + 6 * tt + 5)
                e.tensor_scalar(out=ast[p][:], in0=xt[p][:],
                                scalar1=s_cols[:, 2:3], scalar2=s_cols[:, 7:8],
                                op0=OP.subtract, op1=OP.mult).then_inc(sDV.h, 1)

        @blk.sync
        def _(e):
            esync(e, "g")
            for tt in range(NT):
                p = tt % 2
                wge(e, sDV, b_DV + 6 * tt + 6)
                dst = xnt[:, tt * 4096:(tt + 1) * 4096].rearrange(
                    "p (c r) -> p c r", r=128)
                e.dma_start_transpose(dst, ast[p][:]).then_inc(dX.h, 16)
                wge(e, dX, b_dX + 16 * (tt + 1))
      dLp[0].v += 16 * (NT // 2)
      dLp[1].v += 16 * (NT // 2)
      sAC.v += NT
      sDV.v += 6 * NT
      dX.v += 16 * NT

    if phases >= 2:
      with ExitStack() as pctx, nc.Block() as blk:
        psb = lambda n, s, d, **kw: pctx.enter_context(nc.sbuf_tensor(n, s, d, **kw))
        pps = lambda n, s, d: pctx.enter_context(nc.psum_tensor(n, s, d))
        s_wrt = psb("s_wrt", [128, 128], BF16)
        s_at = psb("s_at", [128, 1024], BF16)
        s_eg = psb("s_eg", [128, 4], F32)
        s_g4 = psb("s_g4", [128, 4], F32)
        s_pg = [psb(f"s_pg{i}", [128, 32], BF16) for i in range(2)]
        pgp = [pps(f"pgp{i}", [128, 512], F32) for i in range(2)]
        ppgt = [pps(f"ppgt{i}", [128, 1024], BF16) for i in range(2)]
        b_dW = dW.v
        b_PE, b_DV, b_AC = sPE.v, sDV.v, sAC.v
        pe_mm, pe_tr = [0] * NT, [0] * NT
        c = 0
        for tt in range(NT):
            c += 1
            pe_mm[tt] = c
            if tt >= 1:
                c += 1
                pe_tr[tt - 1] = c
        c += 1
        pe_tr[NT - 1] = c
        n_pe_pro = c
        dv_negmax, dv_projg, dv_evac = [0] * NT, [0] * NT, [0] * NT
        c = 0
        for tt in range(NT):
            c += 1; dv_negmax[tt] = c
            c += 1
            c += 1
            c += 1; dv_projg[tt] = c
            if tt >= 1:
                c += 1; dv_evac[tt - 1] = c
        c += 1; dv_evac[NT - 1] = c
        n_dv_pro = c

        @blk.gpsimd
        def _(e):
            esync(e, "g")
            b = dW.v
            for k, (s, d) in enumerate([(s_wrt, din["wrtb"]), (s_at, din["atb"])]):
                e.dma_start(out=s[:], in_=d[:, :]).then_inc(dW.h, 16)
                e.wait_ge(dW.h, b + 16 * (k + 1))

        @blk.tensor
        def _(e):
            esync(e, "t")
            wge(e, dW, b_dW + 16 * 2)
            wge(e, dL, 48)
            wge(e, dX, dX.v)
            for tt in range(NT):
                p = tt % 2
                if tt >= 2:
                    wge(e, sDV, b_DV + dv_projg[tt - 2])
                for ic in range(NIC):
                    e.matmul(pgp[p][:, 0:4], xn_stat(ic, tt),
                             s_wrt[:, ic * 4:(ic + 1) * 4],
                             start=(ic == 0), stop=(ic == NIC - 1))
                for ic in range(NIC):
                    ins = e.matmul(pgp[p][:, 4:36], xn_stat(ic, tt),
                                   s_at[:, ic * 32:(ic + 1) * 32],
                                   start=(ic == 0), stop=(ic == NIC - 1))
                ins.then_inc(sPE.h, 1)
                if tt >= 1:
                    wge(e, sDV, b_DV + dv_projg[tt - 1])
                    e.transpose(ppgt[(tt - 1) % 2][0:32, 0:128],
                                s_pg[(tt - 1) % 2][:], s_ident[:]).then_inc(sPE.h, 1)
            wge(e, sDV, b_DV + dv_projg[NT - 1])
            e.transpose(ppgt[(NT - 1) % 2][0:32, 0:128],
                        s_pg[(NT - 1) % 2][:], s_ident[:]).then_inc(sPE.h, 1)

        @blk.vector
        def _(e):
            esync(e, "v")
            for tt in range(NT):
                p = tt % 2
                wge(e, sPE, b_PE + pe_mm[tt])
                if tt >= 1:
                    wge(e, sAC, b_AC + tt)
                e.reduce_max(s_cols[:, 8:9], pgp[p][:, 0:4], axis=AX.X,
                             negate=True).then_inc(sDV.h, 1)
                wge(e, sAC, b_AC + tt + 1)
                e.reciprocal(s_cols[:, 10:11], s_kols[:, 2:3]).then_inc(sDV.h, 1)
                wge(e, sDV, b_DV + dv_negmax[tt] + 1)
                e.tensor_scalar(out=s_g4[:], in0=s_eg[:],
                                scalar1=s_cols[:, 10:11], scalar2=2.0,
                                op0=OP.mult, op1=OP.mult).then_inc(sDV.h, 1)
                wge(e, sDV, b_DV + dv_negmax[tt] + 2)
                for ee in range(4):
                    ins = e.tensor_scalar(out=s_pg[p][:, ee * 8:(ee + 1) * 8],
                                          in0=pgp[p][:, 4 + ee * 8: 4 + (ee + 1) * 8],
                                          scalar1=s_g4[:, ee:ee + 1], scalar2=None,
                                          op0=OP.mult)
                ins.then_inc(sDV.h, 1)
                if tt >= 1:
                    wge(e, sPE, b_PE + pe_tr[tt - 1])
                    e.tensor_copy(s_pgT[0:32, (tt - 1) * 128: tt * 128],
                                  ppgt[(tt - 1) % 2][0:32, 0:128]).then_inc(sDV.h, 1)
            wge(e, sPE, b_PE + pe_tr[NT - 1])
            e.tensor_copy(s_pgT[0:32, (NT - 1) * 128: NT * 128],
                          ppgt[(NT - 1) % 2][0:32, 0:128]).then_inc(sDV.h, 1)

        @blk.scalar
        def _(e):
            esync(e, "a")
            for tt in range(NT):
                wge(e, sDV, b_DV + dv_negmax[tt])
                e.activation(s_eg[:], pgp[tt % 2][:, 0:4], AF.Exp,
                             bias=s_cols[:, 8:9],
                             accum_out=s_kols[:, 2:3])
                e.copy(s_kols[:, 1:2], s_eps[:, 0:1]).then_inc(sAC.h, 1)
      dW.v += 16 * 2
      sPE.v += n_pe_pro
      sAC.v += NT
      sDV.v += n_dv_pro

    NOC = 16
    if phases >= 3:
      with ExitStack() as pctx, nc.Block() as blk:
        psb = lambda n, s, d, **kw: pctx.enter_context(nc.sbuf_tensor(n, s, d, **kw))
        pps = lambda n, s, d: pctx.enter_context(nc.psum_tensor(n, s, d))
        wbuf2 = psb("wbuf2", [128, 16384], BF16)
        s_bmt = psb("s_bmt", [32, DM], BF16)
        s_bb = psb("s_bb", [128, DM], F32)
        s_f1 = [psb(f"s_f1{i}", [128, 256], F32) for i in range(2)]
        s_gt = [psb(f"s_gt{i}", [128, 256], BF16) for i in range(2)]
        pg = [pps(f"pg{i}", [128, 512], F32) for i in range(2)]
        b_dW, b_dS, b_dL = dW.v, dS.v, dL.v
        b_PE, b_DV, b_AC = sPE.v, sDV.v, sAC.v
        b_dWp = [dWp[0].v, dWp[1].v]
        b_dSp = [dSp[0].v, dSp[1].v]

        @blk.gpsimd
        def _(e):
            esync(e, "g")
            e.dma_start(out=s_bmt[:], in_=din["bmt"][:, :]).then_inc(dW.h, 16)
            e.wait_ge(dW.h, b_dW + 16)
            for oc in range(NOC):
                p = oc % 2
                if oc >= 2:
                    wge(e, sPE, b_PE + NT * (oc - 1))
                src = din["wt"][:, oc * 256:(oc + 1) * 256].rearrange(
                    "(c p) o -> p c o", p=128)
                dst = wbuf2[:, p * 8192:(p + 1) * 8192].rearrange(
                    "p (c o) -> p c o", o=256)
                e.dma_start(out=dst, in_=src).then_inc(dWp[p].h, 16)

        @blk.tensor
        def _(e):
            esync(e, "t")
            wge(e, dW, b_dW + 16)
            for oc in range(NOC):
                p = oc % 2
                wge(e, dWp[p], b_dWp[p] + 16 * (oc // 2 + 1))
                for tt in range(NT):
                    i = oc * NT + tt
                    pp = i % 2
                    if i >= 2:
                        wge(e, sDV, b_DV + (i - 2) + 1)
                    for ic in range(NIC):
                        e.matmul(pg[pp][:, 0:256],
                                 xn_stat(ic, tt),
                                 wbuf2[:, p * 8192 + ic * 256: p * 8192 + (ic + 1) * 256],
                                 start=(ic == 0), stop=False)
                    e.matmul(pg[pp][:, 0:256],
                             s_pgT[0:32, tt * 128:(tt + 1) * 128],
                             s_bmt[0:32, oc * 256:(oc + 1) * 256],
                             start=False, stop=True).then_inc(sPE.h, 1)

        @blk.scalar
        def _(e):
            esync(e, "a")
            e.dma_start(out=s_bb[:], in_=din["bb"][:, :]).then_inc(dL.h, 16)
            for oc in range(NOC):
                for tt in range(NT):
                    i = oc * NT + tt
                    pp = i % 2
                    wge(e, sDV, b_DV + i + 1)
                    if i >= 2:
                        wge(e, dSp[pp], b_dSp[pp] + 16 * ((i - 2) // 2 + 1))
                    e.activation(s_gt[pp][:], s_f1[pp][:], AF.Silu).then_inc(sAC.h, 1)
                    wge(e, sAC, b_AC + i + 1)
                    e.dma_start(out=gsc[tt * 128:(tt + 1) * 128,
                                        oc * 256:(oc + 1) * 256],
                                in_=s_gt[pp][:]).then_inc(dSp[pp].h, 16)

        @blk.vector
        def _(e):
            esync(e, "v")
            wge(e, dL, b_dL + 16)
            for oc in range(NOC):
                for tt in range(NT):
                    i = oc * NT + tt
                    pp = i % 2
                    wge(e, sPE, b_PE + i + 1)
                    if i >= 2:
                        wge(e, sAC, b_AC + (i - 2) + 1)
                    e.scalar_tensor_tensor(
                        out=s_f1[pp][:], in0=pg[pp][:, 0:256], scalar=1.0,
                        in1=s_bb[:, oc * 256:(oc + 1) * 256],
                        op0=OP.mult, op1=OP.add).then_inc(sDV.h, 1)
      dW.v += 16
      dWp[0].v += 16 * (NOC // 2)
      dWp[1].v += 16 * (NOC // 2)
      dL.v += 16
      dSp[0].v += 16 * (NOC * NT // 2)
      dSp[1].v += 16 * (NOC * NT // 2)
      sPE.v += NOC * NT
      sAC.v += NOC * NT
      sDV.v += NOC * NT

    def proj_phase(pctx, blk, heads, wsrc, dest_kind):
        psb = lambda n, s, d, **kw: pctx.enter_context(nc.sbuf_tensor(n, s, d, **kw))
        pps = lambda n, s, d: pctx.enter_context(nc.psum_tensor(n, s, d))
        wb = psb("wb_" + dest_kind, [128, 8192], BF16)
        psQ = pps("psQ_" + dest_kind, [128, 2048], F32)
        pbc = pps("pbc_" + dest_kind, [128, 2048], F32)
        norm = dest_kind in ("q", "k")
        if norm:
            s_sq = psb("sq_" + dest_kind, [128, 2048], BF16)
            s_w = psb("sw_" + dest_kind, [128, 2048], F32)
            s_rr = psb("srr_" + dest_kind, [1, 2048], F32)
        s_st = [psb(f"st_{dest_kind}{i}", [128, 2048], BF16) for i in range(2)]
        ndst = qhat if dest_kind == "q" else kscr
        wcol = s_wqe if dest_kind == "q" else s_knw
        b_dW, b_dS = dW.v, dS.v
        b_PE, b_DV, b_AC = sPE.v, sDV.v, sAC.v
        b_dWp = [dWp[0].v, dWp[1].v]
        b_dSp = [dSp[0].v, dSp[1].v]

        if norm:
            pe_mm, pe_msq, pe_bc = [0] * heads, [0] * heads, [0] * heads
            c = 0
            for h in range(heads + 1):
                if h < heads:
                    c += 1; pe_mm[h] = c
                if h >= 1:
                    c += 1; pe_msq[h - 1] = c
                    c += 1; pe_bc[h - 1] = c
            n_pe = c
            ac_sq, ac_ln, ac_rs = [0] * heads, [0] * heads, [0] * heads
            c = 0
            for h in range(heads + 1):
                if h < heads:
                    c += 1; ac_sq[h] = c
                if h >= 1:
                    c += 1; ac_ln[h - 1] = c
                    c += 1; ac_rs[h - 1] = c
            n_ac = c
            dv_w, dv_stt = [0] * heads, [0] * heads
            c = 0
            for h in range(heads + 1):
                if h >= 1:
                    c += 1; dv_stt[h - 1] = c
                if h < heads:
                    c += 1; dv_w[h] = c
            n_dv = c
        else:
            pe_mm = [h + 1 for h in range(heads)]
            n_pe = heads
            dv_cp = [h + 1 for h in range(heads)]
            n_dv = heads
            n_ac = 0

        @blk.gpsimd
        def _(e):
            esync(e, "g")
            for h in range(heads):
                p = h % 2
                for half in range(2):
                    ch = 2 * h + half
                    if h >= 1:
                        wge(e, sPE, b_PE + pe_mm[h - 1])
                    src = wsrc[half * 2048:(half + 1) * 2048,
                               h * 128:(h + 1) * 128].rearrange(
                        "(c p) o -> p c o", p=128)
                    dst = wb[:, p * 4096 + half * 2048:
                             p * 4096 + (half + 1) * 2048].rearrange(
                        "p (c o) -> p c o", o=128)
                    e.dma_start(out=dst, in_=src).then_inc(dWp[ch % 2].h, 16)

        @blk.tensor
        def _(e):
            esync(e, "t")
            for h in range(heads + 1):
                if h < heads:
                    p = h % 2
                    if h >= 1:
                        if norm:
                            wge(e, sDV, b_DV + dv_w[h - 1])
                            wge(e, sAC, b_AC + ac_sq[h - 1])
                        else:
                            wge(e, sDV, b_DV + dv_cp[h - 1])
                    for half in range(2):
                        ch = 2 * h + half
                        wge(e, dWp[ch % 2], b_dWp[ch % 2] + 16 * (ch // 2 + 1))
                        for tc in range(4):
                            for ici in range(16):
                                ic = half * 16 + ici
                                ins = e.matmul(
                                    psQ[:, tc * 512:(tc + 1) * 512],
                                    wb[:, p * 4096 + half * 2048 + ici * 128:
                                       p * 4096 + half * 2048 + (ici + 1) * 128],
                                    xn_mov(ic, tc),
                                    start=(ic == 0), stop=(ic == NIC - 1))
                    ins.then_inc(sPE.h, 1)
                if norm and h >= 1:
                    hh = h - 1
                    wge(e, sAC, b_AC + ac_sq[hh])
                    if hh >= 1:
                        wge(e, sDV, b_DV + dv_stt[hh - 1])
                    for tc in range(4):
                        ins = e.matmul(pbc[0:1, tc * 512:(tc + 1) * 512],
                                       s_ones128[:],
                                       s_sq[:, tc * 512:(tc + 1) * 512],
                                       start=True, stop=True)
                    ins.then_inc(sPE.h, 1)
                    wge(e, sAC, b_AC + ac_rs[hh])
                    for tc in range(4):
                        ins = e.matmul(pbc[:, tc * 512:(tc + 1) * 512], s_ones1f[:],
                                       s_rr[0:1, tc * 512:(tc + 1) * 512],
                                       start=True, stop=True)
                    ins.then_inc(sPE.h, 1)

        if norm:
            @blk.scalar
            def _(e):
                esync(e, "a")
                for h in range(heads + 2):
                    if h < heads:
                        wge(e, sPE, b_PE + pe_mm[h])
                        if h >= 1:
                            wge(e, sPE, b_PE + pe_msq[h - 1])
                        e.activation(s_sq[:], psQ[:],
                                     AF.Square).then_inc(sAC.h, 1)
                    if 1 <= h <= heads:
                        hh = h - 1
                        wge(e, sPE, b_PE + pe_msq[hh])
                        e.activation(pbc[32:33, 0:2048], pbc[0:1, 0:2048], AF.Ln,
                                     scale=1.0 / 128.0, bias=s_eps[0:1, 1:2]).then_inc(sAC.h, 1)
                        if hh >= 1:
                            wge(e, sPE, b_PE + pe_bc[hh - 1])
                        wge(e, sAC, b_AC + ac_ln[hh])
                        e.activation(s_rr[:], pbc[32:33, 0:2048], AF.Exp,
                                     scale=-0.5).then_inc(sAC.h, 1)
                    if h >= 2:
                        hs = h - 2
                        wge(e, sDV, b_DV + dv_stt[hs])
                        e.dma_start(out=ndst[hs * 128:(hs + 1) * 128, :],
                                    in_=s_st[hs % 2][:]).then_inc(dSp[hs % 2].h, 16)
        else:
            @blk.scalar
            def _(e):
                esync(e, "a")
                for h in range(heads):
                    wge(e, sDV, b_DV + dv_cp[h])
                    e.dma_start(out=vsc[h, :, :],
                                in_=s_st[h % 2][:]).then_inc(dSp[h % 2].h, 16)

        @blk.vector
        def _(e):
            esync(e, "v")
            for h in range(heads + 1):
                if norm and h >= 1:
                    hh = h - 1
                    wge(e, sPE, b_PE + pe_bc[hh])
                    if hh >= 2:
                        wge(e, dSp[hh % 2], b_dSp[hh % 2] + 16 * ((hh - 2) // 2 + 1))
                    wge(e, sDV, b_DV + dv_w[hh])
                    dst = s_st[hh % 2][:]
                    e.scalar_tensor_tensor(out=dst, in0=pbc[:], scalar=1.0,
                                           in1=s_w[:], op0=OP.mult,
                                           op1=OP.mult).then_inc(sDV.h, 1)
                if h < heads and norm:
                    wge(e, sPE, b_PE + pe_mm[h])
                    if h >= 1:
                        wge(e, sDV, b_DV + dv_stt[h - 1])
                    e.tensor_scalar(out=s_w[:], in0=psQ[:], scalar1=wcol[:, 0:1],
                                    scalar2=None, op0=OP.mult).then_inc(sDV.h, 1)
                if h < heads and not norm:
                    wge(e, sPE, b_PE + pe_mm[h])
                    if h >= 2:
                        wge(e, dSp[h % 2], b_dSp[h % 2] + 16 * ((h - 2) // 2 + 1))
                    e.tensor_copy(s_st[h % 2][:], psQ[:]).then_inc(sDV.h, 1)

        return dict(dWp0=16 * heads, dWp1=16 * heads,
                    dSp0=16 * ((heads + 1) // 2), dSp1=16 * (heads // 2),
                    sPE=n_pe, sDV=n_dv, sAC=n_ac)

    def apply_inc(inc):
        m = {"dWp0": dWp[0], "dWp1": dWp[1], "dSp0": dSp[0], "dSp1": dSp[1],
             "sPE": sPE, "sDV": sDV, "sAC": sAC}
        for k_, v_ in inc.items():
            m[k_].v += v_

    if phases >= 4:
        with ExitStack() as pctx, nc.Block() as blk:
            apply_inc(proj_phase(pctx, blk, H, din["wqt"], "q"))
    if phases >= 5:
        with ExitStack() as pctx, nc.Block() as blk:
            apply_inc(proj_phase(pctx, blk, HKV, din["wkt"], "k"))
    if phases >= 6:
        with ExitStack() as pctx, nc.Block() as blk:
            apply_inc(proj_phase(pctx, blk, HKV, din["wvt"], "v"))

    if tap_out:
        with nc.Block() as blk:
            b_dS = dS.v
            ntap = len(tap_out)

            @blk.scalar
            def _(e):
                esync(e, "a")
                e.wait_ge(dX.h, dX.v)
                n = 0
                if "xnt" in tap_out:
                    n += 1
                    e.dma_start(out=tap_out["xnt"][:, :], in_=xnt[:]).then_inc(dS.h, 16)
                if "dbg" in tap_out:
                    n += 2
                    e.dma_start(out=tap_out["dbg"][:, :], in_=s_cols[:]).then_inc(dS.h, 16)
                    e.dma_start(out=tap_out["dbg2"][:, :],
                                in_=s_eps[:]).then_inc(dS.h, 16)
                e.wait_ge(dS.h, b_dS + 16 * n)
        dS.v += 16 * ntap

    xnt_ctx.close()

    NQ = H * NT
    if phases >= 7:
      with ExitStack() as pctx, nc.Block() as blk:
        psb = lambda n, s, d, **kw: pctx.enter_context(nc.sbuf_tensor(n, s, d, **kw))
        pps = lambda n, s, d: pctx.enter_context(nc.psum_tensor(n, s, d))
        vv = psb("vv", [128, HKV * SEQ], BF16)
        kt = psb("kt", [128, HKV * SEQ], BF16)
        s_alw = psb("s_alw", [4, H * 128], BF16)
        s_alk = psb("s_alk", [4, SEQ], BF16)
        s_nu = psb("s_nu", [128, 128], BF16)
        qb = [psb(f"qb{i}", [128, SEQ], BF16) for i in range(2)]
        eb = [psb(f"eb{i}", [128, SEQ], BF16) for i in range(4)]
        en = [psb(f"en{i}", [128, SEQ], BF16) for i in range(4)]
        et = [psb(f"et{i}", [128, SEQ], BF16) for i in range(6)]
        ost = [psb(f"ost{i}", [128, SEQ], BF16) for i in range(2)]
        s_np = [psb(f"s_np{i}", [128, 16], F32) for i in range(4)]
        s_al = [psb(f"s_al{i}", [128, 8], F32) for i in range(4)]
        psS = pps("psS", [128, 3072], F32)
        psO = [pps(f"psO{i}", [128, 512], F32) for i in range(2)]
        b_dL, b_dX = dL.v, dX.v
        b_dLp = [dLp[0].v, dLp[1].v]
        b_dSp = [dSp[0].v, dSp[1].v]
        b_S, b_M, b_E, b_N, b_P, b_V = sS5.v, sM5.v, sE5.v, sN5.v, sP5.v, sV5.v
        b_F = sF5.v
        dS_tot = (dS.v, dSp[0].v, dSp[1].v)
        NCONST = 3

        def ncnk(qt):
            kw = (qt + 1) * 128
            return (kw + 511) // 512, qt + 1, kw

        slot_of, sfree_wait = {}, {}
        slot_owner = [None] * 6
        gslot = 0
        for idx in range(NQ):
            for cch in range(ncnk(idx % NT)[0]):
                sl = gslot % 6
                gslot += 1
                own = slot_owner[sl]
                sfree_wait[(idx, cch)] = (own + 1) if own is not None else 0
                slot_of[(idx, cch)] = sl
                slot_owner[sl] = idx

        @blk.scalar
        def _(e):
            esync(e, "a")
            for s, v in zip((dS, dSp[0], dSp[1]), dS_tot):
                if v > 0:
                    e.wait_ge(s.h, v)
            for s, d in [(s_alw, din["alw"]), (s_alk, din["alk"]),
                         (s_nu, din["negu"])]:
                e.dma_start(out=s[:], in_=d[:, :]).then_inc(dL.h, 16)
            for kv in range(HKV):
                e.dma_start(out=kt[:, kv * SEQ:(kv + 1) * SEQ],
                            in_=kscr[kv * 128:(kv + 1) * 128, :]).then_inc(dL.h, 16)
            for h in range(2):
                e.dma_start(out=qb[h][:],
                            in_=qhat[h * 128:(h + 1) * 128, :]).then_inc(dLp[h].h, 16)

            def ostore(e, hs):
                wge(e, sV5, b_V + NT * (hs + 1))
                dsto = othbm[:, :, hs, :].rearrange("q p r -> p q r")
                srco = ost[hs % 2][:].rearrange("p (q r) -> p q r", r=128)
                e.dma_start(out=dsto, in_=srco).then_inc(dSp[hs % 2].h, 16)

            for idx in range(NQ):
                h, qt = idx // NT, idx % NT
                qp = idx % 4
                nc512, nk, kw = ncnk(qt)
                wge(e, sM5, b_M + idx + 1)
                if idx >= 4:
                    wge(e, sN5, b_N + idx - 3)
                for cch in range(nc512):
                    wc = min(512, kw - cch * 512)
                    sl = slot_of[(idx, cch)]
                    e.activation(eb[qp][:, cch * 512: cch * 512 + wc],
                                 psS[:, sl * 512: sl * 512 + wc], AF.Exp,
                                 bias=s_np[qp][:, 4:5],
                                 accum_out=s_al[qp][:, cch:cch + 1])
                e.copy(s_al[qp][:, 4:5],
                       s_eps[:, 0:1]).then_inc(sE5.h, 1)
                if idx >= 17 and (idx - 17) % NT == 0:
                    ostore(e, (idx - 17) // NT)
                if qt == NT - 1 and h + 2 < H:
                    wge(e, sS5, b_S + NT * (h + 1))
                    e.dma_start(out=qb[h % 2][:],
                                in_=qhat[(h + 2) * 128:(h + 3) * 128, :]
                                ).then_inc(dLp[h % 2].h, 16)
            ostore(e, H - 1)

        def _pv(e, I):
            hp, qtp = I // NT, I % NT
            kvp = hp // 4
            nkp = qtp + 1
            wge(e, dX, b_dX + 16 * (HKV + I + 1))
            if I >= 2:
                wge(e, sV5, b_V + I - 1)
            ins = None
            for kc in range(nkp):
                ins = e.matmul(psO[I % 2][:, 0:128],
                               vv[:, kvp * SEQ + kc * 128: kvp * SEQ + (kc + 1) * 128],
                               et[I % 6][:, kc * 128:(kc + 1) * 128],
                               start=(kc == 0), stop=(kc == nkp - 1))
            ins.then_inc(sP5.h, 1)

        @blk.tensor
        def _(e):
            esync(e, "t")
            wge(e, dL, b_dL + 16 * (NCONST + HKV))
            for idx in range(NQ):
                h, qt = idx // NT, idx % NT
                kv = h // 4
                nc512, nk, kw = ncnk(qt)
                wge(e, dLp[h % 2], b_dLp[h % 2] + 16 * (h // 2 + 1))
                cl = nc512 - 1
                off = qt * 128 - cl * 512
                for cch in range(nc512):
                    wc = min(512, kw - cch * 512)
                    sl = slot_of[(idx, cch)]
                    diag = (cch == cl)
                    wge(e, sE5, b_E + sfree_wait[(idx, cch)])
                    e.matmul(psS[:, sl * 512: sl * 512 + wc],
                             qb[h % 2][:, qt * 128:(qt + 1) * 128],
                             kt[:, kv * SEQ + cch * 512:
                                kv * SEQ + cch * 512 + wc],
                             start=True, stop=False)
                    ins = e.matmul(psS[:, sl * 512: sl * 512 + wc],
                                   s_alw[0:4, h * 128:(h + 1) * 128],
                                   s_alk[0:4, cch * 512: cch * 512 + wc],
                                   start=False, stop=not diag)
                    if diag:
                        ins = e.matmul(psS[:, sl * 512 + off: sl * 512 + off + 128],
                                       s_ident[:], s_nu[:],
                                       start=False, stop=True)
                        ins.then_inc(sS5.h, 1)
                if idx >= 4:
                    _pv(e, idx - 4)
            for I in range(NQ - 4, NQ):
                _pv(e, I)

        def _evac(e, I):
            hp, qtp = I // NT, I % NT
            wge(e, sP5, b_P + I + 1)
            if hp >= 2 and qtp == 0:
                wge(e, dSp[hp % 2], b_dSp[hp % 2] + 16 * ((hp - 2) // 2 + 1))
            e.tensor_copy(ost[hp % 2][:, qtp * 128:(qtp + 1) * 128],
                          psO[I % 2][:, 0:128]).then_inc(sV5.h, 1)

        @blk.vector
        def _(e):
            esync(e, "v")
            for idx in range(NQ + 4):
                qp = idx % 4
                qp2 = (idx - 2) % 4
                if 2 <= idx < NQ + 2:
                    nc2 = ncnk((idx - 2) % NT)[0]
                    wge(e, sE5, b_E + idx - 1)
                    e.reduce_sum(s_np[qp2][:, 12:13], s_al[qp2][:, 0:nc2],
                                 axis=AX.X)
                if idx < NQ:
                    nc512, nk, kw = ncnk(idx % NT)
                    wge(e, sS5, b_S + idx + 1)
                    for cch in range(nc512):
                        wc = min(512, kw - cch * 512)
                        sl = slot_of[(idx, cch)]
                        ins = e.reduce_max(s_np[qp][:, cch:cch + 1],
                                           psS[:, sl * 512: sl * 512 + wc],
                                           axis=AX.X, negate=True)
                    ins.then_inc(sF5.h, 1)
                else:
                    e.tensor_copy(s_np[0][:, 15:16],
                                  s_eps[:, 0:1]).then_inc(sF5.h, 1)
                wge(e, sF5, b_F + idx + 1)
                if 2 <= idx < NQ + 2:
                    e.reciprocal(s_np[qp2][:, 13:14], s_np[qp2][:, 12:13])
                if idx < NQ:
                    nc512 = ncnk(idx % NT)[0]
                    e.tensor_reduce(s_np[qp][:, 4:5], s_np[qp][:, 0:nc512],
                                    axis=AX.X, op=OP.min).then_inc(sM5.h, 1)
                else:
                    e.tensor_copy(s_np[0][:, 14:15],
                                  s_eps[:, 0:1]).then_inc(sM5.h, 1)
                wge(e, sM5, b_M + idx + 1)
                if 2 <= idx < NQ + 2:
                    kw2 = ncnk((idx - 2) % NT)[2]
                    if idx >= 6:
                        wge(e, dX, b_dX + 16 * (HKV + idx - 5))
                    e.tensor_scalar(out=en[qp2][:, 0:kw2], in0=eb[qp2][:, 0:kw2],
                                    scalar1=s_np[qp2][:, 13:14], scalar2=None,
                                    op0=OP.mult).then_inc(sN5.h, 1)
                if idx >= 4:
                    _evac(e, idx - 4)

        @blk.sync
        def _(e):
            esync(e, "g")
            for s, v in zip((dS, dSp[0], dSp[1]), dS_tot):
                if v > 0:
                    e.wait_ge(s.h, v)
            for dc in range(HKV):
                dstv = vv[:, dc * SEQ:(dc + 1) * SEQ].rearrange(
                    "p (c r) -> p c r", r=128)
                e.dma_start_transpose(dstv, vsc[dc, :, :]).then_inc(dX.h, 16)
                wge(e, dX, b_dX + 16 * (dc + 1))
            for idx in range(NQ):
                qp = idx % 4
                kw = ncnk(idx % NT)[2]
                wge(e, sN5, b_N + idx + 1)
                if idx >= 6:
                    wge(e, sP5, b_P + idx - 5)
                dstt = et[idx % 6][:, 0:kw].rearrange("p (c r) -> p c r", r=128)
                e.dma_start_transpose(dstt, en[qp][:, 0:kw]).then_inc(dX.h, 16)
                wge(e, dX, b_dX + 16 * (HKV + idx + 1))
      dL.v += 16 * (3 + HKV)
      dLp[0].v += 16 * (H // 2)
      dLp[1].v += 16 * (H // 2)
      dSp[0].v += 16 * (H // 2)
      dSp[1].v += 16 * (H // 2)
      sF5.v += NQ + 4
      dX.v += 16 * (HKV + NQ)
      sS5.v += NQ
      sM5.v += NQ + 4
      sE5.v += NQ
      sN5.v += NQ
      sP5.v += NQ
      sV5.v += NQ

    NJC = 4
    if phases >= 8:
      with ExitStack() as pctx, nc.Block() as blk:
        psb = lambda n, s, d, **kw: pctx.enter_context(nc.sbuf_tensor(n, s, d, **kw))
        pps = lambda n, s, d: pctx.enter_context(nc.psum_tensor(n, s, d))
        wo6 = [psb(f"wo6_{i}", [128, 32768], BF16) for i in range(2)]
        s_ot = [psb(f"s_ot{i}", [128, H * 128], BF16) for i in range(2)]
        s_g6 = [psb(f"s_g6{i}", [128, 1024], BF16) for i in range(2)]
        s_x6 = [psb(f"s_x6{i}", [128, 1024], F32) for i in range(2)]
        s_t6 = [psb(f"s_t6{i}", [128, 512], F32) for i in range(2)]
        s_o6 = [psb(f"s_o6{i}", [128, 1024], F32) for i in range(2)]
        py = [pps(f"py{i}", [128, 512], F32) for i in range(2)]
        b_dW, b_dL, b_dS = dW.v, dL.v, dS.v
        b_PE, b_DV = sPE.v, sDV.v
        b_dWp = [dWp[0].v, dWp[1].v]
        b_dLp = [dLp[0].v, dLp[1].v]
        b_dSp = [dSp[0].v, dSp[1].v]
        dS_prev6 = (dS.v, dSp[0].v, dSp[1].v)

        @blk.gpsimd
        def _(e):
            esync(e, "g")
            for jc in range(NJC):
                p = jc % 2
                if jc >= 2:
                    wge(e, sPE, b_PE + 2 * NT * (jc - 1))
                for cc in range(4):
                    src = din["wot"][cc * 1024:(cc + 1) * 1024,
                                     jc * 1024:(jc + 1) * 1024].rearrange(
                        "(c p) o -> p c o", p=128)
                    dst = wo6[p][:, cc * 8192:(cc + 1) * 8192].rearrange(
                        "p (c o) -> p c o", o=1024)
                    e.dma_start(out=dst, in_=src).then_inc(dWp[jc % 2].h, 16)

        @blk.sync
        def _(e):
            esync(e, "g")
            wge(e, dS, dS_prev6[0])
            wge(e, dSp[0], dS_prev6[1])
            wge(e, dSp[1], dS_prev6[2])
            for jc in range(NJC):
                for tt in range(NT):
                    i = jc * NT + tt
                    p = i % 2
                    if i >= 2:
                        wge(e, sPE, b_PE + 2 * (i - 2) + 2)
                    e.dma_start(out=s_ot[p][:],
                                in_=othbm[tt, :, :, :].rearrange(
                                    "p h r -> p (h r)")).then_inc(dLp[p].h, 16)

        @blk.scalar
        def _(e):
            esync(e, "a")
            wge(e, dS, dS_prev6[0])
            wge(e, dSp[0], dS_prev6[1])
            wge(e, dSp[1], dS_prev6[2])
            for jc in range(NJC):
                for tt in range(NT):
                    i = jc * NT + tt
                    p = i % 2
                    if i >= 2:
                        wge(e, sDV, b_DV + 4 * (i - 2) + 4)
                    e.dma_start(out=s_g6[p][:],
                                in_=gsc[tt * 128:(tt + 1) * 128,
                                        jc * 1024:(jc + 1) * 1024]).then_inc(dLp[p].h, 16)
                    e.dma_start(out=s_x6[p][:],
                                in_=din["x"][tt * 128:(tt + 1) * 128,
                                             jc * 1024:(jc + 1) * 1024]).then_inc(dLp[p].h, 16)
                    wge(e, sDV, b_DV + 4 * i + 4)
                    e.dma_start(out=out[tt * 128:(tt + 1) * 128,
                                        jc * 1024:(jc + 1) * 1024],
                                in_=s_o6[p][:]).then_inc(dSp[p].h, 16)
            e.wait_ge(dSp[0].h, b_dSp[0] + 16 * (NJC * NT // 2))
            e.wait_ge(dSp[1].h, b_dSp[1] + 16 * (NJC * NT // 2))

        @blk.tensor
        def _(e):
            esync(e, "t")
            for jc in range(NJC):
                p = jc % 2
                wge(e, dWp[jc % 2], b_dWp[jc % 2] + 64 * (jc // 2 + 1))
                for tt in range(NT):
                    i = jc * NT + tt
                    pp = i % 2
                    wge(e, dLp[pp], b_dLp[pp] + 48 * (i // 2 + 1))
                    for j2 in range(2):
                        if i >= 1:
                            wge(e, sDV, b_DV + 4 * (i - 1) + 2 * j2 + 2)
                        for hc in range(H):
                            ins = e.matmul(py[j2][:, 0:512],
                                           s_ot[pp][:, hc * 128:(hc + 1) * 128],
                                           wo6[p][:, hc * 1024 + j2 * 512:
                                                  hc * 1024 + (j2 + 1) * 512],
                                           start=(hc == 0), stop=(hc == H - 1))
                        ins.then_inc(sPE.h, 1)

        @blk.vector
        def _(e):
            esync(e, "v")
            for jc in range(NJC):
                for tt in range(NT):
                    i = jc * NT + tt
                    pp = i % 2
                    wge(e, dLp[pp], b_dLp[pp] + 48 * (i // 2 + 1))
                    if i >= 2:
                        wge(e, dSp[pp], b_dSp[pp] + 16 * (i // 2))
                    for j2 in range(2):
                        wge(e, sPE, b_PE + 2 * i + j2 + 1)
                        e.scalar_tensor_tensor(
                            out=s_t6[j2][:], in0=py[j2][:, 0:512], scalar=1.0,
                            in1=s_g6[pp][:, j2 * 512:(j2 + 1) * 512],
                            op0=OP.mult, op1=OP.mult).then_inc(sDV.h, 1)
                    wge(e, sDV, b_DV + 4 * i + 2)
                    for j2 in range(2):
                        e.tensor_tensor(
                            out=s_o6[pp][:, j2 * 512:(j2 + 1) * 512],
                            in0=s_t6[j2][:],
                            in1=s_x6[pp][:, j2 * 512:(j2 + 1) * 512],
                            op=OP.add).then_inc(sDV.h, 1)
      dWp[0].v += 64 * (NJC // 2)
      dWp[1].v += 64 * (NJC // 2)
      dLp[0].v += 48 * (NJC * NT // 2)
      dLp[1].v += 48 * (NJC * NT // 2)
      dSp[0].v += 16 * (NJC * NT // 2)
      dSp[1].v += 16 * (NJC * NT // 2)
      sPE.v += 2 * NJC * NT
      sDV.v += 4 * NJC * NT

    with nc.Block() as blk:
        @blk.scalar
        def _(e):
            for s in (dL, dLp[0], dLp[1], dS, dSp[0], dSp[1], dW,
                      dWp[0], dWp[1], dX):
                if s.v > 0:
                    e.wait_ge(s.h, s.v)

    g.close()
    sctx.close()
    return nc, din, out, tap_out


def _host_layout(inputs):
    x = np.ascontiguousarray(np.asarray(inputs["L0_attn_x"], dtype=np.float32))
    wq = np.asarray(inputs["L0_attn_Wq"], np.float32)
    wk = np.asarray(inputs["L0_attn_Wk"], np.float32)
    wv = np.asarray(inputs["L0_attn_Wv"], np.float32)
    wo = np.asarray(inputs["L0_attn_Wo"], np.float32)
    w = np.asarray(inputs["L0_ffn_W"], np.float32)
    b = np.asarray(inputs["L0_ffn_b"], np.float32)
    A = np.asarray(inputs["L0_ffn_A"], np.float32)
    Bm = np.asarray(inputs["L0_ffn_Bm"], np.float32)
    Wr = np.asarray(inputs["L0_ffn_Wr"], np.float32)
    qnw = np.asarray(inputs["L0_attn_q_norm_w"], np.float32).reshape(128, 1)
    knw = np.asarray(inputs["L0_attn_k_norm_w"], np.float32).reshape(128, 1)

    ident, dmask, alw, alk, negu = host_constants()
    com = {
        "wqt": np.ascontiguousarray(wq.T),
        "wkt": np.ascontiguousarray(wk.T),
        "wvt": np.ascontiguousarray(wv.T),
        "wot": np.ascontiguousarray(wo.T),
        "wt": np.ascontiguousarray(w.T),
        "wrtb": np.ascontiguousarray(
            Wr.T.reshape(32, 128, 4).transpose(1, 0, 2).reshape(128, 128)),
        "atb": np.ascontiguousarray(
            A.reshape(32, DM).T.reshape(32, 128, 32).transpose(1, 0, 2)
            .reshape(128, 1024)),
        "bmt": np.ascontiguousarray(Bm.transpose(0, 2, 1).reshape(32, DM)),
        "bb": np.ascontiguousarray(np.broadcast_to(b, (128, DM))),
        "qnw": np.ascontiguousarray(qnw),
        "knw": np.ascontiguousarray(knw),
        "ident": ident, "dmask": dmask, "alw": alw, "alk": alk, "negu": negu,
    }
    return x, com


def kernel(**inputs):
    x, com = _host_layout(inputs)
    nc, din, out_ap, taps = build()
    in_maps = [dict(com, x=np.ascontiguousarray(x[c])) for c in range(8)]
    res = run_bass_kernel_spmd(nc, in_maps, core_ids=list(range(8)))
    return np.stack([np.asarray(r["out"]) for r in res.results], axis=0)


# version 04
# speedup vs baseline: 1.3343x; 1.0127x over previous
import math
from contextlib import ExitStack

import numpy as np
import ml_dtypes

import concourse.bass as bass
import concourse.mybir as mybir
from concourse.alu_op_type import AluOpType as OP
from concourse.bass_utils import run_bass_kernel_spmd

F32 = mybir.dt.float32
BF16 = mybir.dt.bfloat16
AF = mybir.ActivationFunctionType
AX = mybir.AxisListType

SEQ = 2048
DM = 4096
H = 32
HKV = 8
NT = 16
NIC = 32
NEG = -1.0e30
LN_EPS = 1.0e-5
RMS_EPS = 1.0e-6
INV_SQRT_HD = 1.0 / math.sqrt(128.0)


class Sem:
    def __init__(self, ctx, nc, name):
        self.h = ctx.enter_context(nc.semaphore(name))
        self.v = 0


def _alibi_slopes(n_heads):
    start = 2 ** (-(2 ** -(math.log2(n_heads) - 3)))
    return np.array([start * start ** i for i in range(n_heads)], dtype=np.float32)


def host_constants():
    bf = ml_dtypes.bfloat16
    ident = np.eye(128, dtype=np.float32).astype(bf)
    p = np.arange(128)
    dmask = np.where(p[None, :] <= p[:, None], 0.0, NEG).astype(np.float32)
    slopes = _alibi_slopes(H)
    alw = np.zeros((128, H * 128), dtype=np.float32)
    for h in range(H):
        s_hi = np.float32(bf(slopes[h]))
        s_lo = np.float32(bf(np.float32(slopes[h]) - s_hi))
        alw[0, h * 128:(h + 1) * 128] = 256.0 * s_hi
        alw[1, h * 128:(h + 1) * 128] = s_hi
        alw[2, h * 128:(h + 1) * 128] = 256.0 * s_lo
        alw[3, h * 128:(h + 1) * 128] = s_lo
    alw = alw.astype(bf)
    k = np.arange(SEQ)
    alk = np.zeros((128, SEQ), dtype=np.float32)
    alk[0] = (k // 256).astype(np.float32)
    alk[1] = (k % 256).astype(np.float32)
    alk[2] = alk[0]
    alk[3] = alk[1]
    alk = alk.astype(bf)
    negu = np.where(p[None, :] > p[:, None], NEG, 0.0).astype(np.float32).astype(bf)
    return ident, dmask, alw, alk, negu


def build(taps=(), phases=8):
    nc = bass.Bass("TRN2", target_bir_lowering=False)

    din = {}
    for name, shape, dt in [
        ("x", [SEQ, DM], F32),
        ("wqt", [DM, DM], F32),
        ("wkt", [DM, HKV * 128], F32),
        ("wvt", [DM, HKV * 128], F32),
        ("wot", [DM, DM], F32),
        ("wt", [DM, DM], F32),
        ("wrtb", [128, 128], F32),
        ("atb", [128, 1024], F32),
        ("bmt", [32, DM], F32),
        ("bb", [128, DM], F32),
        ("qnw", [128, 1], F32),
        ("knw", [128, 1], F32),
        ("ident", [128, 128], BF16),
        ("dmask", [128, 128], F32),
        ("alw", [128, H * 128], BF16),
        ("alk", [128, SEQ], BF16),
        ("negu", [128, 128], BF16),
    ]:
        din[name] = nc.dram_tensor(name, shape, dt, kind="ExternalInput").ap()

    def scratch(name, shape):
        kind = "ExternalOutput" if name in taps else "Internal"
        return nc.dram_tensor(name, shape, BF16, kind=kind).ap()
    qhat = scratch("qhat", [H * 128, SEQ])
    kscr = scratch("kscr", [HKV * 128, SEQ])
    othbm = scratch("ot", [NT, 128, H, 128])
    gsc = scratch("gsc", [SEQ, DM])
    vsc = scratch("vsc", [HKV, 128, SEQ])
    out = nc.dram_tensor("out", [SEQ, DM], F32, kind="ExternalOutput").ap()

    tap_out = {}
    for t in taps:
        if t in ("qhat", "kscr", "ot", "gsc", "vsc"):
            continue
        if t == "dbg":
            tap_out[t] = nc.dram_tensor("tap_dbg", [128, 16], F32,
                                        kind="ExternalOutput").ap()
            tap_out["dbg2"] = nc.dram_tensor("tap_dbg2", [128, 2], F32,
                                             kind="ExternalOutput").ap()
            continue
        shp = {"xnt": [128, NIC * SEQ]}[t]
        tap_out[t] = nc.dram_tensor("tap_" + t, shp, BF16, kind="ExternalOutput").ap()

    g = ExitStack()
    sb = lambda n, s, d, **kw: g.enter_context(nc.sbuf_tensor(n, s, d, **kw))
    s_ident = sb("s_ident", [128, 128], BF16)
    s_qnw = sb("s_qnw", [128, 1], F32)
    s_knw = sb("s_knw", [128, 1], F32)
    s_wqe = sb("s_wqe", [128, 1], F32)
    s_ones128 = sb("s_ones128", [128, 1], BF16)
    s_ones1f = sb("s_ones1f", [1, 128], F32)
    s_cols = sb("s_cols", [128, 16], F32)
    s_eps = sb("s_eps", [128, 2], F32)
    s_kols = sb("s_kols", [128, 4], F32)
    s_pgT = sb("s_pgT", [32, SEQ], BF16)

    sctx = ExitStack()
    dL = Sem(sctx, nc, "dL")
    dLp = [Sem(sctx, nc, f"dL{i}") for i in range(2)]
    dS = Sem(sctx, nc, "dS")
    dSp = [Sem(sctx, nc, f"dS{i}") for i in range(2)]
    dW = Sem(sctx, nc, "dW")
    dWp = [Sem(sctx, nc, f"dW{i}") for i in range(2)]
    dX = Sem(sctx, nc, "dX")
    sPE = Sem(sctx, nc, "sPE")
    sDV = Sem(sctx, nc, "sDV")
    sAC = Sem(sctx, nc, "sAC")
    sS5 = Sem(sctx, nc, "sS5")
    sM5 = Sem(sctx, nc, "sM5")
    sE5 = Sem(sctx, nc, "sE5")
    sN5 = Sem(sctx, nc, "sN5")
    sP5 = Sem(sctx, nc, "sP5")
    sV5 = Sem(sctx, nc, "sV5")
    sF5 = Sem(sctx, nc, "sF5")

    def wge(e, sem, val):
        if val > 0:
            e.wait_ge(sem.h, val)

    def esync(e, kind):
        for s in (sPE, sDV, sAC, sS5, sP5, sV5, sM5, sN5, sE5, sF5,
                  dL, dS, dW, dX, dLp[0], dLp[1], dSp[0], dSp[1],
                  dWp[0], dWp[1]):
            if s.v > 0:
                e.wait_ge(s.h, s.v)

    xnt_ctx = ExitStack()
    xnt = xnt_ctx.enter_context(nc.sbuf_tensor("xnt", [128, NIC * SEQ], BF16))

    def xn_stat(ic, tt):
        return xnt[:, tt * 4096 + ic * 128: tt * 4096 + (ic + 1) * 128]

    def xn_mov(ic, tc):
        v3 = xnt[:, :].rearrange("p (a b) -> p a b", b=4096)
        return v3[:, tc * 4:(tc + 1) * 4, ic * 128:(ic + 1) * 128]

    with nc.Block() as blk:
        @blk.scalar
        def _(e):
            esync(e, "a")
            for s, d in [(s_ident, din["ident"]), (s_qnw, din["qnw"]),
                         (s_knw, din["knw"])]:
                e.dma_start(out=s[:], in_=d[:, :]).then_inc(dL.h, 16)

        @blk.vector
        def _(e):
            esync(e, "v")
            e.wait_ge(dL.h, dL.v + 48)
            e.memset(s_ones128[:], 1.0)
            e.memset(s_ones1f[:], 1.0)
            e.memset(s_eps[:, 0:1], LN_EPS)
            e.memset(s_eps[:, 1:2], RMS_EPS)
            e.tensor_scalar_mul(s_wqe[:], s_qnw[:], INV_SQRT_HD).then_inc(sDV.h, 1)
    dL.v += 48
    sDV.v += 1

    if phases >= 1:
      with ExitStack() as pctx, nc.Block() as blk:
        psb = lambda n, s, d, **kw: pctx.enter_context(nc.sbuf_tensor(n, s, d, **kw))
        xt = [psb(f"xt{i}", [128, DM], F32) for i in range(2)]
        ast = [psb(f"ast{i}", [128, DM], BF16) for i in range(2)]
        b_dX, b_AC, b_DV = dX.v, sAC.v, sDV.v
        b_dLp = [dLp[0].v, dLp[1].v]

        @blk.scalar
        def _(e):
            esync(e, "a")
            for tt in range(NT):
                p = tt % 2
                if tt >= 2:
                    wge(e, sDV, b_DV + 6 * (tt - 2) + 6)
                e.dma_start(out=xt[p][:],
                            in_=din["x"][tt * 128:(tt + 1) * 128, :]) \
                    .then_inc(dLp[p].h, 16)
                wge(e, sDV, b_DV + 6 * tt + 4)
                e.activation(s_kols[:, 0:1], s_cols[:, 4:5], AF.Sqrt,
                             scale=1.0 / DM, bias=s_eps[:, 0:1]).then_inc(sAC.h, 1)

        @blk.vector
        def _(e):
            esync(e, "v")
            for tt in range(NT):
                p = tt % 2
                wge(e, dLp[p], b_dLp[p] + 16 * (tt // 2 + 1))
                e.reduce_sum(s_cols[:, 0:1], xt[p][:], axis=AX.X)
                if tt >= 2:
                    wge(e, dX, b_dX + 16 * (tt - 1))
                e.tensor_tensor(out=ast[p][:], in0=xt[p][:], in1=xt[p][:],
                                op=OP.mult).then_inc(sDV.h, 1)
                wge(e, sDV, b_DV + 6 * tt + 1)
                e.tensor_scalar(out=ast[p][:], in0=ast[p][:], scalar1=1.0,
                                scalar2=0.0, op0=OP.mult, op1=OP.add,
                                accum_out=s_cols[:, 1:2]).then_inc(sDV.h, 1)
                wge(e, sDV, b_DV + 6 * tt + 2)
                e.tensor_scalar(out=s_cols[:, 5:6], in0=s_cols[:, 0:1],
                                scalar1=1.0 / DM, scalar2=s_cols[:, 0:1],
                                op0=OP.mult, op1=OP.mult)
                e.tensor_scalar(out=s_cols[:, 2:3], in0=s_cols[:, 0:1],
                                scalar1=1.0 / DM, scalar2=None,
                                op0=OP.mult).then_inc(sDV.h, 1)
                wge(e, sDV, b_DV + 6 * tt + 3)
                if tt >= 1:
                    wge(e, sAC, b_AC + tt)
                e.tensor_tensor(out=s_cols[:, 4:5], in0=s_cols[:, 1:2],
                                in1=s_cols[:, 5:6],
                                op=OP.subtract).then_inc(sDV.h, 1)
                wge(e, sAC, b_AC + tt + 1)
                e.reciprocal(s_cols[:, 7:8], s_kols[:, 0:1]).then_inc(sDV.h, 1)
                wge(e, sDV, b_DV + 6 * tt + 5)
                e.tensor_scalar(out=ast[p][:], in0=xt[p][:],
                                scalar1=s_cols[:, 2:3], scalar2=s_cols[:, 7:8],
                                op0=OP.subtract, op1=OP.mult).then_inc(sDV.h, 1)

        @blk.sync
        def _(e):
            esync(e, "g")
            for tt in range(NT):
                p = tt % 2
                wge(e, sDV, b_DV + 6 * tt + 6)
                dst = xnt[:, tt * 4096:(tt + 1) * 4096].rearrange(
                    "p (c r) -> p c r", r=128)
                e.dma_start_transpose(dst, ast[p][:]).then_inc(dX.h, 16)
                wge(e, dX, b_dX + 16 * (tt + 1))
      dLp[0].v += 16 * (NT // 2)
      dLp[1].v += 16 * (NT // 2)
      sAC.v += NT
      sDV.v += 6 * NT
      dX.v += 16 * NT

    if phases >= 2:
      with ExitStack() as pctx, nc.Block() as blk:
        psb = lambda n, s, d, **kw: pctx.enter_context(nc.sbuf_tensor(n, s, d, **kw))
        pps = lambda n, s, d: pctx.enter_context(nc.psum_tensor(n, s, d))
        s_wrt = psb("s_wrt", [128, 128], BF16)
        s_at = psb("s_at", [128, 1024], BF16)
        s_eg = psb("s_eg", [128, 4], F32)
        s_g4 = psb("s_g4", [128, 4], F32)
        s_pg = [psb(f"s_pg{i}", [128, 32], BF16) for i in range(2)]
        pgp = [pps(f"pgp{i}", [128, 512], F32) for i in range(2)]
        ppgt = [pps(f"ppgt{i}", [128, 1024], BF16) for i in range(2)]
        b_dW = dW.v
        b_PE, b_DV, b_AC = sPE.v, sDV.v, sAC.v
        pe_mm, pe_tr = [0] * NT, [0] * NT
        c = 0
        for tt in range(NT):
            c += 1
            pe_mm[tt] = c
            if tt >= 1:
                c += 1
                pe_tr[tt - 1] = c
        c += 1
        pe_tr[NT - 1] = c
        n_pe_pro = c
        dv_negmax, dv_projg, dv_evac = [0] * NT, [0] * NT, [0] * NT
        c = 0
        for tt in range(NT):
            c += 1; dv_negmax[tt] = c
            c += 1
            c += 1
            c += 1; dv_projg[tt] = c
            if tt >= 1:
                c += 1; dv_evac[tt - 1] = c
        c += 1; dv_evac[NT - 1] = c
        n_dv_pro = c

        @blk.gpsimd
        def _(e):
            esync(e, "g")
            b = dW.v
            for k, (s, d) in enumerate([(s_wrt, din["wrtb"]), (s_at, din["atb"])]):
                e.dma_start(out=s[:], in_=d[:, :]).then_inc(dW.h, 16)
                e.wait_ge(dW.h, b + 16 * (k + 1))

        @blk.tensor
        def _(e):
            esync(e, "t")
            wge(e, dW, b_dW + 16 * 2)
            wge(e, dL, 48)
            wge(e, dX, dX.v)
            for tt in range(NT):
                p = tt % 2
                if tt >= 2:
                    wge(e, sDV, b_DV + dv_projg[tt - 2])
                for ic in range(NIC):
                    e.matmul(pgp[p][:, 0:4], xn_stat(ic, tt),
                             s_wrt[:, ic * 4:(ic + 1) * 4],
                             start=(ic == 0), stop=(ic == NIC - 1))
                for ic in range(NIC):
                    ins = e.matmul(pgp[p][:, 4:36], xn_stat(ic, tt),
                                   s_at[:, ic * 32:(ic + 1) * 32],
                                   start=(ic == 0), stop=(ic == NIC - 1))
                ins.then_inc(sPE.h, 1)
                if tt >= 1:
                    wge(e, sDV, b_DV + dv_projg[tt - 1])
                    e.transpose(ppgt[(tt - 1) % 2][0:32, 0:128],
                                s_pg[(tt - 1) % 2][:], s_ident[:]).then_inc(sPE.h, 1)
            wge(e, sDV, b_DV + dv_projg[NT - 1])
            e.transpose(ppgt[(NT - 1) % 2][0:32, 0:128],
                        s_pg[(NT - 1) % 2][:], s_ident[:]).then_inc(sPE.h, 1)

        @blk.vector
        def _(e):
            esync(e, "v")
            for tt in range(NT):
                p = tt % 2
                wge(e, sPE, b_PE + pe_mm[tt])
                if tt >= 1:
                    wge(e, sAC, b_AC + tt)
                e.reduce_max(s_cols[:, 8:9], pgp[p][:, 0:4], axis=AX.X,
                             negate=True).then_inc(sDV.h, 1)
                wge(e, sAC, b_AC + tt + 1)
                e.reciprocal(s_cols[:, 10:11], s_kols[:, 2:3]).then_inc(sDV.h, 1)
                wge(e, sDV, b_DV + dv_negmax[tt] + 1)
                e.tensor_scalar(out=s_g4[:], in0=s_eg[:],
                                scalar1=s_cols[:, 10:11], scalar2=2.0,
                                op0=OP.mult, op1=OP.mult).then_inc(sDV.h, 1)
                wge(e, sDV, b_DV + dv_negmax[tt] + 2)
                for ee in range(4):
                    ins = e.tensor_scalar(out=s_pg[p][:, ee * 8:(ee + 1) * 8],
                                          in0=pgp[p][:, 4 + ee * 8: 4 + (ee + 1) * 8],
                                          scalar1=s_g4[:, ee:ee + 1], scalar2=None,
                                          op0=OP.mult)
                ins.then_inc(sDV.h, 1)
                if tt >= 1:
                    wge(e, sPE, b_PE + pe_tr[tt - 1])
                    e.tensor_copy(s_pgT[0:32, (tt - 1) * 128: tt * 128],
                                  ppgt[(tt - 1) % 2][0:32, 0:128]).then_inc(sDV.h, 1)
            wge(e, sPE, b_PE + pe_tr[NT - 1])
            e.tensor_copy(s_pgT[0:32, (NT - 1) * 128: NT * 128],
                          ppgt[(NT - 1) % 2][0:32, 0:128]).then_inc(sDV.h, 1)

        @blk.scalar
        def _(e):
            esync(e, "a")
            for tt in range(NT):
                wge(e, sDV, b_DV + dv_negmax[tt])
                e.activation(s_eg[:], pgp[tt % 2][:, 0:4], AF.Exp,
                             bias=s_cols[:, 8:9],
                             accum_out=s_kols[:, 2:3])
                e.copy(s_kols[:, 1:2], s_eps[:, 0:1]).then_inc(sAC.h, 1)
      dW.v += 16 * 2
      sPE.v += n_pe_pro
      sAC.v += NT
      sDV.v += n_dv_pro

    NOC = 16
    if phases >= 3:
      with ExitStack() as pctx, nc.Block() as blk:
        psb = lambda n, s, d, **kw: pctx.enter_context(nc.sbuf_tensor(n, s, d, **kw))
        pps = lambda n, s, d: pctx.enter_context(nc.psum_tensor(n, s, d))
        wbuf2 = psb("wbuf2", [128, 16384], BF16)
        s_bmt = psb("s_bmt", [32, DM], BF16)
        s_bb = psb("s_bb", [128, DM], F32)
        s_f1 = [psb(f"s_f1{i}", [128, 256], F32) for i in range(2)]
        s_gt = [psb(f"s_gt{i}", [128, 256], BF16) for i in range(2)]
        pg = [pps(f"pg{i}", [128, 512], F32) for i in range(2)]
        b_dW, b_dS, b_dL = dW.v, dS.v, dL.v
        b_PE, b_DV, b_AC = sPE.v, sDV.v, sAC.v
        b_dWp = [dWp[0].v, dWp[1].v]
        b_dSp = [dSp[0].v, dSp[1].v]

        @blk.gpsimd
        def _(e):
            esync(e, "g")
            e.dma_start(out=s_bmt[:], in_=din["bmt"][:, :]).then_inc(dW.h, 16)
            e.wait_ge(dW.h, b_dW + 16)
            for oc in range(NOC):
                p = oc % 2
                if oc >= 2:
                    wge(e, sPE, b_PE + NT * (oc - 1))
                src = din["wt"][:, oc * 256:(oc + 1) * 256].rearrange(
                    "(c p) o -> p c o", p=128)
                dst = wbuf2[:, p * 8192:(p + 1) * 8192].rearrange(
                    "p (c o) -> p c o", o=256)
                e.dma_start(out=dst, in_=src).then_inc(dWp[p].h, 16)

        @blk.tensor
        def _(e):
            esync(e, "t")
            wge(e, dW, b_dW + 16)
            for oc in range(NOC):
                p = oc % 2
                wge(e, dWp[p], b_dWp[p] + 16 * (oc // 2 + 1))
                for tt in range(NT):
                    i = oc * NT + tt
                    pp = i % 2
                    if i >= 2:
                        wge(e, sDV, b_DV + (i - 2) + 1)
                    for ic in range(NIC):
                        e.matmul(pg[pp][:, 0:256],
                                 xn_stat(ic, tt),
                                 wbuf2[:, p * 8192 + ic * 256: p * 8192 + (ic + 1) * 256],
                                 start=(ic == 0), stop=False)
                    e.matmul(pg[pp][:, 0:256],
                             s_pgT[0:32, tt * 128:(tt + 1) * 128],
                             s_bmt[0:32, oc * 256:(oc + 1) * 256],
                             start=False, stop=True).then_inc(sPE.h, 1)

        @blk.scalar
        def _(e):
            esync(e, "a")
            e.dma_start(out=s_bb[:], in_=din["bb"][:, :]).then_inc(dL.h, 16)
            for oc in range(NOC):
                for tt in range(NT):
                    i = oc * NT + tt
                    pp = i % 2
                    wge(e, sDV, b_DV + i + 1)
                    if i >= 2:
                        wge(e, dSp[pp], b_dSp[pp] + 16 * ((i - 2) // 2 + 1))
                    e.activation(s_gt[pp][:], s_f1[pp][:], AF.Silu).then_inc(sAC.h, 1)
                    wge(e, sAC, b_AC + i + 1)
                    e.dma_start(out=gsc[tt * 128:(tt + 1) * 128,
                                        oc * 256:(oc + 1) * 256],
                                in_=s_gt[pp][:]).then_inc(dSp[pp].h, 16)

        @blk.vector
        def _(e):
            esync(e, "v")
            wge(e, dL, b_dL + 16)
            for oc in range(NOC):
                for tt in range(NT):
                    i = oc * NT + tt
                    pp = i % 2
                    wge(e, sPE, b_PE + i + 1)
                    if i >= 2:
                        wge(e, sAC, b_AC + (i - 2) + 1)
                    e.scalar_tensor_tensor(
                        out=s_f1[pp][:], in0=pg[pp][:, 0:256], scalar=1.0,
                        in1=s_bb[:, oc * 256:(oc + 1) * 256],
                        op0=OP.mult, op1=OP.add).then_inc(sDV.h, 1)
      dW.v += 16
      dWp[0].v += 16 * (NOC // 2)
      dWp[1].v += 16 * (NOC // 2)
      dL.v += 16
      dSp[0].v += 16 * (NOC * NT // 2)
      dSp[1].v += 16 * (NOC * NT // 2)
      sPE.v += NOC * NT
      sAC.v += NOC * NT
      sDV.v += NOC * NT

    def proj_phase(pctx, blk, heads, wsrc, dest_kind):
        psb = lambda n, s, d, **kw: pctx.enter_context(nc.sbuf_tensor(n, s, d, **kw))
        pps = lambda n, s, d: pctx.enter_context(nc.psum_tensor(n, s, d))
        wb = psb("wb_" + dest_kind, [128, 8192], BF16)
        psQ = pps("psQ_" + dest_kind, [128, 2048], F32)
        pbc = pps("pbc_" + dest_kind, [128, 2048], F32)
        norm = dest_kind in ("q", "k")
        if norm:
            s_sq = psb("sq_" + dest_kind, [128, 2048], BF16)
            s_w = psb("sw_" + dest_kind, [128, 2048], F32)
            s_rr = psb("srr_" + dest_kind, [1, 2048], F32)
        s_st = [psb(f"st_{dest_kind}{i}", [128, 2048], BF16) for i in range(2)]
        ndst = qhat if dest_kind == "q" else kscr
        wcol = s_wqe if dest_kind == "q" else s_knw
        b_dW, b_dS = dW.v, dS.v
        b_PE, b_DV, b_AC = sPE.v, sDV.v, sAC.v
        b_dWp = [dWp[0].v, dWp[1].v]
        b_dSp = [dSp[0].v, dSp[1].v]

        if norm:
            pe_mm, pe_msq, pe_bc = [0] * heads, [0] * heads, [0] * heads
            c = 0
            for h in range(heads + 1):
                if h < heads:
                    c += 1; pe_mm[h] = c
                if h >= 1:
                    c += 1; pe_msq[h - 1] = c
                    c += 1; pe_bc[h - 1] = c
            n_pe = c
            ac_sq, ac_ln, ac_rs = [0] * heads, [0] * heads, [0] * heads
            c = 0
            for h in range(heads + 1):
                if h < heads:
                    c += 1; ac_sq[h] = c
                if h >= 1:
                    c += 1; ac_ln[h - 1] = c
                    c += 1; ac_rs[h - 1] = c
            n_ac = c
            dv_w, dv_stt = [0] * heads, [0] * heads
            c = 0
            for h in range(heads + 1):
                if h >= 1:
                    c += 1; dv_stt[h - 1] = c
                if h < heads:
                    c += 1; dv_w[h] = c
            n_dv = c
        else:
            pe_mm = [h + 1 for h in range(heads)]
            n_pe = heads
            dv_cp = [h + 1 for h in range(heads)]
            n_dv = heads
            n_ac = 0

        @blk.gpsimd
        def _(e):
            esync(e, "g")
            for h in range(heads):
                p = h % 2
                for half in range(2):
                    ch = 2 * h + half
                    if h >= 1:
                        wge(e, sPE, b_PE + pe_mm[h - 1])
                    src = wsrc[half * 2048:(half + 1) * 2048,
                               h * 128:(h + 1) * 128].rearrange(
                        "(c p) o -> p c o", p=128)
                    dst = wb[:, p * 4096 + half * 2048:
                             p * 4096 + (half + 1) * 2048].rearrange(
                        "p (c o) -> p c o", o=128)
                    e.dma_start(out=dst, in_=src).then_inc(dWp[ch % 2].h, 16)

        @blk.tensor
        def _(e):
            esync(e, "t")
            for h in range(heads + 1):
                if h < heads:
                    p = h % 2
                    if h >= 1:
                        if norm:
                            wge(e, sDV, b_DV + dv_w[h - 1])
                            wge(e, sAC, b_AC + ac_sq[h - 1])
                        else:
                            wge(e, sDV, b_DV + dv_cp[h - 1])
                    for half in range(2):
                        ch = 2 * h + half
                        wge(e, dWp[ch % 2], b_dWp[ch % 2] + 16 * (ch // 2 + 1))
                        for tc in range(4):
                            for ici in range(16):
                                ic = half * 16 + ici
                                ins = e.matmul(
                                    psQ[:, tc * 512:(tc + 1) * 512],
                                    wb[:, p * 4096 + half * 2048 + ici * 128:
                                       p * 4096 + half * 2048 + (ici + 1) * 128],
                                    xn_mov(ic, tc),
                                    start=(ic == 0), stop=(ic == NIC - 1))
                    ins.then_inc(sPE.h, 1)
                if norm and h >= 1:
                    hh = h - 1
                    wge(e, sAC, b_AC + ac_sq[hh])
                    if hh >= 1:
                        wge(e, sDV, b_DV + dv_stt[hh - 1])
                    for tc in range(4):
                        ins = e.matmul(pbc[0:1, tc * 512:(tc + 1) * 512],
                                       s_ones128[:],
                                       s_sq[:, tc * 512:(tc + 1) * 512],
                                       start=True, stop=True)
                    ins.then_inc(sPE.h, 1)
                    wge(e, sAC, b_AC + ac_rs[hh])
                    for tc in range(4):
                        ins = e.matmul(pbc[:, tc * 512:(tc + 1) * 512], s_ones1f[:],
                                       s_rr[0:1, tc * 512:(tc + 1) * 512],
                                       start=True, stop=True)
                    ins.then_inc(sPE.h, 1)

        if norm:
            @blk.scalar
            def _(e):
                esync(e, "a")
                for h in range(heads + 2):
                    if h < heads:
                        wge(e, sPE, b_PE + pe_mm[h])
                        if h >= 1:
                            wge(e, sPE, b_PE + pe_msq[h - 1])
                        e.activation(s_sq[:], psQ[:],
                                     AF.Square).then_inc(sAC.h, 1)
                    if 1 <= h <= heads:
                        hh = h - 1
                        wge(e, sPE, b_PE + pe_msq[hh])
                        e.activation(pbc[32:33, 0:2048], pbc[0:1, 0:2048], AF.Ln,
                                     scale=1.0 / 128.0, bias=s_eps[0:1, 1:2]).then_inc(sAC.h, 1)
                        if hh >= 1:
                            wge(e, sPE, b_PE + pe_bc[hh - 1])
                        wge(e, sAC, b_AC + ac_ln[hh])
                        e.activation(s_rr[:], pbc[32:33, 0:2048], AF.Exp,
                                     scale=-0.5).then_inc(sAC.h, 1)
                    if h >= 2:
                        hs = h - 2
                        wge(e, sDV, b_DV + dv_stt[hs])
                        e.dma_start(out=ndst[hs * 128:(hs + 1) * 128, :],
                                    in_=s_st[hs % 2][:]).then_inc(dSp[hs % 2].h, 16)
        else:
            @blk.scalar
            def _(e):
                esync(e, "a")
                for h in range(heads):
                    wge(e, sDV, b_DV + dv_cp[h])
                    e.dma_start(out=vsc[h, :, :],
                                in_=s_st[h % 2][:]).then_inc(dSp[h % 2].h, 16)

        @blk.vector
        def _(e):
            esync(e, "v")
            for h in range(heads + 1):
                if norm and h >= 1:
                    hh = h - 1
                    wge(e, sPE, b_PE + pe_bc[hh])
                    if hh >= 2:
                        wge(e, dSp[hh % 2], b_dSp[hh % 2] + 16 * ((hh - 2) // 2 + 1))
                    wge(e, sDV, b_DV + dv_w[hh])
                    dst = s_st[hh % 2][:]
                    e.scalar_tensor_tensor(out=dst, in0=pbc[:], scalar=1.0,
                                           in1=s_w[:], op0=OP.mult,
                                           op1=OP.mult).then_inc(sDV.h, 1)
                if h < heads and norm:
                    wge(e, sPE, b_PE + pe_mm[h])
                    if h >= 1:
                        wge(e, sDV, b_DV + dv_stt[h - 1])
                    e.tensor_scalar(out=s_w[:], in0=psQ[:], scalar1=wcol[:, 0:1],
                                    scalar2=None, op0=OP.mult).then_inc(sDV.h, 1)
                if h < heads and not norm:
                    wge(e, sPE, b_PE + pe_mm[h])
                    if h >= 2:
                        wge(e, dSp[h % 2], b_dSp[h % 2] + 16 * ((h - 2) // 2 + 1))
                    e.tensor_copy(s_st[h % 2][:], psQ[:]).then_inc(sDV.h, 1)

        return dict(dWp0=16 * heads, dWp1=16 * heads,
                    dSp0=16 * ((heads + 1) // 2), dSp1=16 * (heads // 2),
                    sPE=n_pe, sDV=n_dv, sAC=n_ac)

    def apply_inc(inc):
        m = {"dWp0": dWp[0], "dWp1": dWp[1], "dSp0": dSp[0], "dSp1": dSp[1],
             "sPE": sPE, "sDV": sDV, "sAC": sAC}
        for k_, v_ in inc.items():
            m[k_].v += v_

    if phases >= 4:
        with ExitStack() as pctx, nc.Block() as blk:
            apply_inc(proj_phase(pctx, blk, H, din["wqt"], "q"))
    if phases >= 5:
        with ExitStack() as pctx, nc.Block() as blk:
            apply_inc(proj_phase(pctx, blk, HKV, din["wkt"], "k"))
    if phases >= 6:
        with ExitStack() as pctx, nc.Block() as blk:
            apply_inc(proj_phase(pctx, blk, HKV, din["wvt"], "v"))

    if tap_out:
        with nc.Block() as blk:
            b_dS = dS.v
            ntap = len(tap_out)

            @blk.scalar
            def _(e):
                esync(e, "a")
                e.wait_ge(dX.h, dX.v)
                n = 0
                if "xnt" in tap_out:
                    n += 1
                    e.dma_start(out=tap_out["xnt"][:, :], in_=xnt[:]).then_inc(dS.h, 16)
                if "dbg" in tap_out:
                    n += 2
                    e.dma_start(out=tap_out["dbg"][:, :], in_=s_cols[:]).then_inc(dS.h, 16)
                    e.dma_start(out=tap_out["dbg2"][:, :],
                                in_=s_eps[:]).then_inc(dS.h, 16)
                e.wait_ge(dS.h, b_dS + 16 * n)
        dS.v += 16 * ntap

    xnt_ctx.close()

    NQ = H * NT
    if phases >= 7:
      with ExitStack() as pctx, nc.Block() as blk:
        psb = lambda n, s, d, **kw: pctx.enter_context(nc.sbuf_tensor(n, s, d, **kw))
        pps = lambda n, s, d: pctx.enter_context(nc.psum_tensor(n, s, d))
        vv = psb("vv", [128, HKV * SEQ], BF16)
        kt = psb("kt", [128, HKV * SEQ], BF16)
        s_alw = psb("s_alw", [128, H * 128], BF16)
        s_alk = psb("s_alk", [128, SEQ], BF16)
        s_nu = psb("s_nu", [128, 128], BF16)
        qb = [psb(f"qb{i}", [128, SEQ], BF16) for i in range(2)]
        eb = [psb(f"eb{i}", [128, SEQ], BF16) for i in range(4)]
        en = [psb(f"en{i}", [128, SEQ], BF16) for i in range(4)]
        et = [psb(f"et{i}", [128, SEQ], BF16) for i in range(6)]
        ost = [psb(f"ost{i}", [128, SEQ], BF16) for i in range(2)]
        s_np = [psb(f"s_np{i}", [128, 16], F32) for i in range(4)]
        s_al = [psb(f"s_al{i}", [128, 8], F32) for i in range(4)]
        psS = pps("psS", [128, 3072], F32)
        psO = [pps(f"psO{i}", [128, 512], F32) for i in range(2)]
        b_dL, b_dX = dL.v, dX.v
        b_dLp = [dLp[0].v, dLp[1].v]
        b_dSp = [dSp[0].v, dSp[1].v]
        b_S, b_M, b_E, b_N, b_P, b_V = sS5.v, sM5.v, sE5.v, sN5.v, sP5.v, sV5.v
        b_F = sF5.v
        dS_tot = (dS.v, dSp[0].v, dSp[1].v)
        NCONST = 3

        def ncnk(qt):
            kw = (qt + 1) * 128
            return (kw + 511) // 512, qt + 1, kw

        slot_of, sfree_wait = {}, {}
        slot_owner = [None] * 6
        gslot = 0
        for idx in range(NQ):
            for cch in range(ncnk(idx % NT)[0]):
                sl = gslot % 6
                gslot += 1
                own = slot_owner[sl]
                sfree_wait[(idx, cch)] = (own + 1) if own is not None else 0
                slot_of[(idx, cch)] = sl
                slot_owner[sl] = idx

        @blk.scalar
        def _(e):
            esync(e, "a")
            for s, v in zip((dS, dSp[0], dSp[1]), dS_tot):
                if v > 0:
                    e.wait_ge(s.h, v)
            for s, d in [(s_alw, din["alw"]), (s_alk, din["alk"]),
                         (s_nu, din["negu"])]:
                e.dma_start(out=s[:], in_=d[:, :]).then_inc(dL.h, 16)
            for kv in range(HKV):
                e.dma_start(out=kt[:, kv * SEQ:(kv + 1) * SEQ],
                            in_=kscr[kv * 128:(kv + 1) * 128, :]).then_inc(dL.h, 16)
            for h in range(2):
                e.dma_start(out=qb[h][:],
                            in_=qhat[h * 128:(h + 1) * 128, :]).then_inc(dLp[h].h, 16)

            def ostore(e, hs):
                wge(e, sV5, b_V + NT * (hs + 1))
                dsto = othbm[:, :, hs, :].rearrange("q p r -> p q r")
                srco = ost[hs % 2][:].rearrange("p (q r) -> p q r", r=128)
                e.dma_start(out=dsto, in_=srco).then_inc(dSp[hs % 2].h, 16)

            for idx in range(NQ):
                h, qt = idx // NT, idx % NT
                qp = idx % 4
                nc512, nk, kw = ncnk(qt)
                wge(e, sM5, b_M + idx + 1)
                if idx >= 4:
                    wge(e, sN5, b_N + idx - 3)
                for cch in range(nc512):
                    wc = min(512, kw - cch * 512)
                    sl = slot_of[(idx, cch)]
                    e.activation(eb[qp][:, cch * 512: cch * 512 + wc],
                                 psS[:, sl * 512: sl * 512 + wc], AF.Exp,
                                 bias=s_np[qp][:, 4:5],
                                 accum_out=s_al[qp][:, cch:cch + 1])
                e.copy(s_al[qp][:, 4:5],
                       s_eps[:, 0:1]).then_inc(sE5.h, 1)
                if idx >= 17 and (idx - 17) % NT == 0:
                    ostore(e, (idx - 17) // NT)
                if qt == NT - 1 and h + 2 < H:
                    wge(e, sS5, b_S + NT * (h + 1))
                    e.dma_start(out=qb[h % 2][:],
                                in_=qhat[(h + 2) * 128:(h + 3) * 128, :]
                                ).then_inc(dLp[h % 2].h, 16)
            ostore(e, H - 1)

        def _pv(e, I):
            hp, qtp = I // NT, I % NT
            kvp = hp // 4
            nkp = qtp + 1
            wge(e, dX, b_dX + 16 * (HKV + I + 1))
            if I >= 2:
                wge(e, sV5, b_V + I - 1)
            ins = None
            for kc in range(nkp):
                ins = e.matmul(psO[I % 2][:, 0:128],
                               vv[:, kvp * SEQ + kc * 128: kvp * SEQ + (kc + 1) * 128],
                               et[I % 6][:, kc * 128:(kc + 1) * 128],
                               start=(kc == 0), stop=(kc == nkp - 1))
            ins.then_inc(sP5.h, 1)

        @blk.tensor
        def _(e):
            esync(e, "t")
            wge(e, dL, b_dL + 16 * (NCONST + HKV))
            for idx in range(NQ):
                h, qt = idx // NT, idx % NT
                kv = h // 4
                nc512, nk, kw = ncnk(qt)
                wge(e, dLp[h % 2], b_dLp[h % 2] + 16 * (h // 2 + 1))
                cl = nc512 - 1
                off = qt * 128 - cl * 512
                for cch in range(nc512):
                    wc = min(512, kw - cch * 512)
                    sl = slot_of[(idx, cch)]
                    diag = (cch == cl)
                    wge(e, sE5, b_E + sfree_wait[(idx, cch)])
                    e.matmul(psS[:, sl * 512: sl * 512 + wc],
                             qb[h % 2][:, qt * 128:(qt + 1) * 128],
                             kt[:, kv * SEQ + cch * 512:
                                kv * SEQ + cch * 512 + wc],
                             start=True, stop=False)
                    ins = e.matmul(psS[:, sl * 512: sl * 512 + wc],
                                   s_alw[:, h * 128:(h + 1) * 128],
                                   s_alk[:, cch * 512: cch * 512 + wc],
                                   start=False, stop=not diag)
                    if diag:
                        ins = e.matmul(psS[:, sl * 512 + off: sl * 512 + off + 128],
                                       s_ident[:], s_nu[:],
                                       start=False, stop=True)
                        ins.then_inc(sS5.h, 1)
                if idx >= 4:
                    _pv(e, idx - 4)
            for I in range(NQ - 4, NQ):
                _pv(e, I)

        def _evac(e, I):
            hp, qtp = I // NT, I % NT
            wge(e, sP5, b_P + I + 1)
            if hp >= 2 and qtp == 0:
                wge(e, dSp[hp % 2], b_dSp[hp % 2] + 16 * ((hp - 2) // 2 + 1))
            e.tensor_copy(ost[hp % 2][:, qtp * 128:(qtp + 1) * 128],
                          psO[I % 2][:, 0:128]).then_inc(sV5.h, 1)

        @blk.vector
        def _(e):
            esync(e, "v")
            for idx in range(NQ + 4):
                qp = idx % 4
                qp2 = (idx - 2) % 4
                if 2 <= idx < NQ + 2:
                    nc2 = ncnk((idx - 2) % NT)[0]
                    wge(e, sE5, b_E + idx - 1)
                    e.reduce_sum(s_np[qp2][:, 12:13], s_al[qp2][:, 0:nc2],
                                 axis=AX.X)
                if idx < NQ:
                    nc512, nk, kw = ncnk(idx % NT)
                    wge(e, sS5, b_S + idx + 1)
                    for cch in range(nc512):
                        wc = min(512, kw - cch * 512)
                        sl = slot_of[(idx, cch)]
                        ins = e.reduce_max(s_np[qp][:, cch:cch + 1],
                                           psS[:, sl * 512: sl * 512 + wc],
                                           axis=AX.X, negate=True)
                    ins.then_inc(sF5.h, 1)
                else:
                    e.tensor_copy(s_np[0][:, 15:16],
                                  s_eps[:, 0:1]).then_inc(sF5.h, 1)
                wge(e, sF5, b_F + idx + 1)
                if 2 <= idx < NQ + 2:
                    e.reciprocal(s_np[qp2][:, 13:14], s_np[qp2][:, 12:13])
                if idx < NQ:
                    nc512 = ncnk(idx % NT)[0]
                    e.tensor_reduce(s_np[qp][:, 4:5], s_np[qp][:, 0:nc512],
                                    axis=AX.X, op=OP.min).then_inc(sM5.h, 1)
                else:
                    e.tensor_copy(s_np[0][:, 14:15],
                                  s_eps[:, 0:1]).then_inc(sM5.h, 1)
                wge(e, sM5, b_M + idx + 1)
                if 2 <= idx < NQ + 2:
                    kw2 = ncnk((idx - 2) % NT)[2]
                    if idx >= 6:
                        wge(e, dX, b_dX + 16 * (HKV + idx - 5))
                    e.tensor_scalar(out=en[qp2][:, 0:kw2], in0=eb[qp2][:, 0:kw2],
                                    scalar1=s_np[qp2][:, 13:14], scalar2=None,
                                    op0=OP.mult).then_inc(sN5.h, 1)
                if idx >= 4:
                    _evac(e, idx - 4)

        @blk.sync
        def _(e):
            esync(e, "g")
            for s, v in zip((dS, dSp[0], dSp[1]), dS_tot):
                if v > 0:
                    e.wait_ge(s.h, v)
            for dc in range(HKV):
                dstv = vv[:, dc * SEQ:(dc + 1) * SEQ].rearrange(
                    "p (c r) -> p c r", r=128)
                e.dma_start_transpose(dstv, vsc[dc, :, :]).then_inc(dX.h, 16)
                wge(e, dX, b_dX + 16 * (dc + 1))
            for idx in range(NQ):
                qp = idx % 4
                kw = ncnk(idx % NT)[2]
                wge(e, sN5, b_N + idx + 1)
                if idx >= 6:
                    wge(e, sP5, b_P + idx - 5)
                dstt = et[idx % 6][:, 0:kw].rearrange("p (c r) -> p c r", r=128)
                e.dma_start_transpose(dstt, en[qp][:, 0:kw]).then_inc(dX.h, 16)
                wge(e, dX, b_dX + 16 * (HKV + idx + 1))
      dL.v += 16 * (3 + HKV)
      dLp[0].v += 16 * (H // 2)
      dLp[1].v += 16 * (H // 2)
      dSp[0].v += 16 * (H // 2)
      dSp[1].v += 16 * (H // 2)
      sF5.v += NQ + 4
      dX.v += 16 * (HKV + NQ)
      sS5.v += NQ
      sM5.v += NQ + 4
      sE5.v += NQ
      sN5.v += NQ
      sP5.v += NQ
      sV5.v += NQ

    NJC = 4
    if phases >= 8:
      with ExitStack() as pctx, nc.Block() as blk:
        psb = lambda n, s, d, **kw: pctx.enter_context(nc.sbuf_tensor(n, s, d, **kw))
        pps = lambda n, s, d: pctx.enter_context(nc.psum_tensor(n, s, d))
        wo6 = [psb(f"wo6_{i}", [128, 32768], BF16) for i in range(2)]
        s_ot = [psb(f"s_ot{i}", [128, H * 128], BF16) for i in range(2)]
        s_g6 = [psb(f"s_g6{i}", [128, 1024], BF16) for i in range(2)]
        s_x6 = [psb(f"s_x6{i}", [128, 1024], F32) for i in range(2)]
        s_t6 = [psb(f"s_t6{i}", [128, 512], F32) for i in range(2)]
        s_o6 = [psb(f"s_o6{i}", [128, 1024], F32) for i in range(2)]
        py = [pps(f"py{i}", [128, 512], F32) for i in range(2)]
        b_dW, b_dL, b_dS = dW.v, dL.v, dS.v
        b_PE, b_DV = sPE.v, sDV.v
        b_dWp = [dWp[0].v, dWp[1].v]
        b_dLp = [dLp[0].v, dLp[1].v]
        b_dSp = [dSp[0].v, dSp[1].v]
        dS_prev6 = (dS.v, dSp[0].v, dSp[1].v)

        @blk.gpsimd
        def _(e):
            esync(e, "g")
            for jc in range(NJC):
                p = jc % 2
                if jc >= 2:
                    wge(e, sPE, b_PE + 2 * NT * (jc - 1))
                for cc in range(4):
                    src = din["wot"][cc * 1024:(cc + 1) * 1024,
                                     jc * 1024:(jc + 1) * 1024].rearrange(
                        "(c p) o -> p c o", p=128)
                    dst = wo6[p][:, cc * 8192:(cc + 1) * 8192].rearrange(
                        "p (c o) -> p c o", o=1024)
                    e.dma_start(out=dst, in_=src).then_inc(dWp[jc % 2].h, 16)

        @blk.scalar
        def _(e):
            esync(e, "a")
            wge(e, dS, dS_prev6[0])
            wge(e, dSp[0], dS_prev6[1])
            wge(e, dSp[1], dS_prev6[2])
            for jc in range(NJC):
                for tt in range(NT):
                    i = jc * NT + tt
                    p = i % 2
                    if i >= 2:
                        wge(e, sPE, b_PE + 2 * (i - 2) + 2)
                        wge(e, sDV, b_DV + 4 * (i - 2) + 4)
                    e.dma_start(out=s_ot[p][:],
                                in_=othbm[tt, :, :, :].rearrange(
                                    "p h r -> p (h r)")).then_inc(dLp[p].h, 16)
                    e.dma_start(out=s_g6[p][:],
                                in_=gsc[tt * 128:(tt + 1) * 128,
                                        jc * 1024:(jc + 1) * 1024]).then_inc(dLp[p].h, 16)
                    e.dma_start(out=s_x6[p][:],
                                in_=din["x"][tt * 128:(tt + 1) * 128,
                                             jc * 1024:(jc + 1) * 1024]).then_inc(dLp[p].h, 16)
                    wge(e, sDV, b_DV + 4 * i + 4)
                    e.dma_start(out=out[tt * 128:(tt + 1) * 128,
                                        jc * 1024:(jc + 1) * 1024],
                                in_=s_o6[p][:]).then_inc(dSp[p].h, 16)
            e.wait_ge(dSp[0].h, b_dSp[0] + 16 * (NJC * NT // 2))
            e.wait_ge(dSp[1].h, b_dSp[1] + 16 * (NJC * NT // 2))

        @blk.tensor
        def _(e):
            esync(e, "t")
            for jc in range(NJC):
                p = jc % 2
                wge(e, dWp[jc % 2], b_dWp[jc % 2] + 64 * (jc // 2 + 1))
                for tt in range(NT):
                    i = jc * NT + tt
                    pp = i % 2
                    wge(e, dLp[pp], b_dLp[pp] + 48 * (i // 2 + 1))
                    for j2 in range(2):
                        if i >= 1:
                            wge(e, sDV, b_DV + 4 * (i - 1) + 2 * j2 + 2)
                        for hc in range(H):
                            ins = e.matmul(py[j2][:, 0:512],
                                           s_ot[pp][:, hc * 128:(hc + 1) * 128],
                                           wo6[p][:, hc * 1024 + j2 * 512:
                                                  hc * 1024 + (j2 + 1) * 512],
                                           start=(hc == 0), stop=(hc == H - 1))
                        ins.then_inc(sPE.h, 1)

        @blk.vector
        def _(e):
            esync(e, "v")
            for jc in range(NJC):
                for tt in range(NT):
                    i = jc * NT + tt
                    pp = i % 2
                    wge(e, dLp[pp], b_dLp[pp] + 48 * (i // 2 + 1))
                    if i >= 2:
                        wge(e, dSp[pp], b_dSp[pp] + 16 * (i // 2))
                    for j2 in range(2):
                        wge(e, sPE, b_PE + 2 * i + j2 + 1)
                        e.scalar_tensor_tensor(
                            out=s_t6[j2][:], in0=py[j2][:, 0:512], scalar=1.0,
                            in1=s_g6[pp][:, j2 * 512:(j2 + 1) * 512],
                            op0=OP.mult, op1=OP.mult).then_inc(sDV.h, 1)
                    wge(e, sDV, b_DV + 4 * i + 2)
                    for j2 in range(2):
                        e.tensor_tensor(
                            out=s_o6[pp][:, j2 * 512:(j2 + 1) * 512],
                            in0=s_t6[j2][:],
                            in1=s_x6[pp][:, j2 * 512:(j2 + 1) * 512],
                            op=OP.add).then_inc(sDV.h, 1)
      dWp[0].v += 64 * (NJC // 2)
      dWp[1].v += 64 * (NJC // 2)
      dLp[0].v += 48 * (NJC * NT // 2)
      dLp[1].v += 48 * (NJC * NT // 2)
      dSp[0].v += 16 * (NJC * NT // 2)
      dSp[1].v += 16 * (NJC * NT // 2)
      sPE.v += 2 * NJC * NT
      sDV.v += 4 * NJC * NT

    with nc.Block() as blk:
        @blk.scalar
        def _(e):
            for s in (dL, dLp[0], dLp[1], dS, dSp[0], dSp[1], dW,
                      dWp[0], dWp[1], dX):
                if s.v > 0:
                    e.wait_ge(s.h, s.v)

    g.close()
    sctx.close()
    return nc, din, out, tap_out


def _host_layout(inputs):
    x = np.ascontiguousarray(np.asarray(inputs["L0_attn_x"], dtype=np.float32))
    wq = np.asarray(inputs["L0_attn_Wq"], np.float32)
    wk = np.asarray(inputs["L0_attn_Wk"], np.float32)
    wv = np.asarray(inputs["L0_attn_Wv"], np.float32)
    wo = np.asarray(inputs["L0_attn_Wo"], np.float32)
    w = np.asarray(inputs["L0_ffn_W"], np.float32)
    b = np.asarray(inputs["L0_ffn_b"], np.float32)
    A = np.asarray(inputs["L0_ffn_A"], np.float32)
    Bm = np.asarray(inputs["L0_ffn_Bm"], np.float32)
    Wr = np.asarray(inputs["L0_ffn_Wr"], np.float32)
    qnw = np.asarray(inputs["L0_attn_q_norm_w"], np.float32).reshape(128, 1)
    knw = np.asarray(inputs["L0_attn_k_norm_w"], np.float32).reshape(128, 1)

    ident, dmask, alw, alk, negu = host_constants()
    com = {
        "wqt": np.ascontiguousarray(wq.T),
        "wkt": np.ascontiguousarray(wk.T),
        "wvt": np.ascontiguousarray(wv.T),
        "wot": np.ascontiguousarray(wo.T),
        "wt": np.ascontiguousarray(w.T),
        "wrtb": np.ascontiguousarray(
            Wr.T.reshape(32, 128, 4).transpose(1, 0, 2).reshape(128, 128)),
        "atb": np.ascontiguousarray(
            A.reshape(32, DM).T.reshape(32, 128, 32).transpose(1, 0, 2)
            .reshape(128, 1024)),
        "bmt": np.ascontiguousarray(Bm.transpose(0, 2, 1).reshape(32, DM)),
        "bb": np.ascontiguousarray(np.broadcast_to(b, (128, DM))),
        "qnw": np.ascontiguousarray(qnw),
        "knw": np.ascontiguousarray(knw),
        "ident": ident, "dmask": dmask, "alw": alw, "alk": alk, "negu": negu,
    }
    return x, com


def kernel(**inputs):
    x, com = _host_layout(inputs)
    nc, din, out_ap, taps = build()
    in_maps = [dict(com, x=np.ascontiguousarray(x[c])) for c in range(8)]
    res = run_bass_kernel_spmd(nc, in_maps, core_ids=list(range(8)))
    return np.stack([np.asarray(r["out"]) for r in res.results], axis=0)


# version 05 (final)
# speedup vs baseline: 1.3832x; 1.0366x over previous
import math
from contextlib import ExitStack

import numpy as np
import ml_dtypes

import concourse.bass as bass
import concourse.mybir as mybir
from concourse.alu_op_type import AluOpType as OP
from concourse.bass_utils import run_bass_kernel_spmd

F32 = mybir.dt.float32
BF16 = mybir.dt.bfloat16
AF = mybir.ActivationFunctionType
AX = mybir.AxisListType

SEQ = 2048
DM = 4096
H = 32
HKV = 8
NT = 16
NIC = 32
NEG = -1.0e30
LN_EPS = 1.0e-5
RMS_EPS = 1.0e-6
INV_SQRT_HD = 1.0 / math.sqrt(128.0)


class Sem:
    def __init__(self, ctx, nc, name):
        self.h = ctx.enter_context(nc.semaphore(name))
        self.v = 0


def _alibi_slopes(n_heads):
    start = 2 ** (-(2 ** -(math.log2(n_heads) - 3)))
    return np.array([start * start ** i for i in range(n_heads)], dtype=np.float32)


def host_constants():
    bf = ml_dtypes.bfloat16
    ident = np.eye(128, dtype=np.float32).astype(bf)
    p = np.arange(128)
    dmask = np.where(p[None, :] <= p[:, None], 0.0, NEG).astype(np.float32)
    slopes = _alibi_slopes(H)
    alw = np.zeros((128, H * 128), dtype=np.float32)
    for h in range(H):
        s_hi = np.float32(bf(slopes[h]))
        s_lo = np.float32(bf(np.float32(slopes[h]) - s_hi))
        alw[0, h * 128:(h + 1) * 128] = 256.0 * s_hi
        alw[1, h * 128:(h + 1) * 128] = s_hi
        alw[2, h * 128:(h + 1) * 128] = 256.0 * s_lo
        alw[3, h * 128:(h + 1) * 128] = s_lo
    alw = alw.astype(bf)
    k = np.arange(SEQ)
    alk = np.zeros((128, SEQ), dtype=np.float32)
    alk[0] = (k // 256).astype(np.float32)
    alk[1] = (k % 256).astype(np.float32)
    alk[2] = alk[0]
    alk[3] = alk[1]
    alk = alk.astype(bf)
    negu = np.where(p[None, :] > p[:, None], NEG, 0.0).astype(np.float32).astype(bf)
    return ident, dmask, alw, alk, negu


def build(taps=(), phases=8):
    nc = bass.Bass("TRN2", target_bir_lowering=False)

    din = {}
    for name, shape, dt in [
        ("x", [SEQ, DM], F32),
        ("wqt", [DM, DM], F32),
        ("wkt", [DM, HKV * 128], F32),
        ("wvt", [DM, HKV * 128], F32),
        ("wot", [DM, DM], F32),
        ("wt", [DM, DM], F32),
        ("wrtb", [128, 128], F32),
        ("atb", [128, 1024], F32),
        ("bmt", [32, DM], F32),
        ("bb", [128, DM], F32),
        ("qnw", [128, 1], F32),
        ("knw", [128, 1], F32),
        ("ident", [128, 128], BF16),
        ("dmask", [128, 128], F32),
        ("alw", [128, H * 128], BF16),
        ("alk", [128, SEQ], BF16),
        ("negu", [128, 128], BF16),
    ]:
        din[name] = nc.dram_tensor(name, shape, dt, kind="ExternalInput").ap()

    def scratch(name, shape):
        kind = "ExternalOutput" if name in taps else "Internal"
        return nc.dram_tensor(name, shape, BF16, kind=kind).ap()
    qhat = scratch("qhat", [H * 128, SEQ])
    kscr = scratch("kscr", [HKV * 128, SEQ])
    othbm = scratch("ot", [NT, 128, H, 128])
    gsc = scratch("gsc", [SEQ, DM])
    vsc = scratch("vsc", [HKV, 128, SEQ])
    out = nc.dram_tensor("out", [SEQ, DM], F32, kind="ExternalOutput").ap()

    tap_out = {}
    for t in taps:
        if t in ("qhat", "kscr", "ot", "gsc", "vsc"):
            continue
        if t == "dbg":
            tap_out[t] = nc.dram_tensor("tap_dbg", [128, 16], F32,
                                        kind="ExternalOutput").ap()
            tap_out["dbg2"] = nc.dram_tensor("tap_dbg2", [128, 2], F32,
                                             kind="ExternalOutput").ap()
            continue
        shp = {"xnt": [128, NIC * SEQ]}[t]
        tap_out[t] = nc.dram_tensor("tap_" + t, shp, BF16, kind="ExternalOutput").ap()

    g = ExitStack()
    sb = lambda n, s, d, **kw: g.enter_context(nc.sbuf_tensor(n, s, d, **kw))
    s_ident = sb("s_ident", [128, 128], BF16)
    s_qnw = sb("s_qnw", [128, 1], F32)
    s_knw = sb("s_knw", [128, 1], F32)
    s_wqe = sb("s_wqe", [128, 1], F32)
    s_ones128 = sb("s_ones128", [128, 1], BF16)
    s_ones1f = sb("s_ones1f", [1, 128], F32)
    s_cols = sb("s_cols", [128, 16], F32)
    s_eps = sb("s_eps", [128, 2], F32)
    s_kols = sb("s_kols", [128, 4], F32)
    s_pgT = sb("s_pgT", [32, SEQ], BF16)

    sctx = ExitStack()
    dL = Sem(sctx, nc, "dL")
    dLp = [Sem(sctx, nc, f"dL{i}") for i in range(2)]
    dS = Sem(sctx, nc, "dS")
    dSp = [Sem(sctx, nc, f"dS{i}") for i in range(2)]
    dW = Sem(sctx, nc, "dW")
    dWp = [Sem(sctx, nc, f"dW{i}") for i in range(2)]
    dX = Sem(sctx, nc, "dX")
    sPE = Sem(sctx, nc, "sPE")
    sDV = Sem(sctx, nc, "sDV")
    sAC = Sem(sctx, nc, "sAC")
    sS5 = Sem(sctx, nc, "sS5")
    sM5 = Sem(sctx, nc, "sM5")
    sE5 = Sem(sctx, nc, "sE5")
    sN5 = Sem(sctx, nc, "sN5")
    sP5 = Sem(sctx, nc, "sP5")
    sV5 = Sem(sctx, nc, "sV5")
    sF5 = Sem(sctx, nc, "sF5")
    sEc = Sem(sctx, nc, "sEc")

    def wge(e, sem, val):
        if val > 0:
            e.wait_ge(sem.h, val)

    def esync(e, kind):
        for s in (sPE, sDV, sAC, sS5, sP5, sV5, sM5, sN5, sE5, sF5, sEc,
                  dL, dS, dW, dX, dLp[0], dLp[1], dSp[0], dSp[1],
                  dWp[0], dWp[1]):
            if s.v > 0:
                e.wait_ge(s.h, s.v)

    xnt_ctx = ExitStack()
    xnt = xnt_ctx.enter_context(nc.sbuf_tensor("xnt", [128, NIC * SEQ], BF16))

    def xn_stat(ic, tt):
        return xnt[:, tt * 4096 + ic * 128: tt * 4096 + (ic + 1) * 128]

    def xn_mov(ic, tc):
        v3 = xnt[:, :].rearrange("p (a b) -> p a b", b=4096)
        return v3[:, tc * 4:(tc + 1) * 4, ic * 128:(ic + 1) * 128]

    with nc.Block() as blk:
        @blk.scalar
        def _(e):
            esync(e, "a")
            for s, d in [(s_ident, din["ident"]), (s_qnw, din["qnw"]),
                         (s_knw, din["knw"])]:
                e.dma_start(out=s[:], in_=d[:, :]).then_inc(dL.h, 16)

        @blk.vector
        def _(e):
            esync(e, "v")
            e.wait_ge(dL.h, dL.v + 48)
            e.memset(s_ones128[:], 1.0)
            e.memset(s_ones1f[:], 1.0)
            e.memset(s_eps[:, 0:1], LN_EPS)
            e.memset(s_eps[:, 1:2], RMS_EPS)
            e.tensor_scalar_mul(s_wqe[:], s_qnw[:], INV_SQRT_HD).then_inc(sDV.h, 1)
    dL.v += 48
    sDV.v += 1

    if phases >= 1:
      with ExitStack() as pctx, nc.Block() as blk:
        psb = lambda n, s, d, **kw: pctx.enter_context(nc.sbuf_tensor(n, s, d, **kw))
        xt = [psb(f"xt{i}", [128, DM], F32) for i in range(2)]
        ast = [psb(f"ast{i}", [128, DM], BF16) for i in range(2)]
        b_dX, b_AC, b_DV = dX.v, sAC.v, sDV.v
        b_dLp = [dLp[0].v, dLp[1].v]

        @blk.scalar
        def _(e):
            esync(e, "a")
            for tt in range(NT):
                p = tt % 2
                if tt >= 2:
                    wge(e, sDV, b_DV + 6 * (tt - 2) + 6)
                e.dma_start(out=xt[p][:],
                            in_=din["x"][tt * 128:(tt + 1) * 128, :]) \
                    .then_inc(dLp[p].h, 16)
                wge(e, sDV, b_DV + 6 * tt + 4)
                e.activation(s_kols[:, 0:1], s_cols[:, 4:5], AF.Sqrt,
                             scale=1.0 / DM, bias=s_eps[:, 0:1]).then_inc(sAC.h, 1)

        @blk.vector
        def _(e):
            esync(e, "v")
            for tt in range(NT):
                p = tt % 2
                wge(e, dLp[p], b_dLp[p] + 16 * (tt // 2 + 1))
                e.reduce_sum(s_cols[:, 0:1], xt[p][:], axis=AX.X)
                if tt >= 2:
                    wge(e, dX, b_dX + 16 * (tt - 1))
                e.tensor_tensor(out=ast[p][:], in0=xt[p][:], in1=xt[p][:],
                                op=OP.mult).then_inc(sDV.h, 1)
                wge(e, sDV, b_DV + 6 * tt + 1)
                e.tensor_scalar(out=ast[p][:], in0=ast[p][:], scalar1=1.0,
                                scalar2=0.0, op0=OP.mult, op1=OP.add,
                                accum_out=s_cols[:, 1:2]).then_inc(sDV.h, 1)
                wge(e, sDV, b_DV + 6 * tt + 2)
                e.tensor_scalar(out=s_cols[:, 5:6], in0=s_cols[:, 0:1],
                                scalar1=1.0 / DM, scalar2=s_cols[:, 0:1],
                                op0=OP.mult, op1=OP.mult)
                e.tensor_scalar(out=s_cols[:, 2:3], in0=s_cols[:, 0:1],
                                scalar1=1.0 / DM, scalar2=None,
                                op0=OP.mult).then_inc(sDV.h, 1)
                wge(e, sDV, b_DV + 6 * tt + 3)
                if tt >= 1:
                    wge(e, sAC, b_AC + tt)
                e.tensor_tensor(out=s_cols[:, 4:5], in0=s_cols[:, 1:2],
                                in1=s_cols[:, 5:6],
                                op=OP.subtract).then_inc(sDV.h, 1)
                wge(e, sAC, b_AC + tt + 1)
                e.reciprocal(s_cols[:, 7:8], s_kols[:, 0:1]).then_inc(sDV.h, 1)
                wge(e, sDV, b_DV + 6 * tt + 5)
                e.tensor_scalar(out=ast[p][:], in0=xt[p][:],
                                scalar1=s_cols[:, 2:3], scalar2=s_cols[:, 7:8],
                                op0=OP.subtract, op1=OP.mult).then_inc(sDV.h, 1)

        @blk.sync
        def _(e):
            esync(e, "g")
            for tt in range(NT):
                p = tt % 2
                wge(e, sDV, b_DV + 6 * tt + 6)
                dst = xnt[:, tt * 4096:(tt + 1) * 4096].rearrange(
                    "p (c r) -> p c r", r=128)
                e.dma_start_transpose(dst, ast[p][:]).then_inc(dX.h, 16)
                wge(e, dX, b_dX + 16 * (tt + 1))
      dLp[0].v += 16 * (NT // 2)
      dLp[1].v += 16 * (NT // 2)
      sAC.v += NT
      sDV.v += 6 * NT
      dX.v += 16 * NT

    if phases >= 2:
      with ExitStack() as pctx, nc.Block() as blk:
        psb = lambda n, s, d, **kw: pctx.enter_context(nc.sbuf_tensor(n, s, d, **kw))
        pps = lambda n, s, d: pctx.enter_context(nc.psum_tensor(n, s, d))
        s_wrt = psb("s_wrt", [128, 128], BF16)
        s_at = psb("s_at", [128, 1024], BF16)
        s_eg = psb("s_eg", [128, 4], F32)
        s_g4 = psb("s_g4", [128, 4], F32)
        s_pg = [psb(f"s_pg{i}", [128, 32], BF16) for i in range(2)]
        pgp = [pps(f"pgp{i}", [128, 512], F32) for i in range(2)]
        ppgt = [pps(f"ppgt{i}", [128, 1024], BF16) for i in range(2)]
        b_dW = dW.v
        b_PE, b_DV, b_AC = sPE.v, sDV.v, sAC.v
        pe_mm, pe_tr = [0] * NT, [0] * NT
        c = 0
        for tt in range(NT):
            c += 1
            pe_mm[tt] = c
            if tt >= 1:
                c += 1
                pe_tr[tt - 1] = c
        c += 1
        pe_tr[NT - 1] = c
        n_pe_pro = c
        dv_negmax, dv_projg, dv_evac = [0] * NT, [0] * NT, [0] * NT
        c = 0
        for tt in range(NT):
            c += 1; dv_negmax[tt] = c
            c += 1
            c += 1
            c += 1; dv_projg[tt] = c
            if tt >= 1:
                c += 1; dv_evac[tt - 1] = c
        c += 1; dv_evac[NT - 1] = c
        n_dv_pro = c

        @blk.gpsimd
        def _(e):
            esync(e, "g")
            b = dW.v
            for k, (s, d) in enumerate([(s_wrt, din["wrtb"]), (s_at, din["atb"])]):
                e.dma_start(out=s[:], in_=d[:, :]).then_inc(dW.h, 16)
                e.wait_ge(dW.h, b + 16 * (k + 1))

        @blk.tensor
        def _(e):
            esync(e, "t")
            wge(e, dW, b_dW + 16 * 2)
            wge(e, dL, 48)
            wge(e, dX, dX.v)
            for tt in range(NT):
                p = tt % 2
                if tt >= 2:
                    wge(e, sDV, b_DV + dv_projg[tt - 2])
                for ic in range(NIC):
                    e.matmul(pgp[p][:, 0:4], xn_stat(ic, tt),
                             s_wrt[:, ic * 4:(ic + 1) * 4],
                             start=(ic == 0), stop=(ic == NIC - 1))
                for ic in range(NIC):
                    ins = e.matmul(pgp[p][:, 4:36], xn_stat(ic, tt),
                                   s_at[:, ic * 32:(ic + 1) * 32],
                                   start=(ic == 0), stop=(ic == NIC - 1))
                ins.then_inc(sPE.h, 1)
                if tt >= 1:
                    wge(e, sDV, b_DV + dv_projg[tt - 1])
                    e.transpose(ppgt[(tt - 1) % 2][0:32, 0:128],
                                s_pg[(tt - 1) % 2][:], s_ident[:]).then_inc(sPE.h, 1)
            wge(e, sDV, b_DV + dv_projg[NT - 1])
            e.transpose(ppgt[(NT - 1) % 2][0:32, 0:128],
                        s_pg[(NT - 1) % 2][:], s_ident[:]).then_inc(sPE.h, 1)

        @blk.vector
        def _(e):
            esync(e, "v")
            for tt in range(NT):
                p = tt % 2
                wge(e, sPE, b_PE + pe_mm[tt])
                if tt >= 1:
                    wge(e, sAC, b_AC + tt)
                e.reduce_max(s_cols[:, 8:9], pgp[p][:, 0:4], axis=AX.X,
                             negate=True).then_inc(sDV.h, 1)
                wge(e, sAC, b_AC + tt + 1)
                e.reciprocal(s_cols[:, 10:11], s_kols[:, 2:3]).then_inc(sDV.h, 1)
                wge(e, sDV, b_DV + dv_negmax[tt] + 1)
                e.tensor_scalar(out=s_g4[:], in0=s_eg[:],
                                scalar1=s_cols[:, 10:11], scalar2=2.0,
                                op0=OP.mult, op1=OP.mult).then_inc(sDV.h, 1)
                wge(e, sDV, b_DV + dv_negmax[tt] + 2)
                for ee in range(4):
                    ins = e.tensor_scalar(out=s_pg[p][:, ee * 8:(ee + 1) * 8],
                                          in0=pgp[p][:, 4 + ee * 8: 4 + (ee + 1) * 8],
                                          scalar1=s_g4[:, ee:ee + 1], scalar2=None,
                                          op0=OP.mult)
                ins.then_inc(sDV.h, 1)
                if tt >= 1:
                    wge(e, sPE, b_PE + pe_tr[tt - 1])
                    e.tensor_copy(s_pgT[0:32, (tt - 1) * 128: tt * 128],
                                  ppgt[(tt - 1) % 2][0:32, 0:128]).then_inc(sDV.h, 1)
            wge(e, sPE, b_PE + pe_tr[NT - 1])
            e.tensor_copy(s_pgT[0:32, (NT - 1) * 128: NT * 128],
                          ppgt[(NT - 1) % 2][0:32, 0:128]).then_inc(sDV.h, 1)

        @blk.scalar
        def _(e):
            esync(e, "a")
            for tt in range(NT):
                wge(e, sDV, b_DV + dv_negmax[tt])
                e.activation(s_eg[:], pgp[tt % 2][:, 0:4], AF.Exp,
                             bias=s_cols[:, 8:9],
                             accum_out=s_kols[:, 2:3])
                e.copy(s_kols[:, 1:2], s_eps[:, 0:1]).then_inc(sAC.h, 1)
      dW.v += 16 * 2
      sPE.v += n_pe_pro
      sAC.v += NT
      sDV.v += n_dv_pro

    NOC = 16
    if phases >= 3:
      with ExitStack() as pctx, nc.Block() as blk:
        psb = lambda n, s, d, **kw: pctx.enter_context(nc.sbuf_tensor(n, s, d, **kw))
        pps = lambda n, s, d: pctx.enter_context(nc.psum_tensor(n, s, d))
        wbuf2 = psb("wbuf2", [128, 16384], BF16)
        s_bmt = psb("s_bmt", [32, DM], BF16)
        s_bb = psb("s_bb", [128, DM], F32)
        s_f1 = [psb(f"s_f1{i}", [128, 256], F32) for i in range(2)]
        s_gt = [psb(f"s_gt{i}", [128, 256], BF16) for i in range(2)]
        pg = [pps(f"pg{i}", [128, 512], F32) for i in range(2)]
        b_dW, b_dS, b_dL = dW.v, dS.v, dL.v
        b_PE, b_DV, b_AC = sPE.v, sDV.v, sAC.v
        b_dWp = [dWp[0].v, dWp[1].v]
        b_dSp = [dSp[0].v, dSp[1].v]

        @blk.gpsimd
        def _(e):
            esync(e, "g")
            e.dma_start(out=s_bmt[:], in_=din["bmt"][:, :]).then_inc(dW.h, 16)
            e.wait_ge(dW.h, b_dW + 16)
            for oc in range(NOC):
                p = oc % 2
                if oc >= 2:
                    wge(e, sPE, b_PE + NT * (oc - 1))
                src = din["wt"][:, oc * 256:(oc + 1) * 256].rearrange(
                    "(c p) o -> p c o", p=128)
                dst = wbuf2[:, p * 8192:(p + 1) * 8192].rearrange(
                    "p (c o) -> p c o", o=256)
                e.dma_start(out=dst, in_=src).then_inc(dWp[p].h, 16)

        @blk.tensor
        def _(e):
            esync(e, "t")
            wge(e, dW, b_dW + 16)
            for oc in range(NOC):
                p = oc % 2
                wge(e, dWp[p], b_dWp[p] + 16 * (oc // 2 + 1))
                for tt in range(NT):
                    i = oc * NT + tt
                    pp = i % 2
                    if i >= 2:
                        wge(e, sDV, b_DV + (i - 2) + 1)
                    for ic in range(NIC):
                        e.matmul(pg[pp][:, 0:256],
                                 xn_stat(ic, tt),
                                 wbuf2[:, p * 8192 + ic * 256: p * 8192 + (ic + 1) * 256],
                                 start=(ic == 0), stop=False)
                    e.matmul(pg[pp][:, 0:256],
                             s_pgT[0:32, tt * 128:(tt + 1) * 128],
                             s_bmt[0:32, oc * 256:(oc + 1) * 256],
                             start=False, stop=True).then_inc(sPE.h, 1)

        @blk.scalar
        def _(e):
            esync(e, "a")
            e.dma_start(out=s_bb[:], in_=din["bb"][:, :]).then_inc(dL.h, 16)
            for oc in range(NOC):
                for tt in range(NT):
                    i = oc * NT + tt
                    pp = i % 2
                    wge(e, sDV, b_DV + i + 1)
                    if i >= 2:
                        wge(e, dSp[pp], b_dSp[pp] + 16 * ((i - 2) // 2 + 1))
                    e.activation(s_gt[pp][:], s_f1[pp][:], AF.Silu).then_inc(sAC.h, 1)
                    wge(e, sAC, b_AC + i + 1)
                    e.dma_start(out=gsc[tt * 128:(tt + 1) * 128,
                                        oc * 256:(oc + 1) * 256],
                                in_=s_gt[pp][:]).then_inc(dSp[pp].h, 16)

        @blk.vector
        def _(e):
            esync(e, "v")
            wge(e, dL, b_dL + 16)
            for oc in range(NOC):
                for tt in range(NT):
                    i = oc * NT + tt
                    pp = i % 2
                    wge(e, sPE, b_PE + i + 1)
                    if i >= 2:
                        wge(e, sAC, b_AC + (i - 2) + 1)
                    e.scalar_tensor_tensor(
                        out=s_f1[pp][:], in0=pg[pp][:, 0:256], scalar=1.0,
                        in1=s_bb[:, oc * 256:(oc + 1) * 256],
                        op0=OP.mult, op1=OP.add).then_inc(sDV.h, 1)
      dW.v += 16
      dWp[0].v += 16 * (NOC // 2)
      dWp[1].v += 16 * (NOC // 2)
      dL.v += 16
      dSp[0].v += 16 * (NOC * NT // 2)
      dSp[1].v += 16 * (NOC * NT // 2)
      sPE.v += NOC * NT
      sAC.v += NOC * NT
      sDV.v += NOC * NT

    def proj_phase(pctx, blk, heads, wsrc, dest_kind):
        psb = lambda n, s, d, **kw: pctx.enter_context(nc.sbuf_tensor(n, s, d, **kw))
        pps = lambda n, s, d: pctx.enter_context(nc.psum_tensor(n, s, d))
        wb = psb("wb_" + dest_kind, [128, 8192], BF16)
        psQ = pps("psQ_" + dest_kind, [128, 2048], F32)
        pbc = pps("pbc_" + dest_kind, [128, 2048], F32)
        norm = dest_kind in ("q", "k")
        if norm:
            s_sq = psb("sq_" + dest_kind, [128, 2048], BF16)
            s_w = psb("sw_" + dest_kind, [128, 2048], F32)
            s_rr = psb("srr_" + dest_kind, [1, 2048], F32)
        s_st = [psb(f"st_{dest_kind}{i}", [128, 2048], BF16) for i in range(2)]
        ndst = qhat if dest_kind == "q" else kscr
        wcol = s_wqe if dest_kind == "q" else s_knw
        b_dW, b_dS = dW.v, dS.v
        b_PE, b_DV, b_AC = sPE.v, sDV.v, sAC.v
        b_dWp = [dWp[0].v, dWp[1].v]
        b_dSp = [dSp[0].v, dSp[1].v]

        if norm:
            pe_mm, pe_msq, pe_bc = [0] * heads, [0] * heads, [0] * heads
            c = 0
            for h in range(heads + 1):
                if h < heads:
                    c += 1; pe_mm[h] = c
                if h >= 1:
                    c += 1; pe_msq[h - 1] = c
                    c += 1; pe_bc[h - 1] = c
            n_pe = c
            ac_sq, ac_ln, ac_rs = [0] * heads, [0] * heads, [0] * heads
            c = 0
            for h in range(heads + 1):
                if h < heads:
                    c += 1; ac_sq[h] = c
                if h >= 1:
                    c += 1; ac_ln[h - 1] = c
                    c += 1; ac_rs[h - 1] = c
            n_ac = c
            dv_w, dv_stt = [0] * heads, [0] * heads
            c = 0
            for h in range(heads + 1):
                if h >= 1:
                    c += 1; dv_stt[h - 1] = c
                if h < heads:
                    c += 1; dv_w[h] = c
            n_dv = c
        else:
            pe_mm = [h + 1 for h in range(heads)]
            n_pe = heads
            dv_cp = [h + 1 for h in range(heads)]
            n_dv = heads
            n_ac = 0

        @blk.gpsimd
        def _(e):
            esync(e, "g")
            for h in range(heads):
                p = h % 2
                for half in range(2):
                    ch = 2 * h + half
                    if h >= 1:
                        wge(e, sPE, b_PE + pe_mm[h - 1])
                    src = wsrc[half * 2048:(half + 1) * 2048,
                               h * 128:(h + 1) * 128].rearrange(
                        "(c p) o -> p c o", p=128)
                    dst = wb[:, p * 4096 + half * 2048:
                             p * 4096 + (half + 1) * 2048].rearrange(
                        "p (c o) -> p c o", o=128)
                    e.dma_start(out=dst, in_=src).then_inc(dWp[ch % 2].h, 16)

        @blk.tensor
        def _(e):
            esync(e, "t")
            for h in range(heads + 1):
                if h < heads:
                    p = h % 2
                    if h >= 1:
                        if norm:
                            wge(e, sDV, b_DV + dv_w[h - 1])
                            wge(e, sAC, b_AC + ac_sq[h - 1])
                        else:
                            wge(e, sDV, b_DV + dv_cp[h - 1])
                    for half in range(2):
                        ch = 2 * h + half
                        wge(e, dWp[ch % 2], b_dWp[ch % 2] + 16 * (ch // 2 + 1))
                        for tc in range(4):
                            for ici in range(16):
                                ic = half * 16 + ici
                                ins = e.matmul(
                                    psQ[:, tc * 512:(tc + 1) * 512],
                                    wb[:, p * 4096 + half * 2048 + ici * 128:
                                       p * 4096 + half * 2048 + (ici + 1) * 128],
                                    xn_mov(ic, tc),
                                    start=(ic == 0), stop=(ic == NIC - 1))
                    ins.then_inc(sPE.h, 1)
                if norm and h >= 1:
                    hh = h - 1
                    wge(e, sAC, b_AC + ac_sq[hh])
                    if hh >= 1:
                        wge(e, sDV, b_DV + dv_stt[hh - 1])
                    for tc in range(4):
                        ins = e.matmul(pbc[0:1, tc * 512:(tc + 1) * 512],
                                       s_ones128[:],
                                       s_sq[:, tc * 512:(tc + 1) * 512],
                                       start=True, stop=True)
                    ins.then_inc(sPE.h, 1)
                    wge(e, sAC, b_AC + ac_rs[hh])
                    for tc in range(4):
                        ins = e.matmul(pbc[:, tc * 512:(tc + 1) * 512], s_ones1f[:],
                                       s_rr[0:1, tc * 512:(tc + 1) * 512],
                                       start=True, stop=True)
                    ins.then_inc(sPE.h, 1)

        if norm:
            @blk.scalar
            def _(e):
                esync(e, "a")
                for h in range(heads + 2):
                    if h < heads:
                        wge(e, sPE, b_PE + pe_mm[h])
                        if h >= 1:
                            wge(e, sPE, b_PE + pe_msq[h - 1])
                        e.activation(s_sq[:], psQ[:],
                                     AF.Square).then_inc(sAC.h, 1)
                    if 1 <= h <= heads:
                        hh = h - 1
                        wge(e, sPE, b_PE + pe_msq[hh])
                        e.activation(pbc[32:33, 0:2048], pbc[0:1, 0:2048], AF.Ln,
                                     scale=1.0 / 128.0, bias=s_eps[0:1, 1:2]).then_inc(sAC.h, 1)
                        if hh >= 1:
                            wge(e, sPE, b_PE + pe_bc[hh - 1])
                        wge(e, sAC, b_AC + ac_ln[hh])
                        e.activation(s_rr[:], pbc[32:33, 0:2048], AF.Exp,
                                     scale=-0.5).then_inc(sAC.h, 1)
                    if h >= 2:
                        hs = h - 2
                        wge(e, sDV, b_DV + dv_stt[hs])
                        e.dma_start(out=ndst[hs * 128:(hs + 1) * 128, :],
                                    in_=s_st[hs % 2][:]).then_inc(dSp[hs % 2].h, 16)
        else:
            @blk.scalar
            def _(e):
                esync(e, "a")
                for h in range(heads):
                    wge(e, sDV, b_DV + dv_cp[h])
                    e.dma_start(out=vsc[h, :, :],
                                in_=s_st[h % 2][:]).then_inc(dSp[h % 2].h, 16)

        @blk.vector
        def _(e):
            esync(e, "v")
            for h in range(heads + 1):
                if norm and h >= 1:
                    hh = h - 1
                    wge(e, sPE, b_PE + pe_bc[hh])
                    if hh >= 2:
                        wge(e, dSp[hh % 2], b_dSp[hh % 2] + 16 * ((hh - 2) // 2 + 1))
                    wge(e, sDV, b_DV + dv_w[hh])
                    dst = s_st[hh % 2][:]
                    e.scalar_tensor_tensor(out=dst, in0=pbc[:], scalar=1.0,
                                           in1=s_w[:], op0=OP.mult,
                                           op1=OP.mult).then_inc(sDV.h, 1)
                if h < heads and norm:
                    wge(e, sPE, b_PE + pe_mm[h])
                    if h >= 1:
                        wge(e, sDV, b_DV + dv_stt[h - 1])
                    e.tensor_scalar(out=s_w[:], in0=psQ[:], scalar1=wcol[:, 0:1],
                                    scalar2=None, op0=OP.mult).then_inc(sDV.h, 1)
                if h < heads and not norm:
                    wge(e, sPE, b_PE + pe_mm[h])
                    if h >= 2:
                        wge(e, dSp[h % 2], b_dSp[h % 2] + 16 * ((h - 2) // 2 + 1))
                    e.tensor_copy(s_st[h % 2][:], psQ[:]).then_inc(sDV.h, 1)

        return dict(dWp0=16 * heads, dWp1=16 * heads,
                    dSp0=16 * ((heads + 1) // 2), dSp1=16 * (heads // 2),
                    sPE=n_pe, sDV=n_dv, sAC=n_ac)

    def apply_inc(inc):
        m = {"dWp0": dWp[0], "dWp1": dWp[1], "dSp0": dSp[0], "dSp1": dSp[1],
             "sPE": sPE, "sDV": sDV, "sAC": sAC}
        for k_, v_ in inc.items():
            m[k_].v += v_

    if phases >= 4:
        with ExitStack() as pctx, nc.Block() as blk:
            apply_inc(proj_phase(pctx, blk, H, din["wqt"], "q"))
    if phases >= 5:
        with ExitStack() as pctx, nc.Block() as blk:
            apply_inc(proj_phase(pctx, blk, HKV, din["wkt"], "k"))
    if phases >= 6:
        with ExitStack() as pctx, nc.Block() as blk:
            apply_inc(proj_phase(pctx, blk, HKV, din["wvt"], "v"))

    if tap_out:
        with nc.Block() as blk:
            b_dS = dS.v
            ntap = len(tap_out)

            @blk.scalar
            def _(e):
                esync(e, "a")
                e.wait_ge(dX.h, dX.v)
                n = 0
                if "xnt" in tap_out:
                    n += 1
                    e.dma_start(out=tap_out["xnt"][:, :], in_=xnt[:]).then_inc(dS.h, 16)
                if "dbg" in tap_out:
                    n += 2
                    e.dma_start(out=tap_out["dbg"][:, :], in_=s_cols[:]).then_inc(dS.h, 16)
                    e.dma_start(out=tap_out["dbg2"][:, :],
                                in_=s_eps[:]).then_inc(dS.h, 16)
                e.wait_ge(dS.h, b_dS + 16 * n)
        dS.v += 16 * ntap

    xnt_ctx.close()

    NQ = H * NT
    if phases >= 7:
      with ExitStack() as pctx, nc.Block() as blk:
        psb = lambda n, s, d, **kw: pctx.enter_context(nc.sbuf_tensor(n, s, d, **kw))
        pps = lambda n, s, d: pctx.enter_context(nc.psum_tensor(n, s, d))
        vv = psb("vv", [128, HKV * SEQ], BF16)
        kt = psb("kt", [128, HKV * SEQ], BF16)
        s_alw = psb("s_alw", [128, H * 128], BF16)
        s_alk = psb("s_alk", [128, SEQ], BF16)
        s_nu = psb("s_nu", [128, 128], BF16)
        qb = [psb(f"qb{i}", [128, SEQ], BF16) for i in range(2)]
        eb = [psb(f"eb{i}", [128, SEQ], BF16) for i in range(4)]
        en = [psb(f"en{i}", [128, SEQ], BF16) for i in range(4)]
        et = [psb(f"et{i}", [128, SEQ], BF16) for i in range(6)]
        ost = [psb(f"ost{i}", [128, SEQ], BF16) for i in range(2)]
        s_np = [psb(f"s_np{i}", [128, 16], F32) for i in range(4)]
        s_al = [psb(f"s_al{i}", [128, 8], F32) for i in range(4)]
        psS = pps("psS", [128, 3072], F32)
        psO = [pps(f"psO{i}", [128, 512], F32) for i in range(2)]
        b_dL, b_dX = dL.v, dX.v
        b_dLp = [dLp[0].v, dLp[1].v]
        b_dSp = [dSp[0].v, dSp[1].v]
        b_S, b_M, b_E, b_N, b_P, b_V = sS5.v, sM5.v, sE5.v, sN5.v, sP5.v, sV5.v
        b_F, b_Ec = sF5.v, sEc.v
        dS_tot = (dS.v, dSp[0].v, dSp[1].v)
        NCONST = 3

        def ncnk(qt):
            kw = (qt + 1) * 128
            return (kw + 511) // 512, qt + 1, kw

        slot_of, sfree_wait = {}, {}
        slot_owner = [None] * 6
        gslot = 0
        for idx in range(NQ):
            for cch in range(ncnk(idx % NT)[0]):
                sl = gslot % 6
                own = slot_owner[sl]
                sfree_wait[(idx, cch)] = (own + 1) if own is not None else 0
                slot_of[(idx, cch)] = sl
                slot_owner[sl] = gslot
                gslot += 1
        n_chunks = gslot

        @blk.scalar
        def _(e):
            esync(e, "a")
            for s, v in zip((dS, dSp[0], dSp[1]), dS_tot):
                if v > 0:
                    e.wait_ge(s.h, v)
            for s, d in [(s_alw, din["alw"]), (s_alk, din["alk"]),
                         (s_nu, din["negu"])]:
                e.dma_start(out=s[:], in_=d[:, :]).then_inc(dL.h, 16)
            for kv in range(HKV):
                e.dma_start(out=kt[:, kv * SEQ:(kv + 1) * SEQ],
                            in_=kscr[kv * 128:(kv + 1) * 128, :]).then_inc(dL.h, 16)
            for h in range(2):
                e.dma_start(out=qb[h][:],
                            in_=qhat[h * 128:(h + 1) * 128, :]).then_inc(dLp[h].h, 16)

            def ostore(e, hs):
                wge(e, sV5, b_V + NT * (hs + 1))
                dsto = othbm[:, :, hs, :].rearrange("q p r -> p q r")
                srco = ost[hs % 2][:].rearrange("p (q r) -> p q r", r=128)
                e.dma_start(out=dsto, in_=srco).then_inc(dSp[hs % 2].h, 16)

            for idx in range(NQ):
                h, qt = idx // NT, idx % NT
                qp = idx % 4
                nc512, nk, kw = ncnk(qt)
                wge(e, sM5, b_M + idx + 1)
                if idx >= 4:
                    wge(e, sN5, b_N + idx - 3)
                for cch in range(nc512):
                    wc = min(512, kw - cch * 512)
                    sl = slot_of[(idx, cch)]
                    e.activation(eb[qp][:, cch * 512: cch * 512 + wc],
                                 psS[:, sl * 512: sl * 512 + wc], AF.Exp,
                                 bias=s_np[qp][:, 4:5],
                                 accum_out=s_al[qp][:, cch:cch + 1]
                                 ).then_inc(sEc.h, 1)
                e.copy(s_al[qp][:, 4:5],
                       s_eps[:, 0:1]).then_inc(sE5.h, 1)
                if idx >= 17 and (idx - 17) % NT == 0:
                    ostore(e, (idx - 17) // NT)
                if qt == NT - 1 and h + 2 < H:
                    wge(e, sS5, b_S + NT * (h + 1))
                    e.dma_start(out=qb[h % 2][:],
                                in_=qhat[(h + 2) * 128:(h + 3) * 128, :]
                                ).then_inc(dLp[h % 2].h, 16)
            ostore(e, H - 1)

        def _pv(e, I):
            hp, qtp = I // NT, I % NT
            kvp = hp // 4
            nkp = qtp + 1
            wge(e, dX, b_dX + 16 * (HKV + I + 1))
            if I >= 2:
                wge(e, sV5, b_V + I - 1)
            ins = None
            for kc in range(nkp):
                ins = e.matmul(psO[I % 2][:, 0:128],
                               vv[:, kvp * SEQ + kc * 128: kvp * SEQ + (kc + 1) * 128],
                               et[I % 6][:, kc * 128:(kc + 1) * 128],
                               start=(kc == 0), stop=(kc == nkp - 1))
            ins.then_inc(sP5.h, 1)

        @blk.tensor
        def _(e):
            esync(e, "t")
            wge(e, dL, b_dL + 16 * (NCONST + HKV))
            for idx in range(NQ):
                h, qt = idx // NT, idx % NT
                kv = h // 4
                nc512, nk, kw = ncnk(qt)
                wge(e, dLp[h % 2], b_dLp[h % 2] + 16 * (h // 2 + 1))
                cl = nc512 - 1
                off = qt * 128 - cl * 512
                for cch in range(nc512):
                    wc = min(512, kw - cch * 512)
                    sl = slot_of[(idx, cch)]
                    diag = (cch == cl)
                    wge(e, sEc, b_Ec + sfree_wait[(idx, cch)])
                    e.matmul(psS[:, sl * 512: sl * 512 + wc],
                             qb[h % 2][:, qt * 128:(qt + 1) * 128],
                             kt[:, kv * SEQ + cch * 512:
                                kv * SEQ + cch * 512 + wc],
                             start=True, stop=False)
                    ins = e.matmul(psS[:, sl * 512: sl * 512 + wc],
                                   s_alw[:, h * 128:(h + 1) * 128],
                                   s_alk[:, cch * 512: cch * 512 + wc],
                                   start=False, stop=not diag)
                    if diag:
                        ins = e.matmul(psS[:, sl * 512 + off: sl * 512 + off + 128],
                                       s_ident[:], s_nu[:],
                                       start=False, stop=True)
                        ins.then_inc(sS5.h, 1)
                if idx >= 4:
                    _pv(e, idx - 4)
            for I in range(NQ - 4, NQ):
                _pv(e, I)

        def _evac(e, I):
            hp, qtp = I // NT, I % NT
            wge(e, sP5, b_P + I + 1)
            if hp >= 2 and qtp == 0:
                wge(e, dSp[hp % 2], b_dSp[hp % 2] + 16 * ((hp - 2) // 2 + 1))
            e.tensor_copy(ost[hp % 2][:, qtp * 128:(qtp + 1) * 128],
                          psO[I % 2][:, 0:128]).then_inc(sV5.h, 1)

        @blk.vector
        def _(e):
            esync(e, "v")
            for idx in range(NQ + 4):
                qp = idx % 4
                qp2 = (idx - 2) % 4
                if 2 <= idx < NQ + 2:
                    nc2 = ncnk((idx - 2) % NT)[0]
                    wge(e, sE5, b_E + idx - 1)
                    e.reduce_sum(s_np[qp2][:, 12:13], s_al[qp2][:, 0:nc2],
                                 axis=AX.X)
                if idx < NQ:
                    nc512, nk, kw = ncnk(idx % NT)
                    wge(e, sS5, b_S + idx + 1)
                    for cch in range(nc512):
                        wc = min(512, kw - cch * 512)
                        sl = slot_of[(idx, cch)]
                        ins = e.reduce_max(s_np[qp][:, cch:cch + 1],
                                           psS[:, sl * 512: sl * 512 + wc],
                                           axis=AX.X, negate=True)
                    ins.then_inc(sF5.h, 1)
                else:
                    e.tensor_copy(s_np[0][:, 15:16],
                                  s_eps[:, 0:1]).then_inc(sF5.h, 1)
                wge(e, sF5, b_F + idx + 1)
                if 2 <= idx < NQ + 2:
                    e.reciprocal(s_np[qp2][:, 13:14], s_np[qp2][:, 12:13])
                if idx < NQ:
                    nc512 = ncnk(idx % NT)[0]
                    e.tensor_reduce(s_np[qp][:, 4:5], s_np[qp][:, 0:nc512],
                                    axis=AX.X, op=OP.min).then_inc(sM5.h, 1)
                else:
                    e.tensor_copy(s_np[0][:, 14:15],
                                  s_eps[:, 0:1]).then_inc(sM5.h, 1)
                wge(e, sM5, b_M + idx + 1)
                if 2 <= idx < NQ + 2:
                    kw2 = ncnk((idx - 2) % NT)[2]
                    if idx >= 6:
                        wge(e, dX, b_dX + 16 * (HKV + idx - 5))
                    e.tensor_scalar(out=en[qp2][:, 0:kw2], in0=eb[qp2][:, 0:kw2],
                                    scalar1=s_np[qp2][:, 13:14], scalar2=None,
                                    op0=OP.mult).then_inc(sN5.h, 1)
                if idx >= 4:
                    _evac(e, idx - 4)

        @blk.sync
        def _(e):
            esync(e, "g")
            for s, v in zip((dS, dSp[0], dSp[1]), dS_tot):
                if v > 0:
                    e.wait_ge(s.h, v)
            for dc in range(HKV):
                dstv = vv[:, dc * SEQ:(dc + 1) * SEQ].rearrange(
                    "p (c r) -> p c r", r=128)
                e.dma_start_transpose(dstv, vsc[dc, :, :]).then_inc(dX.h, 16)
                wge(e, dX, b_dX + 16 * (dc + 1))
            for idx in range(NQ):
                qp = idx % 4
                kw = ncnk(idx % NT)[2]
                wge(e, sN5, b_N + idx + 1)
                if idx >= 6:
                    wge(e, sP5, b_P + idx - 5)
                dstt = et[idx % 6][:, 0:kw].rearrange("p (c r) -> p c r", r=128)
                e.dma_start_transpose(dstt, en[qp][:, 0:kw]).then_inc(dX.h, 16)
                wge(e, dX, b_dX + 16 * (HKV + idx + 1))
      dL.v += 16 * (3 + HKV)
      dLp[0].v += 16 * (H // 2)
      dLp[1].v += 16 * (H // 2)
      dSp[0].v += 16 * (H // 2)
      dSp[1].v += 16 * (H // 2)
      sF5.v += NQ + 4
      sEc.v += n_chunks
      dX.v += 16 * (HKV + NQ)
      sS5.v += NQ
      sM5.v += NQ + 4
      sE5.v += NQ
      sN5.v += NQ
      sP5.v += NQ
      sV5.v += NQ

    NJC = 4
    if phases >= 8:
      with ExitStack() as pctx, nc.Block() as blk:
        psb = lambda n, s, d, **kw: pctx.enter_context(nc.sbuf_tensor(n, s, d, **kw))
        pps = lambda n, s, d: pctx.enter_context(nc.psum_tensor(n, s, d))
        wo6 = [psb(f"wo6_{i}", [128, 32768], BF16) for i in range(2)]
        s_ot = [psb(f"s_ot{i}", [128, H * 128], BF16) for i in range(2)]
        s_g6 = [psb(f"s_g6{i}", [128, 1024], BF16) for i in range(2)]
        s_x6 = [psb(f"s_x6{i}", [128, 1024], F32) for i in range(2)]
        s_t6 = [psb(f"s_t6{i}", [128, 512], F32) for i in range(2)]
        s_o6 = [psb(f"s_o6{i}", [128, 1024], F32) for i in range(2)]
        py = [pps(f"py{i}", [128, 512], F32) for i in range(2)]
        b_dW, b_dL, b_dS = dW.v, dL.v, dS.v
        b_PE, b_DV = sPE.v, sDV.v
        b_dWp = [dWp[0].v, dWp[1].v]
        b_dLp = [dLp[0].v, dLp[1].v]
        b_dSp = [dSp[0].v, dSp[1].v]
        dS_prev6 = (dS.v, dSp[0].v, dSp[1].v)

        @blk.gpsimd
        def _(e):
            esync(e, "g")
            for jc in range(NJC):
                p = jc % 2
                if jc >= 2:
                    wge(e, sPE, b_PE + 2 * NT * (jc - 1))
                for cc in range(4):
                    src = din["wot"][cc * 1024:(cc + 1) * 1024,
                                     jc * 1024:(jc + 1) * 1024].rearrange(
                        "(c p) o -> p c o", p=128)
                    dst = wo6[p][:, cc * 8192:(cc + 1) * 8192].rearrange(
                        "p (c o) -> p c o", o=1024)
                    e.dma_start(out=dst, in_=src).then_inc(dWp[jc % 2].h, 16)

        @blk.scalar
        def _(e):
            esync(e, "a")
            wge(e, dS, dS_prev6[0])
            wge(e, dSp[0], dS_prev6[1])
            wge(e, dSp[1], dS_prev6[2])
            for jc in range(NJC):
                for tt in range(NT):
                    i = jc * NT + tt
                    p = i % 2
                    if i >= 2:
                        wge(e, sPE, b_PE + 2 * (i - 2) + 2)
                        wge(e, sDV, b_DV + 4 * (i - 2) + 4)
                    e.dma_start(out=s_ot[p][:],
                                in_=othbm[tt, :, :, :].rearrange(
                                    "p h r -> p (h r)")).then_inc(dLp[p].h, 16)
                    e.dma_start(out=s_g6[p][:],
                                in_=gsc[tt * 128:(tt + 1) * 128,
                                        jc * 1024:(jc + 1) * 1024]).then_inc(dLp[p].h, 16)
                    e.dma_start(out=s_x6[p][:],
                                in_=din["x"][tt * 128:(tt + 1) * 128,
                                             jc * 1024:(jc + 1) * 1024]).then_inc(dLp[p].h, 16)
                    wge(e, sDV, b_DV + 4 * i + 4)
                    e.dma_start(out=out[tt * 128:(tt + 1) * 128,
                                        jc * 1024:(jc + 1) * 1024],
                                in_=s_o6[p][:]).then_inc(dSp[p].h, 16)
            e.wait_ge(dSp[0].h, b_dSp[0] + 16 * (NJC * NT // 2))
            e.wait_ge(dSp[1].h, b_dSp[1] + 16 * (NJC * NT // 2))

        @blk.tensor
        def _(e):
            esync(e, "t")
            for jc in range(NJC):
                p = jc % 2
                wge(e, dWp[jc % 2], b_dWp[jc % 2] + 64 * (jc // 2 + 1))
                for tt in range(NT):
                    i = jc * NT + tt
                    pp = i % 2
                    wge(e, dLp[pp], b_dLp[pp] + 48 * (i // 2 + 1))
                    for j2 in range(2):
                        if i >= 1:
                            wge(e, sDV, b_DV + 4 * (i - 1) + 2 * j2 + 2)
                        for hc in range(H):
                            ins = e.matmul(py[j2][:, 0:512],
                                           s_ot[pp][:, hc * 128:(hc + 1) * 128],
                                           wo6[p][:, hc * 1024 + j2 * 512:
                                                  hc * 1024 + (j2 + 1) * 512],
                                           start=(hc == 0), stop=(hc == H - 1))
                        ins.then_inc(sPE.h, 1)

        @blk.vector
        def _(e):
            esync(e, "v")
            for jc in range(NJC):
                for tt in range(NT):
                    i = jc * NT + tt
                    pp = i % 2
                    wge(e, dLp[pp], b_dLp[pp] + 48 * (i // 2 + 1))
                    if i >= 2:
                        wge(e, dSp[pp], b_dSp[pp] + 16 * (i // 2))
                    for j2 in range(2):
                        wge(e, sPE, b_PE + 2 * i + j2 + 1)
                        e.scalar_tensor_tensor(
                            out=s_t6[j2][:], in0=py[j2][:, 0:512], scalar=1.0,
                            in1=s_g6[pp][:, j2 * 512:(j2 + 1) * 512],
                            op0=OP.mult, op1=OP.mult).then_inc(sDV.h, 1)
                    wge(e, sDV, b_DV + 4 * i + 2)
                    for j2 in range(2):
                        e.tensor_tensor(
                            out=s_o6[pp][:, j2 * 512:(j2 + 1) * 512],
                            in0=s_t6[j2][:],
                            in1=s_x6[pp][:, j2 * 512:(j2 + 1) * 512],
                            op=OP.add).then_inc(sDV.h, 1)
      dWp[0].v += 64 * (NJC // 2)
      dWp[1].v += 64 * (NJC // 2)
      dLp[0].v += 48 * (NJC * NT // 2)
      dLp[1].v += 48 * (NJC * NT // 2)
      dSp[0].v += 16 * (NJC * NT // 2)
      dSp[1].v += 16 * (NJC * NT // 2)
      sPE.v += 2 * NJC * NT
      sDV.v += 4 * NJC * NT

    with nc.Block() as blk:
        @blk.scalar
        def _(e):
            for s in (dL, dLp[0], dLp[1], dS, dSp[0], dSp[1], dW,
                      dWp[0], dWp[1], dX):
                if s.v > 0:
                    e.wait_ge(s.h, s.v)

    g.close()
    sctx.close()
    return nc, din, out, tap_out


def _host_layout(inputs):
    x = np.ascontiguousarray(np.asarray(inputs["L0_attn_x"], dtype=np.float32))
    wq = np.asarray(inputs["L0_attn_Wq"], np.float32)
    wk = np.asarray(inputs["L0_attn_Wk"], np.float32)
    wv = np.asarray(inputs["L0_attn_Wv"], np.float32)
    wo = np.asarray(inputs["L0_attn_Wo"], np.float32)
    w = np.asarray(inputs["L0_ffn_W"], np.float32)
    b = np.asarray(inputs["L0_ffn_b"], np.float32)
    A = np.asarray(inputs["L0_ffn_A"], np.float32)
    Bm = np.asarray(inputs["L0_ffn_Bm"], np.float32)
    Wr = np.asarray(inputs["L0_ffn_Wr"], np.float32)
    qnw = np.asarray(inputs["L0_attn_q_norm_w"], np.float32).reshape(128, 1)
    knw = np.asarray(inputs["L0_attn_k_norm_w"], np.float32).reshape(128, 1)

    ident, dmask, alw, alk, negu = host_constants()
    com = {
        "wqt": np.ascontiguousarray(wq.T),
        "wkt": np.ascontiguousarray(wk.T),
        "wvt": np.ascontiguousarray(wv.T),
        "wot": np.ascontiguousarray(wo.T),
        "wt": np.ascontiguousarray(w.T),
        "wrtb": np.ascontiguousarray(
            Wr.T.reshape(32, 128, 4).transpose(1, 0, 2).reshape(128, 128)),
        "atb": np.ascontiguousarray(
            A.reshape(32, DM).T.reshape(32, 128, 32).transpose(1, 0, 2)
            .reshape(128, 1024)),
        "bmt": np.ascontiguousarray(Bm.transpose(0, 2, 1).reshape(32, DM)),
        "bb": np.ascontiguousarray(np.broadcast_to(b, (128, DM))),
        "qnw": np.ascontiguousarray(qnw),
        "knw": np.ascontiguousarray(knw),
        "ident": ident, "dmask": dmask, "alw": alw, "alk": alk, "negu": negu,
    }
    return x, com


def kernel(**inputs):
    x, com = _host_layout(inputs)
    nc, din, out_ap, taps = build()
    in_maps = [dict(com, x=np.ascontiguousarray(x[c])) for c in range(8)]
    res = run_bass_kernel_spmd(nc, in_maps, core_ids=list(range(8)))
    return np.stack([np.asarray(r["out"]) for r in res.results], axis=0)
